# version 30 (final)
# baseline (speedup 1.0000x reference)
import math
import numpy as np
import ml_dtypes
import concourse.bass as bass
import concourse.mybir as mybir
from concourse.bass_utils import run_bass_kernel_spmd

F32 = mybir.dt.float32
BF16 = mybir.dt.bfloat16
AF = mybir.ActivationFunctionType
ALU = mybir.AluOpType
AX = mybir.AxisListType
BF16NP = ml_dtypes.bfloat16

B, T, D = 16, 2048, 1024
NCORE = 8
BPC = B // NCORE
TC = BPC * T
NTILE = TC // 128
NCH = T // 128
LN_EPS = 1e-5
GN_EPS = 1e-5

ENGINES = ("sync", "gpsimd", "tensor", "vector", "scalar")
ENG_SEM = {"sync": "ld", "gpsimd": "lw", "tensor": "mm", "vector": "vv", "scalar": "ss"}
DMA_ENG = {"sync", "gpsimd"}


class Plan:

    def __init__(self, gsfx=""):
        self.events = []
        self.gsfx = gsfx

    def ev(self, engine, fn, wait=(), tag=None, sem=None):
        assert engine in ENGINES
        if sem is None:
            sem = ENG_SEM[engine]
        elif engine == "gpsimd":
            sem = sem + self.gsfx
        self.events.append((engine, fn, tuple(wait), tag, False, sem))

    def barrier(self, engine, wait=()):
        self.events.append((engine, None, tuple(wait), None, True, None))

    def emit(self, nc, sems=None, bases=None, entry=None):
        semnames = sorted({e[5] for e in self.events if e[5] is not None})
        if sems is None:
            sems = {}
        if bases is None:
            bases = {}
        for n in semnames:
            if n not in sems:
                sems[n] = nc.semaphore(n).__enter__()
            bases.setdefault(n, 0)
        tagpos = {}
        tagidx = {}
        run = {n: bases[n] for n in semnames}
        for idx, (engine, fn, wait_tags, tag, is_b, sem) in enumerate(self.events):
            if not is_b:
                run[sem] += 16 if engine in DMA_ENG else 1
            if tag is not None:
                assert tag not in tagpos, f"dup tag {tag}"
                tagpos[tag] = (sem, run[sem])
                tagidx[tag] = idx
        with nc.Block() as block:
            for engine in ENGINES:
                evs = [(i, f, w, b) for i, (e, f, w, t, b, sm) in enumerate(self.events)
                       if e == engine]
                if not evs:
                    continue

                def body(eng, evs=evs, engine=engine):
                    seen = {n: bases[n] for n in semnames}
                    if entry:
                        for n, cnt in sorted(entry.items()):
                            if n in sems and cnt > 0:
                                eng.wait_ge(sems[n], cnt)
                                if n in seen:
                                    seen[n] = max(seen[n], cnt)
                    for gi, fn, wait_tags, is_b in evs:
                        for wt in wait_tags:
                            psem, pcount = tagpos[wt]
                            assert tagidx[wt] < gi, f"wait on later event {wt}"
                            if pcount > seen[psem]:
                                eng.wait_ge(sems[psem], pcount)
                                seen[psem] = pcount
                        if is_b:
                            continue
                        ins = fn(eng)
                        assert ins is not None, f"event on {engine} returned no instruction"
                        _, _, _, tag, _, sem = self.events[gi]
                        ins.then_inc(sems[sem], 16 if engine in DMA_ENG else 1)

                getattr(block, engine)(body)
        for n in semnames:
            bases[n] = run[n]
        return sems


import contextlib


class Alloc:

    _seq = 0

    def __init__(self, nc, es):
        self.nc = nc
        self.es = es
        Alloc._seq += 1
        self.pfx = f"g{Alloc._seq}_"

    def sb(self, name, shape, dtype):
        return self.es.enter_context(self.nc.sbuf_tensor(self.pfx + name, shape, dtype))

    def ps(self, name, shape, dtype):
        return self.es.enter_context(self.nc.psum_tensor(self.pfx + name, shape, dtype))


_HOST = {}


def host_tables():
    if _HOST:
        return _HOST
    t = np.arange(T, dtype=np.float64)
    i = np.arange(512, dtype=np.float64)
    base_scale = (2 * i + 0.4 * D) / (1.4 * D)
    scale = base_scale[None, :] ** (t[:, None] / 512.0)
    inv_freq = 1.0 / 10000.0 ** (i / 512.0)
    ang = t[:, None] * inv_freq[None, :]
    sin, cos = np.sin(ang), np.cos(ang)
    rep = lambda a: np.ascontiguousarray(np.repeat(a, 2, axis=1).T.astype(np.float32))
    _HOST["CQ0"] = rep(cos * scale)
    _HOST["SQ0"] = rep(sin * scale)
    _HOST["CK0"] = rep(cos / scale)
    _HOST["SK0"] = rep(sin / scale)
    R0 = np.zeros((128, 128), dtype=np.float32)
    for j in range(64):
        R0[2 * j + 1, 2 * j] = -1.0
        R0[2 * j, 2 * j + 1] = 1.0
    _HOST["R0"] = R0.astype(BF16NP)
    j32 = np.arange(32, dtype=np.float64)
    invf = 1.0 / 10000.0 ** (j32 / 32.0)
    ang1 = t[:, None] * invf[None, :]
    _HOST["C1"] = np.ascontiguousarray(np.tile(np.cos(ang1).T, (4, 1)).astype(np.float32))
    _HOST["S1"] = np.ascontiguousarray(np.tile(np.sin(ang1).T, (4, 1)).astype(np.float32))
    R1 = np.zeros((128, 128), dtype=np.float32)
    for h in range(2):
        for j in range(32):
            R1[64 * h + j + 32, 64 * h + j] = -1.0
            R1[64 * h + j, 64 * h + j + 32] = 1.0
    _HOST["R1"] = R1.astype(BF16NP)
    ii = np.arange(128, dtype=np.float32)
    EH = ii[None, :] - ii[:, None]
    _HOST["EH"] = np.where(EH >= 0, EH, 1.0e6).astype(np.float32)
    _HOST["ROWI"] = np.ascontiguousarray(np.broadcast_to(ii[None, :], (128, 128)).astype(np.float32))
    _HOST["COLJ"] = np.ascontiguousarray((128.0 - ii)[:, None].astype(np.float32))
    _HOST["ONESROW"] = np.ones((1, 128), dtype=np.float32)
    half2 = np.zeros((128, 2), dtype=np.float32)
    half2[:64, 0] = 1.0
    half2[64:, 1] = 1.0
    _HOST["HALF2"] = half2
    _HOST["HALF2T"] = np.ascontiguousarray(half2.T)
    dsa = np.zeros((128, 128), dtype=np.float32)
    dsa[:, :64] = 1.0
    _HOST["DENSEL_A"] = dsa
    _HOST["DENSEL_B"] = np.ascontiguousarray(1.0 - dsa)
    cc = np.zeros((128, 4), dtype=np.float32)
    cc[:, 0] = LN_EPS
    cc[:, 1] = 1e-6
    cc[:, 2] = -0.5 * math.log(128.0)
    _HOST["CONSTC"] = cc
    return _HOST


def emit_ln(nc, al, plan, x, h, cc, xout=None, xw=None):
    sb = al.sb
    ccs = sb("ln_cc", [128, 4], F32)
    xt = [sb(f"ln_xt{i}", [128, D], F32) for i in range(2)]
    ht = [sb(f"ln_ht{i}", [128, D], BF16) for i in range(2)]
    ssq = sb("ln_ssq", [128, 2], F32)
    ssum = sb("ln_sum", [128, 2], F32)
    mean = sb("ln_mean", [128, 2], F32)
    msq = sb("ln_msq", [128, 2], F32)
    var = sb("ln_var", [128, 2], F32)
    sd = sb("ln_sd", [128, 2], F32)
    rs = sb("ln_rs", [128, 2], F32)

    plan.ev("sync", lambda e: e.dma_start(ccs[:], cc), tag=("ln_cc",), sem="s_cc")
    for k in range(NTILE):
        p = k % 2
        c = lambda s_: s_[:, p:p + 1]
        w = (("ln_app", k - 2),) if k >= 2 else ()
        if xw is not None:
            w = w + tuple(xw(k))
        plan.ev("sync", lambda e, k=k, p=p: e.dma_start(xt[p][:], x[128 * k:128 * (k + 1), :]),
                wait=w, tag=("ln_ld", k), sem=f"s_x{p}")
        if xout is not None:
            plan.ev("sync", lambda e, k=k: e.dma_start(xout[128 * k:128 * (k + 1), :],
                                                       x[128 * k:128 * (k + 1), :]),
                    wait=tuple(xw(k)) if xw is not None else (), tag=("ln_xc", k))
        plan.ev("vector", lambda e, p=p: e.reduce_sum(ssum[:, p:p + 1], xt[p][:], AX.X),
                wait=(("ln_ld", k),), tag=("ln_sum", k))
        plan.ev("scalar", lambda e, p=p: e.activation(ht[p][:], xt[p][:], AF.Square,
                                                      accum_out=ssq[:, p:p + 1]),
                wait=(("ln_ld", k),) + ((("ln_var", k - 2), ("ln_st", k - 2))
                                        if k >= 2 else ()),
                tag=("ln_sq", k))
        plan.ev("vector", lambda e, p=p: e.tensor_single_scalar(mean[:, p:p + 1],
                                                                ssum[:, p:p + 1], 1.0 / D, ALU.mult),
                wait=(("ln_sum", k),), tag=("ln_mean", k))
        plan.ev("vector", lambda e, p=p: e.tensor_single_scalar(var[:, p:p + 1],
                                                                ssq[:, p:p + 1], 1.0 / D, ALU.mult),
                wait=(("ln_sq", k),), tag=("ln_ex2", k))
        plan.ev("vector", lambda e, p=p: e.tensor_mul(msq[:, p:p + 1], mean[:, p:p + 1],
                                                      mean[:, p:p + 1]),
                wait=(("ln_mean", k),), tag=("ln_msq", k))
        plan.ev("vector", lambda e, p=p: e.tensor_sub(var[:, p:p + 1], var[:, p:p + 1],
                                                      msq[:, p:p + 1]),
                wait=(("ln_ex2", k), ("ln_msq", k)), tag=("ln_var", k))
        plan.ev("scalar", lambda e, p=p: e.activation(sd[:, p:p + 1], var[:, p:p + 1],
                                                      AF.Sqrt, bias=ccs[:, 0:1]),
                wait=(("ln_var", k), ("ln_cc",)), tag=("ln_sd", k))
        plan.ev("vector", lambda e, p=p: e.reciprocal(rs[:, p:p + 1], sd[:, p:p + 1]),
                wait=(("ln_sd", k),), tag=("ln_rs", k))
        plan.ev("vector", lambda e, p=p: e.tensor_scalar(ht[p][:], xt[p][:], mean[:, p:p + 1],
                                                         rs[:, p:p + 1], ALU.subtract, ALU.mult),
                wait=(("ln_rs", k),) + ((("ln_st", k - 2),) if k >= 2 else ()),
                tag=("ln_app", k))
        plan.ev("sync", lambda e, k=k, p=p: e.dma_start(h[128 * k:128 * (k + 1), :], ht[p][:]),
                wait=(("ln_app", k),), tag=("ln_st", k), sem=f"s_h{p}")
    out_tags = [("ln_st", k) for k in range(NTILE)]
    if xout is not None:
        out_tags += [("ln_xc", k) for k in range(NTILE)]
    return out_tags


def load_hfm(nc, plan, sb, h_tm, name="hf", hw=None):
    hf = [sb(f"{name}{b}", [128, 8 * T], BF16) for b in range(BPC)]
    for b in range(BPC):
        for c in range(8):
            plan.ev("sync", lambda e, b=b, c=c: e.dma_start(
                hf[b][:, c * T:(c + 1) * T],
                h_tm[T * b:T * (b + 1), 128 * c:128 * (c + 1)], transpose=True),
                wait=tuple(hw(b)) if hw is not None else (),
                tag=(name, b, c), sem=f"sm_{name}{b}")
    ready = tuple((name, b, c) for b in range(BPC) for c in range(8))
    return hf, ready


def load_w_bf16(nc, plan, sb, w_ap, nchunk, ncol, name):
    w_sb = sb(name, [128, nchunk * ncol], BF16)
    for c in range(nchunk):
        plan.ev("gpsimd", lambda e, c=c: e.dma_start(
            w_sb[:, c * ncol:(c + 1) * ncol], w_ap[128 * c:128 * (c + 1), :]),
            tag=(name, c), sem="sm_" + name)
    return w_sb, tuple((name, c) for c in range(nchunk))


def emit_l0_qkv(nc, al, plan, h_tm, wq, wk, cq, sq_t, ck, sk_t, r0, qf, kf, hfp=None,
                mate=None):
    sb = al.sb
    hf, hf_rdy = hfp if hfp is not None else load_hfm(nc, plan, sb, h_tm)
    wq_sb, wq_rdy = load_w_bf16(nc, plan, sb, wq, 8, 1024, "wq_sb")
    wk_sb, wk_rdy = load_w_bf16(nc, plan, sb, wk, 8, 1024, "wk_sb")
    r0_sb = sb("r0_sb", [128, 128], BF16)
    plan.ev("sync", lambda e: e.dma_start(r0_sb[:], r0), tag=("r0",), sem="s_r0")
    ct = [sb(f"rq_ct{i}", [128, 512], F32) for i in range(2)]
    st_ = [sb(f"rq_st{i}", [128, 512], F32) for i in range(2)]
    tA = [sb(f"rq_tA{i}", [128, 512], F32) for i in range(2)]
    tB = [sb(f"rq_tB{i}", [128, 512], F32) for i in range(2)]
    qraw = [sb(f"rq_qraw{i}", [128, 512], BF16) for i in range(2)]
    qout = [sb(f"rq_qout{i}", [128, 512], BF16) for i in range(2)]
    ps_q = [al.ps(f"rq_psq{i}", [128, 512], F32) for i in range(2)]
    ps_r = [al.ps(f"rq_psr{i}", [128, 512], F32) for i in range(2)]

    units = []
    for pi, (w_sb, w_rdy, ctab, stab, outt) in enumerate(
            ((wq_sb, wq_rdy, cq, sq_t, qf), (wk_sb, wk_rdy, ck, sk_t, kf))):
        for b in range(BPC):
            for oc in range(8):
                for ts in range(4):
                    units.append((pi, w_sb, w_rdy, ctab, stab, outt, b, oc, ts))
    NU = len(units)

    def rq_head(u):
        (pi, w_sb, w_rdy, ctab, stab, outt, b, oc, ts) = units[u]
        p = u % 2
        wprev = lambda t: ((t, u - 2),) if u >= 2 else ()
        plan.ev("gpsimd", lambda e, p=p, ctab=ctab, oc=oc, ts=ts: e.dma_start(
            ct[p][:], ctab[128 * oc:128 * (oc + 1), 512 * ts:512 * (ts + 1)]),
            wait=wprev("rq_tA"), tag=("rq_ct", u), sem=f"s_ct{p}")
        plan.ev("gpsimd", lambda e, p=p, stab=stab, oc=oc, ts=ts: e.dma_start(
            st_[p][:], stab[128 * oc:128 * (oc + 1), 512 * ts:512 * (ts + 1)]),
            wait=wprev("rq_tB"), tag=("rq_st", u), sem=f"s_st{p}")

        def mmq(e, p=p, w_sb=w_sb, b=b, oc=oc, ts=ts):
            for dc in range(8):
                ins = e.matmul(ps_q[p][:], w_sb[:, dc * 1024 + oc * 128:dc * 1024 + oc * 128 + 128],
                               hf[b][:, dc * T + ts * 512:dc * T + ts * 512 + 512],
                               start=(dc == 0), stop=(dc == 7))
            return ins
        plan.ev("tensor", mmq, wait=(w_rdy[-1], hf_rdy[8 * b + 7])
                + wprev("rq_tA") + wprev("rq_qcp"), tag=("rq_mmq", u))
        plan.ev("scalar", lambda e, p=p: e.copy(qraw[p][:], ps_q[p][:]),
                wait=(("rq_mmq", u),) + wprev("rq_mmr"), tag=("rq_qcp", u))
        plan.ev("vector", lambda e, p=p: e.tensor_mul(tA[p][:], ps_q[p][:], ct[p][:]),
                wait=(("rq_mmq", u), ("rq_qcp", u), ("rq_ct", u)) + wprev("rq_qadd"),
                tag=("rq_tA", u))

    def rq_tail(u):
        (pi, w_sb, w_rdy, ctab, stab, outt, b, oc, ts) = units[u]
        p = u % 2
        wprev = lambda t: ((t, u - 2),) if u >= 2 else ()
        plan.ev("tensor", lambda e, p=p: e.matmul(ps_r[p][:], r0_sb[:], qraw[p][:],
                                                  start=True, stop=True),
                wait=(("rq_qcp", u), ("r0",)) + wprev("rq_tB"), tag=("rq_mmr", u))
        plan.ev("vector", lambda e, p=p: e.tensor_mul(tB[p][:], ps_r[p][:], st_[p][:]),
                wait=(("rq_mmr", u), ("rq_st", u)) + wprev("rq_qadd"), tag=("rq_tB", u))
        plan.ev("vector", lambda e, p=p: e.tensor_add(qout[p][:], tA[p][:], tB[p][:]),
                wait=(("rq_tA", u), ("rq_tB", u)) + wprev("rq_qst"), tag=("rq_qadd", u))
        plan.ev("sync", lambda e, p=p, outt=outt, b=b, oc=oc, ts=ts: e.dma_start(
            outt[oc, :, T * b + 512 * ts:T * b + 512 * (ts + 1)], qout[p][:]),
            wait=(("rq_qadd", u),), tag=("rq_qst", u), sem=f"s_qo{p}")

    for u in range(NU):
        rq_head(u)
        if u >= 1:
            rq_tail(u - 1)
        if mate is not None:
            mate(u)
    rq_tail(NU - 1)
    return [("rq_qst", u) for u in range(len(units))]


def emit_l0_vg(nc, al, plan, h_tm, specs, hfp=None, defer=False):
    sb = al.sb
    hf, hf_rdy = hfp if hfp is not None else load_hfm(nc, plan, sb, h_tm)
    stage = [sb(f"vg_st{i}", [128, 512], BF16) for i in range(2)]
    ps = [al.ps(f"vg_ps{i}", [128, 512], F32) for i in range(2)]

    units = []
    for pi, (w_ap, ncols, outt, act) in enumerate(specs):
        w_sb, w_rdy = load_w_bf16(nc, plan, sb, w_ap, 8, ncols, f"vgw{pi}")
        for b in range(BPC):
            for tt in range(16):
                for vs in range(ncols // 512):
                    units.append((pi, w_sb, w_rdy, ncols, outt, act, b, tt, vs))
    def vg_unit(u):
        (pi, w_sb, w_rdy, ncols, outt, act, b, tt, vs) = units[u]
        p = u % 2
        wprev = lambda t: ((t, u - 2),) if u >= 2 else ()

        def mm(e, p=p, w_sb=w_sb, ncols=ncols, b=b, tt=tt, vs=vs):
            for dc in range(8):
                ins = e.matmul(ps[p][:], hf[b][:, dc * T + tt * 128:dc * T + tt * 128 + 128],
                               w_sb[:, dc * ncols + vs * 512:dc * ncols + vs * 512 + 512],
                               start=(dc == 0), stop=(dc == 7))
            return ins
        plan.ev("tensor", mm, wait=(w_rdy[-1], hf_rdy[8 * b + 7]) + wprev("vg_ev"),
                tag=("vg_mm", u))
        if act is None:
            plan.ev("scalar", lambda e, p=p: e.copy(stage[p][:], ps[p][:]),
                    wait=(("vg_mm", u),) + wprev("vg_st"), tag=("vg_ev", u))
        else:
            plan.ev("scalar", lambda e, p=p, act=act: e.activation(stage[p][:], ps[p][:], act),
                    wait=(("vg_mm", u),) + wprev("vg_st"), tag=("vg_ev", u))
        plan.ev("sync", lambda e, p=p, outt=outt, b=b, tt=tt, vs=vs: e.dma_start(
            outt[T * b + 128 * tt:T * b + 128 * (tt + 1), 512 * vs:512 * (vs + 1)],
            stage[p][:]),
            wait=(("vg_ev", u),), tag=("vg_st", u), sem=f"s_vo{p}")
    return vg_unit, len(units), [("vg_st", u) for u in range(len(units))]


def emit_l0_ret(nc, al, plan, qf, kf, v_tm, gate_tm, decay_col, eh, rowi, colj, cc, gg_tm):
    sb = al.sb
    ps = al.ps
    dc_sb = sb("rt_dc", [128, 8], F32)
    eh_sb = sb("rt_eh", [128, 128], F32)
    ri_sb = sb("rt_ri", [128, 128], F32)
    cj_sb = sb("rt_cj", [128, 1], F32)
    ccs = sb("rt_cc", [128, 4], F32)
    lncol = sb("rt_lncol", [128, 8], F32)
    MT = sb("rt_MT", [128, 8 * 128], F32)
    BH = sb("rt_BH", [128, 8 * 128], BF16)
    cvec = sb("rt_cvec", [128, 8], F32)
    Dcol = sb("rt_Dcol", [128, 8], F32)
    tmpm = sb("rt_tmpm", [128, 128], F32)
    tmpb = tmpm
    tmpc = sb("rt_tmpc", [128, 8], F32)
    S_f = sb("rt_Sf", [128, 8 * 256], F32)
    S_b = sb("rt_Sb", [128, 8 * 256], BF16)
    Qc = [sb(f"rt_Qc{i}", [128, 8 * 128], BF16) for i in range(2)]
    Kc = [sb(f"rt_Kc{i}", [128, 8 * 128], BF16) for i in range(2)]
    Vc = [sb(f"rt_Vc{i}", [128, 2048], BF16) for i in range(2)]
    Ktm = [sb(f"rt_Ktm{i}", [128, 1024], BF16) for i in range(2)]
    gatec = [sb(f"rt_gatec{i}", [128, 2048], BF16) for i in range(2)]
    Am = [sb(f"rt_Am{i}", [128, 128], BF16) for i in range(2)]
    Qd = [sb(f"rt_Qd{i}", [128, 128], BF16) for i in range(2)]
    Kw = [sb(f"rt_Kw{i}", [128, 128], BF16) for i in range(2)]
    ret = [sb(f"rt_ret{i}", [128, 2048], F32) for i in range(3)]
    ggst = [sb(f"rt_ggst{i}", [128, 2048], BF16) for i in range(2)]
    gtmp = [sb(f"rt_gtmp{i}", [128, 256], BF16) for i in range(2)]
    sqsc = sb("rt_sqsc", [128, 256], BF16)
    sum8 = [sb(f"rt_sum8{i}", [128, 8], F32) for i in range(3)]
    ssq8 = [sb(f"rt_ssq8{i}", [128, 8], F32) for i in range(3)]
    mean8 = [sb(f"rt_mean8{i}", [128, 8], F32) for i in range(3)]
    msq8 = [sb(f"rt_msq8{i}", [128, 8], F32) for i in range(3)]
    var8 = [sb(f"rt_var8{i}", [128, 8], F32) for i in range(3)]
    sd8 = [sb(f"rt_sd8{i}", [128, 8], F32) for i in range(3)]
    rs8 = [sb(f"rt_rs8{i}", [128, 8], F32) for i in range(3)]
    ps_a = [ps(f"rt_psa{i}", [128, 128], F32) for i in range(2)]
    ps_o = [ps(f"rt_pso{i}", [128, 256], F32) for i in range(2)]
    ps_s = [ps(f"rt_pss{i}", [128, 256], F32) for i in range(2)]

    for name, dst, src in (("rt_pdc", dc_sb, decay_col), ("rt_peh", eh_sb, eh),
                           ("rt_pri", ri_sb, rowi), ("rt_pcj", cj_sb, colj),
                           ("rt_pcc", ccs, cc)):
        plan.ev("sync", lambda e, dst=dst, src=src: e.dma_start(dst[:], src),
                tag=(name,), sem="s_pre")
    plan.ev("scalar", lambda e: e.activation(lncol[:], dc_sb[:], AF.Ln),
            wait=(("rt_pcc",),), tag=("rt_lnc",))
    for h in range(8):
        plan.ev("vector", lambda e, h=h: e.tensor_scalar_mul(tmpm[:], eh_sb[:], lncol[:, h:h + 1]),
                wait=(("rt_lnc",),) + ((("rt_BH", h - 1),) if h else ()),
                tag=("rt_tm", h))
        plan.ev("scalar", lambda e, h=h: e.activation(MT[:, 128 * h:128 * (h + 1)], tmpm[:],
                                                      AF.Exp, bias=ccs[:, 2:3]),
                wait=(("rt_tm", h),), tag=("rt_MT", h))
        plan.ev("vector", lambda e, h=h: e.tensor_scalar_mul(tmpb[:], ri_sb[:], lncol[:, h:h + 1]),
                wait=(("rt_lnc",), ("rt_MT", h)),
                tag=("rt_tb", h))
        plan.ev("scalar", lambda e, h=h: e.activation(BH[:, 128 * h:128 * (h + 1)], tmpb[:],
                                                      AF.Exp, bias=ccs[:, 2:3]),
                wait=(("rt_tb", h),), tag=("rt_BH", h))
        plan.ev("vector", lambda e, h=h: e.tensor_scalar_mul(tmpc[:, h:h + 1], cj_sb[:],
                                                             lncol[:, h:h + 1]),
                wait=(("rt_lnc",),), tag=("rt_tc", h))
        plan.ev("scalar", lambda e, h=h: e.activation(cvec[:, h:h + 1], tmpc[:, h:h + 1], AF.Exp),
                wait=(("rt_tc", h),), tag=("rt_cv", h))
    plan.ev("vector", lambda e: e.tensor_single_scalar(msq8[0][:], lncol[:], 128.0, ALU.mult),
            wait=(("rt_lnc",),), tag=("rt_td",))
    plan.ev("scalar", lambda e: e.activation(Dcol[:], msq8[0][:], AF.Exp),
            wait=(("rt_td",),), tag=("rt_Dc",))

    NH = 8
    def emit_gn(g):
            p2 = g % 3
            wg2p = lambda t: ((t, g - 3),) if g >= 3 else ()
            for h in range(8):
                H = g * NH + h
                plan.ev("scalar", lambda e, h=h, p2=p2: e.activation(
                    sqsc[:], ret[p2][:, 256 * h:256 * (h + 1)], AF.Square,
                    accum_out=ssq8[p2][:, h:h + 1]),
                    wait=(("rt_rc", H),) + wg2p("rt_gv2"),
                    tag=("rt_gs", g, h))
            plan.ev("vector", lambda e, p2=p2: e.tensor_single_scalar(
                mean8[p2][:], sum8[p2][:], 1.0 / 256, ALU.mult),
                    wait=(("rt_rc", g * NH + 7),) +
                    ((("rt_gt", g - 3, 7),) if g >= 3 else ()), tag=("rt_gv1", g))
            plan.ev("vector", lambda e, p2=p2: e.tensor_single_scalar(
                var8[p2][:], ssq8[p2][:], 1.0 / 256, ALU.mult),
                    wait=(("rt_gs", g, 7),) + wg2p("rt_gsd"), tag=("rt_gv2", g))
            plan.ev("vector", lambda e, p2=p2: e.tensor_mul(
                msq8[p2][:], mean8[p2][:], mean8[p2][:]),
                    wait=(("rt_gv1", g), ("rt_Dc",)), tag=("rt_gv3", g))
            plan.ev("vector", lambda e, p2=p2: e.tensor_sub(
                var8[p2][:], var8[p2][:], msq8[p2][:]),
                    wait=(("rt_gv2", g), ("rt_gv3", g)), tag=("rt_gv4", g))
            plan.ev("scalar", lambda e, p2=p2: e.activation(
                sd8[p2][:], var8[p2][:], AF.Sqrt, bias=ccs[:, 0:1]),
                    wait=(("rt_gv4", g), ("rt_pcc",)) + wg2p("rt_grs"),
                    tag=("rt_gsd", g))
            plan.ev("vector", lambda e, p2=p2: e.reciprocal(rs8[p2][:], sd8[p2][:]),
                    wait=(("rt_gsd", g),) +
                    ((("rt_gt", g - 3, 7),) if g >= 3 else ()), tag=("rt_grs", g))
            for h in range(8):
                hp = h % 2
                plan.ev("vector", lambda e, hp=hp, h=h, p2=p2: e.tensor_scalar(
                    gtmp[hp][:], ret[p2][:, 256 * h:256 * (h + 1)],
                    mean8[p2][:, h:h + 1], rs8[p2][:, h:h + 1],
                    ALU.subtract, ALU.mult),
                    wait=(("rt_grs", g),) + ((("rt_gg", g, h - 2),) if h >= 2 else ()),
                    tag=("rt_gt", g, h))
                plan.ev("vector", lambda e, hp=hp, h=h, p2=p2, pg=g % 2: e.tensor_mul(
                    ggst[pg][:, 256 * h:256 * (h + 1)], gtmp[hp][:],
                    gatec[pg][:, 256 * h:256 * (h + 1)]),
                    wait=(("rt_gt", g, h), ("rt_lg", g)) +
                    ((("rt_gst", g - 2),) if g >= 2 else ()),
                    tag=("rt_gg", g, h))
            plan.ev("gpsimd", lambda e, g=g, pg=g % 2: e.dma_start(
                gg_tm[128 * g:128 * (g + 1), :], ggst[pg][:]),
                wait=(("rt_gg", g, 7),), tag=("rt_gst", g), sem=f"s_go{g % 2}")
    for b in range(BPC):
        wmf = (("rt_sbcg", 15),) if b else ()
        wmb = (("rt_mo", 127),) if b else ()
        plan.ev("gpsimd", lambda e: e.memset(S_f[:], 0.0), wait=wmf, tag=("rt_msf", b))
        plan.ev("gpsimd", lambda e: e.memset(S_b[:], 0.0), wait=wmb, tag=("rt_msb", b))
        for n in range(NCH):
            g = b * NCH + n
            p = g % 2
            wg2 = lambda t, d=2: ((t, g - d),) if g >= d else ()
            plan.ev("sync", lambda e, p=p, g=g: e.dma_start(
                Qc[p][:].rearrange("p (c q) -> p c q", c=8),
                qf[:, :, 128 * g:128 * (g + 1)].rearrange("c p q -> p c q")),
                wait=(("rt_qd", (g - 2) * NH + 7), ("rt_ma", (g - 2) * NH + 7)) if g >= 2 else (),
                tag=("rt_lq", g), sem=f"s_q{p}")
            plan.ev("sync", lambda e, p=p, g=g: e.dma_start(
                Kc[p][:].rearrange("p (c q) -> p c q", c=8),
                kf[:, :, 128 * g:128 * (g + 1)].rearrange("c p q -> p c q")),
                wait=(("rt_ma", (g - 2) * NH + 7),) if g >= 2 else (),
                tag=("rt_lk", g), sem=f"s_k{p}")
            plan.ev("sync", lambda e, pv=g % 2, g=g: e.dma_start(
                Vc[pv][:], v_tm[128 * g:128 * (g + 1), :]),
                wait=(("rt_mss", (g - 2) * NH + 7),) if g >= 2 else (),
                tag=("rt_lv", g), sem=f"s_v{g % 2}")
            plan.ev("sync", lambda e, p=p, g=g: e.dma_start(
                Ktm[p][:], kf[:, :, 128 * g:128 * (g + 1)].rearrange("c p q -> (c p) q"),
                transpose=True),
                wait=(("rt_kw", (g - 2) * NH + 7),) if g >= 2 else (),
                tag=("rt_lkt", g), sem=f"s_kt{p}")
            if g < 2:
                plan.ev("gpsimd", lambda e, pv=g % 2, g=g: e.dma_start(
                    gatec[pv][:], gate_tm[128 * g:128 * (g + 1), :]),
                    wait=(), tag=("rt_lg", g), sem=f"s_g{g % 2}")
            def head_pre(h):
                H = g * NH + h
                hp = h % 2
                hp = h % 2
                plan.ev("tensor", lambda e, p=p, hp=hp, h=h: e.matmul(
                    ps_a[hp][:], Kc[p][:, 128 * h:128 * (h + 1)],
                    Qc[p][:, 128 * h:128 * (h + 1)], start=True, stop=True),
                    wait=(("rt_lk", g), ("rt_lq", g)) + ((("rt_am", H - 2),) if H >= 2 else ()),
                    tag=("rt_ma", H))
                plan.ev("vector", lambda e, hp=hp, h=h: e.tensor_mul(
                    Am[hp][:], ps_a[hp][:], MT[:, 128 * h:128 * (h + 1)]),
                    wait=(("rt_ma", H), ("rt_MT", h)) + ((("rt_mo", H - 2),) if H >= 2 else ()),
                    tag=("rt_am", H))
                plan.ev("vector", lambda e, p=p, hp=hp, h=h: e.tensor_mul(
                    Qd[hp][:], Qc[p][:, 128 * h:128 * (h + 1)], BH[:, 128 * h:128 * (h + 1)]),
                    wait=(("rt_lq", g), ("rt_BH", h)) + ((("rt_mo", H - 2),) if H >= 2 else ()),
                    tag=("rt_qd", H))

            def head_main(h):
                H = g * NH + h
                hp = h % 2
                def mmo(e, pv=g % 2, hp=hp, h=h):
                    e.matmul(ps_o[hp][:], Am[hp][:], Vc[pv][:, 256 * h:256 * (h + 1)],
                             start=True, stop=False)
                    return e.matmul(ps_o[hp][:], Qd[hp][:], S_b[:, 256 * h:256 * (h + 1)],
                                    start=False, stop=True)
                plan.ev("tensor", mmo,
                        wait=(("rt_am", H), ("rt_qd", H), ("rt_lv", g),
                              (("rt_sbcg", g - 1) if n > 0 else ("rt_msb", b))) +
                             ((("rt_rc", H - 2),) if H >= 2 else ()),
                        tag=("rt_mo", H))
                plan.ev("scalar", lambda e, hp=hp, h=h, p2=g % 3: e.activation(
                    ret[p2][:, 256 * h:256 * (h + 1)], ps_o[hp][:], AF.Copy,
                    accum_out=sum8[p2][:, h:h + 1]),
                    wait=(("rt_mo", H),) + ((("rt_gt", g - 3, h),) if g >= 3 else ())
                    + ((("rt_gv1", g - 3),) if g >= 3 else ()),
                    tag=("rt_rc", H))
                plan.ev("vector", lambda e, p=p, hp=hp, h=h: e.tensor_scalar_mul(
                    Kw[hp][:], Ktm[p][:, 128 * h:128 * (h + 1)], cvec[:, h:h + 1]),
                    wait=(("rt_lkt", g), ("rt_cv", h)) + ((("rt_mss", H - 2),) if H >= 2 else ()),
                    tag=("rt_kw", H))
                plan.ev("tensor", lambda e, pv=g % 2, hp=hp, h=h: e.matmul(
                    ps_s[hp][:], Kw[hp][:], Vc[pv][:, 256 * h:256 * (h + 1)],
                    start=True, stop=True),
                    wait=(("rt_kw", H), ("rt_lv", g)) + ((("rt_su", H - 2),) if H >= 2 else ()),
                    tag=("rt_mss", H))
                plan.ev("vector", lambda e, hp=hp, h=h: e.scalar_tensor_tensor(
                    S_f[:, 256 * h:256 * (h + 1)], S_f[:, 256 * h:256 * (h + 1)],
                    Dcol[:, h:h + 1], ps_s[hp][:], ALU.mult, ALU.add),
                    wait=(("rt_mss", H), ("rt_Dc",)) +
                         ((("rt_msf", b),) if n == 0 else ()),
                    tag=("rt_su", H))
            for h in range(8):
                head_pre(h)
                if h >= 1:
                    head_main(h - 1)
            head_main(7)
            plan.ev("scalar", lambda e: e.copy(S_b[:], S_f[:]),
                    wait=(("rt_su", g * NH + 7),), tag=("rt_sbcg", g))
            if g >= 2:
                emit_gn(g - 2)
                plan.ev("gpsimd", lambda e, pv=g % 2, g=g: e.dma_start(
                    gatec[pv][:], gate_tm[128 * g:128 * (g + 1), :]),
                    wait=(("rt_gg", g - 2, 7),), tag=("rt_lg", g),
                    sem=f"s_g{g % 2}")

    emit_gn(BPC * NCH - 2)
    emit_gn(BPC * NCH - 1)
    return [("rt_gst", g) for g in range(BPC * NCH)]


def emit_proj_res(nc, al, plan, act_tm, w, x_in, x_out, nck=16, afw=None):
    sb = al.sb
    w_sb, w_rdy = load_w_bf16(nc, plan, sb, w, nck, D, "pj_w")
    af = sb("pj_af", [128, nck * T], BF16)
    xt = [sb(f"pj_xt{i}", [128, D], F32) for i in range(2)]
    xo = [sb(f"pj_xo{i}", [128, D], F32) for i in range(2)]
    pso = [al.ps(f"pj_ps{i}", [128, 512], F32) for i in range(2)]

    for b in range(BPC):
        for c in range(nck):
            wtr = (("pj_mm", b * 16 - 1, 1),) if b >= 1 else ()
            if afw is not None:
                wtr = wtr + tuple(afw(b))
            plan.ev("sync", lambda e, b=b, c=c: e.dma_start(
                af[:, c * T:(c + 1) * T],
                act_tm[T * b:T * (b + 1), 128 * c:128 * (c + 1)], transpose=True),
                wait=wtr, tag=("pj_af", b, c), sem=f"s_af{c % 2}")
        for tt in range(16):
            u = b * 16 + tt
            p = u % 2
            plan.ev("sync", lambda e, b=b, tt=tt, p=p: e.dma_start(
                xt[p][:], x_in[T * b + 128 * tt:T * b + 128 * (tt + 1), :]),
                wait=(("pj_add", u - 2, 1),) if u >= 2 else (),
                tag=("pj_xt", u), sem=f"s_x{p}")
            for ds in range(2):
                def mm(e, b=b, tt=tt, ds=ds):
                    for c in range(nck):
                        ins = e.matmul(pso[ds][:],
                                       af[:, c * T + tt * 128:c * T + tt * 128 + 128],
                                       w_sb[:, c * D + ds * 512:c * D + ds * 512 + 512],
                                       start=(c == 0), stop=(c == nck - 1))
                    return ins
                plan.ev("tensor", mm,
                        wait=(w_rdy[-1], ("pj_af", b, nck - 2), ("pj_af", b, nck - 1)) +
                             ((("pj_add", u - 1, ds),) if u >= 1 else ()),
                        tag=("pj_mm", u, ds))
                plan.ev("vector", lambda e, ds=ds, p=p: e.tensor_add(
                    xo[p][:, 512 * ds:512 * (ds + 1)], pso[ds][:],
                    xt[p][:, 512 * ds:512 * (ds + 1)]),
                    wait=(("pj_mm", u, ds), ("pj_xt", u)) +
                         ((("pj_st", u - 2),) if u >= 2 else ()),
                    tag=("pj_add", u, ds))
            plan.ev("sync", lambda e, b=b, tt=tt, p=p: e.dma_start(
                x_out[T * b + 128 * tt:T * b + 128 * (tt + 1), :], xo[p][:]),
                wait=(("pj_add", u, 0), ("pj_add", u, 1)),
                tag=("pj_st", u), sem=f"s_xo{p}")
    return [("pj_st", u) for u in range(BPC * 16)]


def emit_moe(nc, al, plan, h_tm, wg, w1l, w2l, w3l, b1a, b2a, b3a, onesrow, act,
             x_in, x_out, use_shared, hfp=None):
    E = len(w1l)
    glu = w2l is not None
    sb = al.sb
    ps = al.ps
    hfs = sb("mo_hfs", [128, 8 * T], BF16)

    def reload_hf(ei, b, wait):
        for c in range(8):
            plan.ev("sync", lambda e, b=b, c=c: e.dma_start(
                hfs[:, c * T:(c + 1) * T],
                h_tm[T * b:T * (b + 1), 128 * c:128 * (c + 1)], transpose=True),
                wait=wait if c == 0 else (), tag=("mo_hf", ei, b, c), sem="sm_mo_hf")
    wg_sb, wg_rdy = load_w_bf16(nc, plan, sb, wg, 8, 8, "mo_wg")
    w1_sb = sb("mo_w1", [128, 8 * 2048], BF16)
    w2_sb = sb("mo_w2", [128, 8 * 2048], BF16) if glu else None
    w3_sb = sb("mo_w3", [128, 16 * D], BF16)
    onesr = sb("mo_ones", [1, 128], BF16)
    if b1a is not None:
        b1_sb = sb("mo_b1", [128, E * 16], F32)
        b2_sb = sb("mo_b2", [128, E * 16], F32)
        b3_sb = sb("mo_b3", [1, D], BF16)
        plan.ev("gpsimd", lambda e: e.dma_start(b1_sb[:], b1a), tag=("mo_b1",), sem="s_pre2")
        plan.ev("gpsimd", lambda e: e.dma_start(b2_sb[:], b2a), tag=("mo_b2",), sem="s_pre2")
        plan.ev("gpsimd", lambda e: e.dma_start(onesr[:], onesrow),
                tag=("mo_on",), sem="s_pre2")
        pre2 = (("mo_on",),)
    else:
        pre2 = ()
    comb = sb("mo_comb", [128, NTILE * E], F32)
    m1c = sb("mo_m1", [128, 2], F32)
    m2c = sb("mo_m2", [128, 2], F32)
    ddc = sb("mo_dd", [128, 2], F32)
    w1c = sb("mo_w1c", [128, 2], F32)
    w2c = sb("mo_w2c", [128, 2], F32)
    mk1 = sb("mo_mk1", [128, 2 * 8], F32)
    mk2 = sb("mo_mk2", [128, 2 * 8], F32)
    gg2 = sb("mo_gg2", [128, 2 * 8], F32)
    t1c = sb("mo_t1", [128, 2 * 8], F32)
    sg = [sb(f"mo_sg{i}", [128, 512], F32) for i in range(2)]
    he = [sb(f"mo_he{i}", [128, 16 * 512], BF16) for i in range(2)]
    st4 = [sb(f"mo_st4{i}", [128, D], F32) for i in range(2)]
    pgt = [ps(f"mo_pgt{i}", [128, 8], F32) for i in range(2)]
    pg = [ps(f"mo_pg{i}", [128, 512], F32) for i in range(2)]
    pv = [ps(f"mo_pv{i}", [128, 512], F32) for i in range(2)] if glu else None
    po = [ps(f"mo_po{i}", [128, 512], F32) for i in range(2)]

    if x_in is not None:
        for k in range(NTILE):
            plan.ev("gpsimd", lambda e, k=k: e.dma_start(x_out[128 * k:128 * (k + 1), :],
                                                         x_in[128 * k:128 * (k + 1), :]),
                    tag=("mo_xcp", k), sem=f"s_mac{k % 2}")
        acc_prev0 = (("mo_xcp", NTILE - 2), ("mo_xcp", NTILE - 1))
    else:
        acc_prev0 = ()

    if use_shared:
        plan.ev("gpsimd", lambda e: e.memset(comb[:], 1.0), tag=("mo_cmbi",))
        cmbi = (("mo_cmbi",),)
    else:
        cmbi = ()
    reload_hf(0, 0, ())

    def emit_gating(k):
        b, tt = divmod(k, 16)
        p = k % 2

        def mmg(e, b=b, tt=tt, p=p):
            for dc in range(8):
                ins = e.matmul(pgt[p][:], hfs[:, dc * T + tt * 128:dc * T + tt * 128 + 128],
                               wg_sb[:, dc * 8:dc * 8 + 8],
                               start=(dc == 0), stop=(dc == 7))
            return ins
        plan.ev("tensor", mmg, wait=(wg_rdy[-1], ("mo_hf", 0, b, 7)) +
                ((("mo_mk2", k - 2),) if k >= 2 else ()), tag=("mo_gmm", k))
        plan.ev("vector", lambda e, p=p: e.reduce_max(m1c[:, p:p + 1], pgt[p][:], AX.X),
                wait=(("mo_gmm", k),) + ((("mo_cw2", k - 2),) if k >= 2 else ()),
                tag=("mo_m1", k))
        plan.ev("vector", lambda e, p=p: e.tensor_single_scalar(
            mk1[:, 8 * p:8 * (p + 1)], pgt[p][:], m1c[:, p:p + 1], ALU.is_ge),
            wait=(("mo_m1", k),) + ((("mo_ct1", k - 2),) if k >= 2 else ()),
            tag=("mo_mk1", k))
        plan.ev("vector", lambda e, p=p: e.scalar_tensor_tensor(
            gg2[:, 8 * p:8 * (p + 1)], mk1[:, 8 * p:8 * (p + 1)], -1.0e30, pgt[p][:],
            ALU.mult, ALU.add),
            wait=(("mo_mk1", k),), tag=("mo_g2", k))
        plan.ev("vector", lambda e, p=p: e.reduce_max(m2c[:, p:p + 1],
                                                      gg2[:, 8 * p:8 * (p + 1)], AX.X),
                wait=(("mo_g2", k),), tag=("mo_m2", k))
        plan.ev("vector", lambda e, p=p: e.tensor_single_scalar(
            mk2[:, 8 * p:8 * (p + 1)], gg2[:, 8 * p:8 * (p + 1)], m2c[:, p:p + 1], ALU.is_ge),
            wait=(("mo_m2", k),), tag=("mo_mk2", k))
        plan.ev("vector", lambda e, p=p: e.tensor_sub(ddc[:, p:p + 1], m1c[:, p:p + 1],
                                                      m2c[:, p:p + 1]),
                wait=(("mo_m2", k),), tag=("mo_dd", k))
        plan.ev("scalar", lambda e, p=p: e.activation(w1c[:, p:p + 1], ddc[:, p:p + 1],
                                                      AF.Sigmoid),
                wait=(("mo_dd", k),), tag=("mo_cw1", k))
        plan.ev("vector", lambda e, p=p: e.tensor_scalar(w2c[:, p:p + 1], w1c[:, p:p + 1],
                                                         -1.0, 1.0, ALU.mult, ALU.add),
                wait=(("mo_cw1", k),), tag=("mo_cw2", k))
        plan.ev("vector", lambda e, p=p: e.tensor_scalar_mul(
            t1c[:, 8 * p:8 * (p + 1)], mk1[:, 8 * p:8 * (p + 1)], w1c[:, p:p + 1]),
            wait=(("mo_cw1", k),), tag=("mo_ct1", k))
        plan.ev("vector", lambda e, p=p, k=k, E=E: e.scalar_tensor_tensor(
            comb[:, E * k:E * k + 8], mk2[:, 8 * p:8 * (p + 1)], w2c[:, p:p + 1],
            t1c[:, 8 * p:8 * (p + 1)], ALU.mult, ALU.add),
            wait=(("mo_cw2", k), ("mo_ct1", k)) + cmbi, tag=("mo_cmb", k))

    for k in range(16):
        emit_gating(k)
    gating_b1_pending = [True]

    def emit_w12(ei):
        for c in range(8):
            if ei >= 1:
                gu_last = ((ei - 1) * 8 + 7) * 16 + 15
                wl = (("mo_gv", ei - 1, 7, 15),) if glu else (("mo_g1", gu_last),)
            else:
                wl = ()
            plan.ev("gpsimd", lambda e, ei=ei, c=c: e.dma_start(
                w1_sb[:, c * 2048:(c + 1) * 2048], w1l[ei][128 * c:128 * (c + 1), :]),
                wait=wl, tag=("mo_lw1", ei, c), sem="s_w1")
            if glu:
                plan.ev("gpsimd", lambda e, ei=ei, c=c: e.dma_start(
                    w2_sb[:, c * 2048:(c + 1) * 2048], w2l[ei][128 * c:128 * (c + 1), :]),
                    wait=wl, tag=("mo_lw2", ei, c), sem="s_w2")

    def emit_w3(ei):
        wl = (("mo_w3", ei - 1, 7, 3, 1),) if ei >= 1 else ()
        if b1a is not None:
            plan.ev("gpsimd", lambda e, ei=ei: e.dma_start(b3_sb[:], b3a[ei:ei + 1, :]),
                    wait=wl, tag=("mo_lb3", ei), sem="s_b3")
        for c in range(16):
            plan.ev("gpsimd", lambda e, ei=ei, c=c: e.dma_start(
                w3_sb[:, c * D:(c + 1) * D], w3l[ei][128 * c:128 * (c + 1), :]),
                wait=wl, tag=("mo_lw3", ei, c), sem="s_w3")

    emit_w12(0)
    emit_w3(0)

    def gv_section(ei, tsl, w1rdy):
        b, ts = divmod(tsl, 4)
        tp = tsl % 2
        if tsl >= 2:
            heprev = (("mo_w3", ei, tsl - 2, 3, 1),)
        elif ei >= 1:
            heprev = (("mo_w3", ei - 1, tsl + 6, 3, 1),)
        else:
            heprev = ()
        for fc in range(16):
            gu = (ei * 8 + tsl) * 16 + fc
            fp = fc % 2

            def mm1(e, b=b, ts=ts, fc=fc, fp=fp):
                for dc in range(8):
                    ins = e.matmul(pg[fp][:],
                                   w1_sb[:, dc * 2048 + fc * 128:dc * 2048 + fc * 128 + 128],
                                   hfs[:, dc * T + ts * 512:dc * T + ts * 512 + 512],
                                   start=(dc == 0), stop=(dc == 7))
                return ins
            plan.ev("tensor", mm1, wait=w1rdy + (("mo_hf", ei, b, 7),) +
                    ((("mo_sg", gu - 2),) if gu >= 2 else ()), tag=("mo_g1", gu))
            if glu:
                def mm2(e, b=b, ts=ts, fc=fc, fp=fp):
                    for dc in range(8):
                        ins = e.matmul(pv[fp][:],
                                       w2_sb[:, dc * 2048 + fc * 128:dc * 2048 + fc * 128 + 128],
                                       hfs[:, dc * T + ts * 512:dc * T + ts * 512 + 512],
                                       start=(dc == 0), stop=(dc == 7))
                    return ins
                plan.ev("tensor", mm2, wait=((("mo_he", gu - 2),) if gu >= 2 else ()),
                        tag=("mo_gv", ei, tsl, fc))
                plan.ev("scalar", lambda e, fp=fp, ei=ei, fc=fc: e.activation(
                    sg[fp][:], pg[fp][:], act,
                    bias=b1_sb[:, ei * 16 + fc:ei * 16 + fc + 1]),
                    wait=(("mo_g1", gu), ("mo_on",)) +
                         ((("mo_he", gu - 2),) if gu >= 2 else ()),
                    tag=("mo_sg", gu))
                plan.ev("vector", lambda e, tp=tp, fp=fp, fc=fc, ei=ei: e.scalar_tensor_tensor(
                    he[tp][:, fc * 512:(fc + 1) * 512], pv[fp][:],
                    b2_sb[:, ei * 16 + fc:ei * 16 + fc + 1], sg[fp][:],
                    ALU.add, ALU.mult),
                    wait=(("mo_gv", ei, tsl, fc), ("mo_sg", gu), ("mo_on",)) + heprev,
                    tag=("mo_he", gu))
            else:
                plan.ev("scalar", lambda e, tp=tp, fp=fp, fc=fc: e.activation(
                    he[tp][:, fc * 512:(fc + 1) * 512], pg[fp][:], act),
                    wait=(("mo_g1", gu),) + heprev, tag=("mo_sg", gu))

    def w3_section(ei, tsl, w3rdy):
        b, ts = divmod(tsl, 4)
        tp = tsl % 2
        hedone = [("mo_sg", (ei * 8 + tsl) * 16 + fc) for fc in range(16)] if not glu \
            else [("mo_he", (ei * 8 + tsl) * 16 + fc) for fc in range(16)]
        for tt4 in range(4):
            su = (ei * 8 + tsl) * 4 + tt4
            sp = su % 2
            k = b * 16 + ts * 4 + tt4
            for ds in range(2):
                def mm3(e, tp=tp, tt4=tt4, ds=ds, ei=ei):
                    withb3 = b1a is not None
                    for c in range(16):
                        ins = e.matmul(po[ds][:],
                                       he[tp][:, c * 512 + tt4 * 128:c * 512 + tt4 * 128 + 128],
                                       w3_sb[:, c * D + ds * 512:c * D + ds * 512 + 512],
                                       start=(c == 0), stop=(c == 15 and not withb3))
                    if withb3:
                        ins = e.matmul(po[ds][:], onesr[:],
                                       b3_sb[0:1, ds * 512:ds * 512 + 512],
                                       start=False, stop=True)
                    return ins
                b3w = (("mo_lb3", ei),) if b1a is not None else ()
                plan.ev("tensor", mm3, wait=tuple(hedone) + w3rdy + pre2 + b3w +
                        ((("mo_cb", su - 1, ds),) if su >= 1 else ()),
                        tag=("mo_w3", ei, tsl, tt4, ds))
                plan.ev("vector", lambda e, sp=sp, ds=ds, k=k, ei=ei, E=E: e.tensor_scalar_mul(
                    st4[sp][:, ds * 512:(ds + 1) * 512], po[ds][:],
                    comb[:, E * k + ei:E * k + ei + 1]),
                    wait=(("mo_w3", ei, tsl, tt4, ds), ("mo_cmb", k)) +
                         ((("mo_acc", su - 2),) if su >= 2 else ()),
                    tag=("mo_cb", su, ds))
            accw = acc_prev0 if ei == 0 else (("mo_acc", (ei - 1) * 32 + 30),
                                              ("mo_acc", (ei - 1) * 32 + 31))
            plan.ev("gpsimd", lambda e, sp=sp, b=b, ts=ts, tt4=tt4: e.dma_start(
                x_out[T * b + 128 * (ts * 4 + tt4):T * b + 128 * (ts * 4 + tt4 + 1), :],
                st4[sp][:], accum_op=ALU.add),
                wait=(("mo_cb", su, 0), ("mo_cb", su, 1)) + accw,
                tag=("mo_acc", su), sem=f"s_mac{su % 2}")

    for ei in range(E):
        w1rdy = (("mo_lw1", ei, 7),) + ((("mo_lw2", ei, 7),) if glu else ())
        w3rdy = (("mo_lw3", ei, 15),)
        for tsl in range(8):
            if tsl == 4:
                rw = (("mo_g1", (ei * 8 + 3) * 16 + 15),)
                if glu:
                    rw = rw + (("mo_gv", ei, 3, 15),)
                if ei == 0:
                    rw = rw + (("mo_gmm", 15),)
                reload_hf(ei, 1, rw)
                if gating_b1_pending[0]:
                    gating_b1_pending[0] = False
                    for k in range(16, NTILE):
                        emit_gating(k)
            gv_section(ei, tsl, w1rdy)
            if tsl == 7 and ei + 1 < E:
                emit_w12(ei + 1)
            if tsl >= 1:
                w3_section(ei, tsl - 1, w3rdy)
        w3_section(ei, 7, w3rdy)
        if ei + 1 < E:
            rw = (("mo_g1", (ei * 8 + 7) * 16 + 15),)
            if glu:
                rw = rw + (("mo_gv", ei, 7, 15),)
            if ei == 0:
                rw = rw + (("mo_gmm", 31),)
            reload_hf(ei + 1, 0, rw)
            emit_w3(ei + 1)
    return [("mo_acc", (E - 1) * 32 + 31)]


def emit_l1_qkv(nc, al, plan, h_tm, wq, wk, c1, s1, r1, half2, half2t, cc, qn_f, kn_f,
                hfp=None, mate=None):
    sb = al.sb
    ps = al.ps
    hf, hf_rdy = hfp if hfp is not None else load_hfm(nc, plan, sb, h_tm, name="l1_hf")
    wq_sb, wq_rdy = load_w_bf16(nc, plan, sb, wq, 8, 1024, "l1_wq")
    wk_sb, wk_rdy = load_w_bf16(nc, plan, sb, wk, 8, 1024, "l1_wk")
    r1_sb = sb("l1_r1", [128, 128], BF16)
    h2_sb = sb("l1_h2", [128, 2], BF16)
    h2t_sb = sb("l1_h2t", [2, 128], BF16)
    c1_sb = sb("l1_c1", [128, T], F32)
    s1_sb = sb("l1_s1", [128, T], F32)
    ccs = sb("l1_cc", [128, 4], F32)
    plan.ev("sync", lambda e: e.dma_start(r1_sb[:], r1), tag=("l1_r1",), sem="s_pre")
    plan.ev("sync", lambda e: e.dma_start(c1_sb[:], c1), tag=("l1_c1",), sem="s_pre")
    plan.ev("sync", lambda e: e.dma_start(s1_sb[:], s1), tag=("l1_s1",), sem="s_pre")
    plan.ev("sync", lambda e: e.dma_start(ccs[:], cc), tag=("l1_ccs",), sem="s_pre")
    plan.ev("gpsimd", lambda e: e.dma_start(h2_sb[:], half2), tag=("l1_h2",), sem="s_pre2")
    plan.ev("gpsimd", lambda e: e.dma_start(h2t_sb[:], half2t), tag=("l1_h2t",), sem="s_pre2")
    tA = [sb(f"l1_tA{i}", [128, 512], F32) for i in range(2)]
    tB = [sb(f"l1_tB{i}", [128, 512], F32) for i in range(2)]
    qraw = [sb(f"l1_qraw{i}", [128, 512], BF16) for i in range(2)]
    qr = [sb(f"l1_qr{i}", [128, 512], F32) for i in range(2)]
    sqq = [sb(f"l1_sqq{i}", [128, 512], BF16) for i in range(2)]
    sd2 = [sb(f"l1_sd2{i}", [2, 512], F32) for i in range(2)]
    rsf = [sb(f"l1_rsf{i}", [2, 512], F32) for i in range(2)]
    rs2 = [sb(f"l1_rs2{i}", [2, 512], BF16) for i in range(2)]
    qn = [sb(f"l1_qn{i}", [128, 512], BF16) for i in range(2)]
    ps_q = [ps(f"l1_psq{i}", [128, 512], F32) for i in range(2)]
    ps_r = [ps(f"l1_psr{i}", [128, 512], F32) for i in range(2)]
    ps_sbc = [ps(f"l1_pssbc{i}", [128, 512], F32) for i in range(2)]
    ps_ss = [t[0:2, :] for t in ps_sbc]
    ps_bc = ps_sbc

    units = []
    for pi, (w_sb, w_rdy, outt) in enumerate(((wq_sb, wq_rdy, qn_f), (wk_sb, wk_rdy, kn_f))):
        for b in range(BPC):
            for oc in range(8):
                for ts in range(4):
                    units.append((pi, w_sb, w_rdy, outt, b, oc, ts))
    NU = len(units)

    def part_a(u):
        (pi, w_sb, w_rdy, outt, b, oc, ts) = units[u]
        p = u % 2
        wprev = lambda t: ((t, u - 2),) if u >= 2 else ()

        def mmq(e, p=p, w_sb=w_sb, b=b, oc=oc, ts=ts):
            for dc in range(8):
                ins = e.matmul(ps_q[p][:], w_sb[:, dc * 1024 + oc * 128:dc * 1024 + oc * 128 + 128],
                               hf[b][:, dc * T + ts * 512:dc * T + ts * 512 + 512],
                               start=(dc == 0), stop=(dc == 7))
            return ins
        plan.ev("tensor", mmq, wait=(w_rdy[-1], hf_rdy[8 * b + 7]) + wprev("l1_tA")
                + wprev("l1_qcp"), tag=("l1_mmq", u))
        plan.ev("scalar", lambda e, p=p: e.copy(qraw[p][:], ps_q[p][:]),
                wait=(("l1_mmq", u),) + wprev("l1_mmr"), tag=("l1_qcp", u))

    def part_b(u):
        p = u % 2
        wprev = lambda t: ((t, u - 2),) if u >= 2 else ()
        plan.ev("tensor", lambda e, p=p: e.matmul(ps_r[p][:], r1_sb[:], qraw[p][:],
                                                  start=True, stop=True),
                wait=(("l1_qcp", u), ("l1_ccs",)) + wprev("l1_tB"), tag=("l1_mmr", u))

    def part_c(u):
        (pi, w_sb, w_rdy, outt, b, oc, ts) = units[u]
        p = u % 2
        wprev = lambda t: ((t, u - 2),) if u >= 2 else ()
        plan.ev("vector", lambda e, p=p, ts=ts: e.tensor_mul(
            tA[p][:], ps_q[p][:], c1_sb[:, 512 * ts:512 * (ts + 1)]),
            wait=(("l1_mmq", u), ("l1_qcp", u), ("l1_ccs",)) + wprev("l1_qr"),
            tag=("l1_tA", u))
        plan.ev("vector", lambda e, p=p, ts=ts: e.tensor_mul(
            tB[p][:], ps_r[p][:], s1_sb[:, 512 * ts:512 * (ts + 1)]),
            wait=(("l1_mmr", u), ("l1_ccs",)) + wprev("l1_qr"), tag=("l1_tB", u))
        plan.ev("vector", lambda e, p=p: e.tensor_add(qr[p][:], tA[p][:], tB[p][:]),
                wait=(("l1_tA", u), ("l1_tB", u)) + wprev("l1_qn"), tag=("l1_qr", u))
        plan.ev("scalar", lambda e, p=p: e.activation(sqq[p][:], qr[p][:], AF.Square),
                wait=(("l1_qr", u),) + wprev("l1_mss"), tag=("l1_sq", u))

    def part_d(u):
        p = u % 2
        wprev = lambda t: ((t, u - 2),) if u >= 2 else ()
        plan.ev("tensor", lambda e, p=p: e.matmul(ps_ss[p][:], h2_sb[:], sqq[p][:],
                                                  start=True, stop=True),
                wait=(("l1_sq", u), ("l1_h2t",)) + wprev("l1_qn"), tag=("l1_mss", u))
        plan.ev("scalar", lambda e, p=p: e.activation(sd2[p][:], ps_ss[p][:], AF.Ln,
                                                      bias=ccs[0:2, 1:2]),
                wait=(("l1_mss", u), ("l1_ccs",)) + wprev("l1_rcc"), tag=("l1_sd", u))
        plan.ev("scalar", lambda e, p=p: e.activation(rs2[p][:], sd2[p][:], AF.Exp,
                                                      scale=-0.5),
                wait=(("l1_sd", u),) + wprev("l1_mbc"), tag=("l1_rcc", u))

    def part_e(u):
        (pi, w_sb, w_rdy, outt, b, oc, ts) = units[u]
        p = u % 2
        wprev = lambda t: ((t, u - 2),) if u >= 2 else ()
        plan.ev("tensor", lambda e, p=p: e.matmul(ps_bc[p][:], h2t_sb[:], rs2[p][:],
                                                  start=True, stop=True),
                wait=(("l1_rcc", u), ("l1_h2t",)) + wprev("l1_qn"), tag=("l1_mbc", u))
        plan.ev("vector", lambda e, p=p: e.tensor_mul(qn[p][:], qr[p][:], ps_bc[p][:]),
                wait=(("l1_qr", u), ("l1_mbc", u)) + wprev("l1_st"), tag=("l1_qn", u))
        plan.ev("sync", lambda e, p=p, outt=outt, b=b, oc=oc, ts=ts: e.dma_start(
            outt[oc, :, T * b + 512 * ts:T * b + 512 * (ts + 1)], qn[p][:]),
            wait=(("l1_qn", u),), tag=("l1_st", u), sem=f"s_o{p}")

    for u in range(NU + 3):
        if u < NU:
            part_a(u)
        if 1 <= u <= NU:
            part_b(u - 1)
        if 3 <= u <= NU + 2:
            part_e(u - 3)
        if 2 <= u <= NU + 1:
            part_d(u - 2)
        if 1 <= u <= NU:
            part_c(u - 1)
        if mate is not None:
            mate(u)
    return [("l1_st", u) for u in range(len(units))]


def emit_l1_attn(nc, al, plan, qn_f, kn_f, v1_tm, wo, bo, onesrow, dsel_a, dsel_b,
                 x_in, x_out):
    sb = al.sb
    ps = al.ps
    woA = sb("at_woA", [64, 8 * 1024], BF16)
    woB = sb("at_woB", [64, 8 * 1024], BF16)
    for c in range(8):
        plan.ev("gpsimd", lambda e, c=c: e.dma_start(
            woA[:, c * 1024:(c + 1) * 1024], wo[128 * c:128 * c + 64, :]),
            tag=("at_woA", c), sem="s_pre")
        plan.ev("gpsimd", lambda e, c=c: e.dma_start(
            woB[:, c * 1024:(c + 1) * 1024], wo[128 * c + 64:128 * (c + 1), :]),
            tag=("at_woB", c), sem="s_pre")
    wo_rdy = (("at_woA", 7), ("at_woB", 7))
    bo_sb = sb("at_bo", [1, D], BF16)
    onesr = sb("at_ones", [1, 128], BF16)
    dsa = sb("at_dsa", [128, 128], BF16)
    for nm, dst, src in (("at_bo", bo_sb, bo), ("at_on", onesr, onesrow),
                         ("at_da", dsa, dsel_a)):
        plan.ev("gpsimd", lambda e, dst=dst, src=src: e.dma_start(dst[:], src),
                tag=(nm,), sem="s_pre2")
    pre2 = (("at_da",),)
    KnP = [[sb(f"at_Kn{i}_{hf}", [64, T], BF16) for hf in range(2)] for i in range(2)]
    QnP = [[sb(f"at_Qn{i}_{hf}", [64, T], BF16) for hf in range(2)] for i in range(2)]
    V1P = [sb(f"at_V1{i}", [128, 16 * 130], BF16) for i in range(2)]
    onesf = sb("at_onesf", [1, 128], F32)
    plan.ev("gpsimd", lambda e: e.dma_start(onesf[:], onesrow), tag=("at_onf",), sem="s_pre2")
    for i in range(2):
        plan.ev("gpsimd", lambda e, i=i: e.memset(V1P[i][:], 1.0), tag=("at_vms", i))
    PA = [sb(f"at_PA{i}", [128, 8 * 512], BF16) for i in range(2)]
    PB = [sb(f"at_PB{i}", [128, 8 * 512], BF16) for i in range(2)]
    rdA = sb("at_rdA", [1, 512], F32)
    rdB = sb("at_rdB", [1, 512], F32)
    rbA = sb("at_rbA", [64, 512], F32)
    rbB = sb("at_rbB", [64, 512], F32)
    pvsA = [sb(f"at_pvsA{i}", [128, 512], F32) for i in range(2)]
    pvsB = [sb(f"at_pvsB{i}", [128, 512], F32) for i in range(2)]
    attnA = sb("at_attnA", [64, 8 * T], BF16)
    attnB = sb("at_attnB", [64, 8 * T], BF16)
    xt = [sb(f"at_xt{i}", [128, D], F32) for i in range(2)]
    psA = [ps(f"at_psA{i}", [128, 512], F32) for i in range(2)]
    psB = [ps(f"at_psB{i}", [128, 512], F32) for i in range(2)]
    ps_pvA = ps("at_pspvA", [128, 512], F32)
    ps_pvB = ps("at_pspvB", [128, 512], F32)
    ps_bcA = ps("at_psbcA", [128, 512], F32)
    ps_bcB = ps("at_psbcB", [128, 512], F32)

    done_t1 = set()
    done_t2 = set()

    def emit_tail1(iu):
        if iu in done_t1 or iu < 0:
            return
        done_t1.add(iu)
        bq, rem = divmod(iu, 32)
        prq, tsq = divmod(rem, 4)
        ppq = prq % 2
        ipq = iu % 2
        for hh in range(2):
            pdone = tuple(("at_eA", iu, 8 * hh + sl) for sl in range(8)) + \
                    tuple(("at_eB", iu, 8 * hh + sl) for sl in range(8))

            def mmpv(e, pp=ppq, hh=hh):
                for sl in range(8):
                    sc = 8 * hh + sl
                    e.matmul(ps_pvA[0:65, :],
                             V1P[pp][:, 130 * sc:130 * sc + 65],
                             PA[hh][:, 512 * sl:512 * (sl + 1)],
                             start=(sc == 0), stop=(sc == 15))
                    ins = e.matmul(ps_pvB[0:65, :],
                                   V1P[pp][:, 130 * sc + 65:130 * (sc + 1)],
                                   PB[hh][:, 512 * sl:512 * (sl + 1)],
                                   start=(sc == 0), stop=(sc == 15))
                return ins
            plan.ev("tensor", mmpv, wait=pdone +
                    ((("at_cpA", iu - 1), ("at_cpB", iu - 1)) if (iu and hh == 0) else ()),
                    tag=("at_pv", iu, hh))
        plan.ev("vector", lambda e, ip=ipq: e.tensor_copy(pvsA[ip][0:65, :], ps_pvA[0:65, :]),
                wait=(("at_pv", iu, 1),) + ((("at_oA", iu - 2),) if iu >= 2 else ()),
                tag=("at_cpA", iu))
        plan.ev("vector", lambda e, ip=ipq: e.tensor_copy(pvsB[ip][0:65, :], ps_pvB[0:65, :]),
                wait=(("at_pv", iu, 1),) + ((("at_oB", iu - 2),) if iu >= 2 else ()),
                tag=("at_cpB", iu))

    def emit_tail2(iu):
        if iu in done_t2 or iu < 0:
            return
        done_t2.add(iu)
        bq, rem = divmod(iu, 32)
        prq, tsq = divmod(rem, 4)
        ipq = iu % 2
        plan.ev("vector", lambda e, ip=ipq: e.reciprocal(rdA[:], pvsA[ip][64:65, :]),
                wait=(("at_cpA", iu),) + ((("at_mbc", iu - 1, 0),) if iu else ()),
                tag=("at_rdA", iu))
        plan.ev("vector", lambda e, ip=ipq: e.reciprocal(rdB[:], pvsB[ip][64:65, :]),
                wait=(("at_cpB", iu),) + ((("at_mbc", iu - 1, 1),) if iu else ()),
                tag=("at_rdB", iu))
        plan.ev("tensor", lambda e: e.matmul(ps_bcA[0:64, :], onesf[0:1, 0:64], rdA[:],
                                             start=True, stop=True),
                wait=(("at_rdA", iu), ("at_onf",)) + ((("at_cbc", iu - 1, 0),) if iu else ()),
                tag=("at_mbc", iu, 0))
        plan.ev("tensor", lambda e: e.matmul(ps_bcB[0:64, :], onesf[0:1, 0:64], rdB[:],
                                             start=True, stop=True),
                wait=(("at_rdB", iu), ("at_onf",)) + ((("at_cbc", iu - 1, 1),) if iu else ()),
                tag=("at_mbc", iu, 1))
        plan.ev("vector", lambda e: e.tensor_copy(rbA[:], ps_bcA[0:64, :]),
                wait=(("at_mbc", iu, 0),) + ((("at_oA", iu - 1),) if iu else ()),
                tag=("at_cbc", iu, 0))
        plan.ev("vector", lambda e: e.tensor_copy(rbB[:], ps_bcB[0:64, :]),
                wait=(("at_mbc", iu, 1),) + ((("at_oB", iu - 1),) if iu else ()),
                tag=("at_cbc", iu, 1))
        bwo = (("at_mo", bq * 16 - 1, 1),) if (bq and rem == 0) else ()
        plan.ev("vector", lambda e, pr=prq, ts=tsq, ip=ipq: e.tensor_mul(
            attnA[:, pr * T + 512 * ts:pr * T + 512 * (ts + 1)],
            pvsA[ip][0:64, :], rbA[:]),
            wait=(("at_cbc", iu, 0), ("at_cpA", iu)) + bwo, tag=("at_oA", iu))
        plan.ev("vector", lambda e, pr=prq, ts=tsq, ip=ipq: e.tensor_mul(
            attnB[:, pr * T + 512 * ts:pr * T + 512 * (ts + 1)],
            pvsB[ip][0:64, :], rbB[:]),
            wait=(("at_cbc", iu, 1), ("at_cpB", iu)) + bwo, tag=("at_oB", iu))

    for b in range(BPC):
        bw = (("at_mo", b * 16 - 1, 1),) if b else ()
        for pr in range(8):
            pp = pr % 2
            if pr >= 2 or b:
                prev = (b * 8 + pr - 2) * 4 + 3 if pr >= 2 else ((b - 1) * 8 + pr + 6) * 4 + 3
                bwp = (("at_sA", prev, 15), ("at_sB", prev, 15))
            else:
                bwp = ()
            bvp = (("at_pv", ((b * 8 + pr - 2) * 4 + 3) if pr >= 2 else
                    (((b - 1) * 8 + pr + 6) * 4 + 3), 1),) if (pr >= 2 or b) else ()
            for hf2 in range(2):
                plan.ev("sync", lambda e, b=b, pr=pr, pp=pp, hf2=hf2: e.dma_start(
                    V1P[pp][:].rearrange("p (n c) -> p n c", c=130)[:, :, 65 * hf2:65 * hf2 + 64],
                    v1_tm[T * b:T * (b + 1), 128 * pr + 64 * hf2:128 * pr + 64 * (hf2 + 1)]
                    .rearrange("(n p) q -> p n q", p=128)),
                    wait=bvp + bw + (("at_vms", pp),), tag=("at_lv", b, pr, hf2),
                    sem=f"s_v1{pp}{hf2}")
            for hf2 in range(2):
                plan.ev("sync", lambda e, b=b, pr=pr, pp=pp, hf2=hf2: e.dma_start(
                    KnP[pp][hf2][:],
                    kn_f[pr, 64 * hf2:64 * (hf2 + 1), T * b:T * (b + 1)]),
                    wait=bwp + bw, tag=("at_lk", b, pr, hf2), sem=f"s_kn{pp}{hf2}")
                plan.ev("sync", lambda e, b=b, pr=pr, pp=pp, hf2=hf2: e.dma_start(
                    QnP[pp][hf2][:],
                    qn_f[pr, 64 * hf2:64 * (hf2 + 1), T * b:T * (b + 1)]),
                    wait=bwp + bw, tag=("at_lq", b, pr, hf2), sem=f"s_qn{pp}{hf2}")
            lds = (("at_lk", b, pr, 0), ("at_lq", b, pr, 0),
                   ("at_lk", b, pr, 1), ("at_lq", b, pr, 1),
                   ("at_lv", b, pr, 0), ("at_lv", b, pr, 1))
            for ts in range(4):
                iu = (b * 8 + pr) * 4 + ts

                def emit_sc(sc, iu=iu, pp=pp, ts=ts):
                    gsc = iu * 16 + sc
                    p = gsc % 2
                    wA = (("at_eA", (gsc - 2) // 16, (gsc - 2) % 16),) if gsc >= 2 else ()
                    wB = (("at_eB", (gsc - 2) // 16, (gsc - 2) % 16),) if gsc >= 2 else ()
                    plan.ev("tensor", lambda e, p=p, pp=pp, ts=ts, sc=sc: e.matmul(
                        psA[p][:], KnP[pp][0][:, 128 * sc:128 * (sc + 1)],
                        QnP[pp][0][:, 512 * ts:512 * (ts + 1)],
                        start=True, stop=True),
                        wait=lds + wA, tag=("at_sA", iu, sc))
                    plan.ev("tensor", lambda e, p=p, pp=pp, ts=ts, sc=sc: e.matmul(
                        psB[p][:], KnP[pp][1][:, 128 * sc:128 * (sc + 1)],
                        QnP[pp][1][:, 512 * ts:512 * (ts + 1)],
                        start=True, stop=True),
                        wait=lds + wB, tag=("at_sB", iu, sc))

                def emit_exp(sc, iu=iu):
                    p = (iu * 16 + sc) % 2
                    plan.ev("scalar", lambda e, p=p, hh=sc // 8, sl=sc % 8: e.activation(
                        PA[hh][:, 512 * sl:512 * (sl + 1)], psA[p][:], AF.Exp, scale=0.125),
                        wait=(("at_sA", iu, sc),) +
                             ((("at_pv", iu - 1, sc // 8),) if iu >= 1 else ()),
                        tag=("at_eA", iu, sc))
                    plan.ev("scalar", lambda e, p=p, hh=sc // 8, sl=sc % 8: e.activation(
                        PB[hh][:, 512 * sl:512 * (sl + 1)], psB[p][:], AF.Exp, scale=0.125),
                        wait=(("at_sB", iu, sc),) +
                             ((("at_pv", iu - 1, sc // 8),) if iu >= 1 else ()),
                        tag=("at_eB", iu, sc))

                emit_sc(0)
                emit_sc(1)
                emit_tail1(iu - 1)
                emit_tail2(iu - 2)
                for sc in range(16):
                    emit_exp(sc)
                    if sc + 2 < 16:
                        emit_sc(sc + 2)

        emit_tail1((b * 8 + 7) * 4 + 3)
        emit_tail2((b * 8 + 7) * 4 + 2)
        emit_tail2((b * 8 + 7) * 4 + 3)
        adone = tuple(("at_oA", (b * 8 + pr) * 4 + ts) for pr in range(8) for ts in range(4)) +                 tuple(("at_oB", (b * 8 + pr) * 4 + ts) for pr in range(8) for ts in range(4))
        for tt in range(16):
            u = b * 16 + tt
            p = u % 2
            plan.ev("sync", lambda e, b=b, tt=tt, p=p: e.dma_start(
                xt[p][:], x_in[T * b + 128 * tt:T * b + 128 * (tt + 1), :]),
                wait=(("at_add", u - 2, 1),) if u >= 2 else (),
                tag=("at_xt", u), sem=f"s_x{p}")
            for ds in range(2):
                def mmo(e, b=b, tt=tt, ds=ds):
                    for c in range(8):
                        e.matmul(psA[ds][:], attnA[:, c * T + tt * 128:c * T + tt * 128 + 128],
                                 woA[:, c * 1024 + ds * 512:c * 1024 + ds * 512 + 512],
                                 start=(c == 0), stop=False)
                        e.matmul(psA[ds][:], attnB[:, c * T + tt * 128:c * T + tt * 128 + 128],
                                 woB[:, c * 1024 + ds * 512:c * 1024 + ds * 512 + 512],
                                 start=False, stop=False)
                    return e.matmul(psA[ds][:], onesr[:],
                                    bo_sb[0:1, ds * 512:ds * 512 + 512],
                                    start=False, stop=True)
                plan.ev("tensor", mmo, wait=adone + wo_rdy + (("at_da",),) +
                        ((("at_add", u - 1, ds),) if u >= 1 else ()),
                        tag=("at_mo", u, ds))
                plan.ev("vector", lambda e, ds=ds, p=p: e.tensor_add(
                    xt[p][:, 512 * ds:512 * (ds + 1)], psA[ds][:],
                    xt[p][:, 512 * ds:512 * (ds + 1)]),
                    wait=(("at_mo", u, ds), ("at_xt", u)) +
                         ((("at_st", u - 2),) if u >= 2 else ()),
                    tag=("at_add", u, ds))
            plan.ev("sync", lambda e, b=b, tt=tt, p=p: e.dma_start(
                x_out[T * b + 128 * tt:T * b + 128 * (tt + 1), :], xt[p][:]),
                wait=(("at_add", u, 0), ("at_add", u, 1)),
                tag=("at_st", u), sem=f"s_xo{p}")
    return [("at_st", u) for u in range(BPC * 16)]


def build_ln(copy_x):
    nc = bass.Bass("TRN2", target_bir_lowering=False)
    x = nc.dram_tensor("x", [TC, D], F32, kind="ExternalInput").ap()
    cc = nc.dram_tensor("cc", [128, 4], F32, kind="ExternalInput").ap()
    h = nc.dram_tensor("h", [TC, D], BF16, kind="ExternalOutput").ap()
    xout = nc.dram_tensor("xout", [TC, D], F32, kind="ExternalOutput").ap() if copy_x else None
    es = contextlib.ExitStack()
    al = Alloc(nc, es)
    plan = Plan()
    out = emit_ln(nc, al, plan, x, h, cc, xout)
    plan.barrier("sync", wait=tuple(out))
    plan.emit(nc)
    return nc


def build_l0_qkv():
    nc = bass.Bass("TRN2", target_bir_lowering=False)
    ap = lambda n, shp, dt, k: nc.dram_tensor(n, shp, dt, kind=k).ap()
    h = ap("h", [TC, D], BF16, "ExternalInput")
    wq = ap("wq", [D, D], F32, "ExternalInput")
    wk = ap("wk", [D, D], F32, "ExternalInput")
    cq = ap("cq", [D, T], F32, "ExternalInput")
    sq = ap("sq", [D, T], F32, "ExternalInput")
    ck = ap("ck", [D, T], F32, "ExternalInput")
    sk = ap("sk", [D, T], F32, "ExternalInput")
    r0 = ap("r0", [128, 128], BF16, "ExternalInput")
    qf = ap("qf", [8, 128, TC], BF16, "ExternalOutput")
    kf = ap("kf", [8, 128, TC], BF16, "ExternalOutput")
    es = contextlib.ExitStack()
    al = Alloc(nc, es)
    plan = Plan()
    out = emit_l0_qkv(nc, al, plan, h, wq, wk, cq, sq, ck, sk, r0, qf, kf)
    plan.barrier("sync", wait=tuple(out))
    plan.emit(nc)
    return nc


def build_l0_vg():
    nc = bass.Bass("TRN2", target_bir_lowering=False)
    ap = lambda n, shp, dt, k: nc.dram_tensor(n, shp, dt, kind=k).ap()
    h = ap("h", [TC, D], BF16, "ExternalInput")
    wv = ap("wv", [D, 2048], F32, "ExternalInput")
    wg = ap("wg", [D, 2048], F32, "ExternalInput")
    v = ap("v", [TC, 2048], BF16, "ExternalOutput")
    gate = ap("gate", [TC, 2048], BF16, "ExternalOutput")
    es = contextlib.ExitStack()
    al = Alloc(nc, es)
    plan = Plan()
    vg_unit, nvg, out = emit_l0_vg(nc, al, plan, h, [(wv, 2048, v, None), (wg, 2048, gate, AF.Silu)])
    for u in range(nvg):
        vg_unit(u)
    plan.barrier("sync", wait=tuple(out))
    plan.emit(nc)
    return nc


def build_l0_ret():
    nc = bass.Bass("TRN2", target_bir_lowering=False)
    ap = lambda n, shp, dt, k: nc.dram_tensor(n, shp, dt, kind=k).ap()
    qf = ap("qf", [8, 128, TC], BF16, "ExternalInput")
    kf = ap("kf", [8, 128, TC], BF16, "ExternalInput")
    v = ap("v", [TC, 2048], BF16, "ExternalInput")
    gate = ap("gate", [TC, 2048], BF16, "ExternalInput")
    dcol = ap("dcol", [128, 8], F32, "ExternalInput")
    eh = ap("eh", [128, 128], F32, "ExternalInput")
    rowi = ap("rowi", [128, 128], F32, "ExternalInput")
    colj = ap("colj", [128, 1], F32, "ExternalInput")
    cc = ap("cc", [128, 4], F32, "ExternalInput")
    gg = ap("gg", [TC, 2048], BF16, "ExternalOutput")
    es = contextlib.ExitStack()
    al = Alloc(nc, es)
    plan = Plan()
    out = emit_l0_ret(nc, al, plan, qf, kf, v, gate, dcol, eh, rowi, colj, cc, gg)
    plan.barrier("sync", wait=tuple(out))
    plan.emit(nc)
    return nc


def build_l0_proj():
    nc = bass.Bass("TRN2", target_bir_lowering=False)
    ap = lambda n, shp, dt, k: nc.dram_tensor(n, shp, dt, kind=k).ap()
    gg = ap("gg", [TC, 2048], BF16, "ExternalInput")
    w = ap("w", [2048, D], F32, "ExternalInput")
    xin = ap("xin", [TC, D], F32, "ExternalInput")
    xout = ap("xout", [TC, D], F32, "ExternalOutput")
    es = contextlib.ExitStack()
    al = Alloc(nc, es)
    plan = Plan()
    out = emit_proj_res(nc, al, plan, gg, w, xin, xout)
    plan.barrier("sync", wait=tuple(out))
    plan.emit(nc)
    return nc


def build_moe(layer):
    nc = bass.Bass("TRN2", target_bir_lowering=False)
    ap = lambda n, shp, dt, k: nc.dram_tensor(n, shp, dt, kind=k).ap()
    h = ap("h", [TC, D], BF16, "ExternalInput")
    wg = ap("wgate", [D, 8], F32, "ExternalInput")
    E = 8 if layer == 0 else 9
    w1l = [ap(f"w1_{e}", [D, 2048], F32, "ExternalInput") for e in range(E)]
    w3l = [ap(f"w3_{e}", [2048, D], F32, "ExternalInput") for e in range(E)]
    xin = ap("xin", [TC, D], F32, "ExternalInput")
    xout = ap("xout", [TC, D], F32, "ExternalOutput")
    es = contextlib.ExitStack()
    al = Alloc(nc, es)
    plan = Plan()
    if layer == 0:
        w2l = [ap(f"w2_{e}", [D, 2048], F32, "ExternalInput") for e in range(E)]
        b1a = ap("b1a", [128, E * 16], F32, "ExternalInput")
        b2a = ap("b2a", [128, E * 16], F32, "ExternalInput")
        b3a = ap("b3a", [E, D], F32, "ExternalInput")
        onesrow = ap("onesrow", [1, 128], F32, "ExternalInput")
        out = emit_moe(nc, al, plan, h, wg, w1l, w2l, w3l, b1a, b2a, b3a, onesrow,
                       AF.Silu, xin, xout, use_shared=False)
    else:
        out = emit_moe(nc, al, plan, h, wg, w1l, None, w3l, None, None, None, None,
                       AF.Gelu, xin, xout, use_shared=True)
    plan.barrier("gpsimd", wait=tuple(out))
    plan.emit(nc)
    return nc


def build_l1_qkv():
    nc = bass.Bass("TRN2", target_bir_lowering=False)
    ap = lambda n, shp, dt, k: nc.dram_tensor(n, shp, dt, kind=k).ap()
    h = ap("h", [TC, D], BF16, "ExternalInput")
    wq = ap("wq", [D, D], F32, "ExternalInput")
    wk = ap("wk", [D, D], F32, "ExternalInput")
    c1 = ap("c1", [128, T], F32, "ExternalInput")
    s1 = ap("s1", [128, T], F32, "ExternalInput")
    r1 = ap("r1", [128, 128], BF16, "ExternalInput")
    h2 = ap("h2", [128, 2], F32, "ExternalInput")
    h2t = ap("h2t", [2, 128], F32, "ExternalInput")
    cc = ap("cc", [128, 4], F32, "ExternalInput")
    qf = ap("qf", [8, 128, TC], BF16, "ExternalOutput")
    kf = ap("kf", [8, 128, TC], BF16, "ExternalOutput")
    es = contextlib.ExitStack()
    al = Alloc(nc, es)
    plan = Plan()
    out = emit_l1_qkv(nc, al, plan, h, wq, wk, c1, s1, r1, h2, h2t, cc, qf, kf)
    plan.barrier("sync", wait=tuple(out))
    plan.emit(nc)
    return nc


def build_l1_v():
    nc = bass.Bass("TRN2", target_bir_lowering=False)
    ap = lambda n, shp, dt, k: nc.dram_tensor(n, shp, dt, kind=k).ap()
    h = ap("h", [TC, D], BF16, "ExternalInput")
    wv = ap("wv", [D, D], F32, "ExternalInput")
    v = ap("v", [TC, D], BF16, "ExternalOutput")
    es = contextlib.ExitStack()
    al = Alloc(nc, es)
    plan = Plan()
    vg_unit, nvg, out = emit_l0_vg(nc, al, plan, h, [(wv, 1024, v, None)])
    for u in range(nvg):
        vg_unit(u)
    plan.barrier("sync", wait=tuple(out))
    plan.emit(nc)
    return nc


def build_l1_attn():
    nc = bass.Bass("TRN2", target_bir_lowering=False)
    ap = lambda n, shp, dt, k: nc.dram_tensor(n, shp, dt, kind=k).ap()
    qf = ap("qf", [8, 128, TC], BF16, "ExternalInput")
    kf = ap("kf", [8, 128, TC], BF16, "ExternalInput")
    v = ap("v", [TC, D], BF16, "ExternalInput")
    wo = ap("wo", [D, D], F32, "ExternalInput")
    bo = ap("bo", [1, D], F32, "ExternalInput")
    onesrow = ap("onesrow", [1, 128], F32, "ExternalInput")
    dsa = ap("dsa", [128, 128], F32, "ExternalInput")
    dsb = ap("dsb", [128, 128], F32, "ExternalInput")
    xin = ap("xin", [TC, D], F32, "ExternalInput")
    xout = ap("xout", [TC, D], F32, "ExternalOutput")
    es = contextlib.ExitStack()
    al = Alloc(nc, es)
    plan = Plan()
    out = emit_l1_attn(nc, al, plan, qf, kf, v, wo, bo, onesrow, dsa, dsb, xin, xout)
    plan.barrier("sync", wait=tuple(out))
    plan.emit(nc)
    return nc


def build_fused():
    nc = bass.Bass("TRN2", target_bir_lowering=False)
    ap = lambda n, shp, dt, k: nc.dram_tensor(n, shp, dt, kind=k).ap()
    I = lambda n, shp, dt: ap(n, shp, dt, "ExternalInput")
    N = lambda n, shp, dt: ap(n, shp, dt, "Internal")
    x_in = I("x", [TC, D], F32)
    cc = I("cc", [128, 4], F32)
    cq, sq, ck, sk = (I(n, [D, T], F32) for n in ("cq", "sq", "ck", "sk"))
    r0 = I("r0", [128, 128], BF16)
    r1 = I("r1", [128, 128], BF16)
    c1 = I("c1", [128, T], F32)
    s1 = I("s1", [128, T], F32)
    h2 = I("h2", [128, 2], F32)
    h2t = I("h2t", [2, 128], F32)
    onesrow = I("onesrow", [1, 128], F32)
    dsa = I("dsa", [128, 128], F32)
    dsb = I("dsb", [128, 128], F32)
    eh = I("eh", [128, 128], F32)
    rowi = I("rowi", [128, 128], F32)
    colj = I("colj", [128, 1], F32)
    dcol = I("dcol", [128, 8], F32)
    l0_wq = I("l0_wq", [D, D], F32)
    l0_wk = I("l0_wk", [D, D], F32)
    l0_wv = I("l0_wv", [D, 2048], F32)
    l0_wg = I("l0_wg", [D, 2048], F32)
    l0_wp = I("l0_wp", [2048, D], F32)
    wg0 = I("wgate0", [D, 8], F32)
    m0w1 = [I(f"m0w1_{e}", [D, 2048], F32) for e in range(8)]
    m0w2 = [I(f"m0w2_{e}", [D, 2048], F32) for e in range(8)]
    m0w3 = [I(f"m0w3_{e}", [2048, D], F32) for e in range(8)]
    b1a = I("b1a", [128, 128], F32)
    b2a = I("b2a", [128, 128], F32)
    b3a = I("b3a", [8, D], F32)
    l1_wq = I("l1_wq", [D, D], F32)
    l1_wk = I("l1_wk", [D, D], F32)
    l1_wv = I("l1_wv", [D, D], F32)
    l1_wo = I("l1_wo", [D, D], F32)
    l1_bo = I("l1_bo", [1, D], F32)
    wg1 = I("wgate1", [D, 8], F32)
    m1w1 = [I(f"m1w1_{e}", [D, 2048], F32) for e in range(9)]
    m1w3 = [I(f"m1w3_{e}", [2048, D], F32) for e in range(9)]
    out = ap("out", [TC, D], F32, "ExternalOutput")
    h = N("h", [TC, D], BF16)
    qf = N("qf", [8, 128, TC], BF16)
    kf = N("kf", [8, 128, TC], BF16)
    v0 = N("v0", [TC, 2048], BF16)
    gate = N("gate", [TC, 2048], BF16)
    gg = N("gg", [TC, 2048], BF16)
    qf1 = N("qf1", [8, 128, TC], BF16)
    kf1 = N("kf1", [8, 128, TC], BF16)
    v1 = N("v1", [TC, D], BF16)

    sems = {}
    bases = {}
    entry = [{}]

    snum = [0]

    def stage(emitfn):
        snum[0] += 1
        with contextlib.ExitStack() as es:
            al = Alloc(nc, es)
            plan = Plan(gsfx=f"_{snum[0]}")
            emitfn(al, plan)
            plan.emit(nc, sems, bases, entry=dict(entry[0]))
        entry[0] = dict(bases)

    def g1(al, plan):
        emit_ln(nc, al, plan, x_in, h, cc, out)
        hfp = load_hfm(nc, plan, al.sb, h,
                       hw=lambda b: (("ln_st", 16 * b + 14), ("ln_st", 16 * b + 15)))
        vg_unit, nvg, _ = emit_l0_vg(nc, al, plan, h,
                                     [(l0_wv, 2048, v0, None),
                                      (l0_wg, 2048, gate, AF.Silu)], hfp=hfp,
                                     defer=True)
        done = [0]

        def mate(u):
            while done[0] < nvg and done[0] < 2 * (u + 1):
                vg_unit(done[0])
                done[0] += 1
        emit_l0_qkv(nc, al, plan, h, l0_wq, l0_wk, cq, sq, ck, sk, r0, qf, kf, hfp=hfp,
                    mate=mate)
        while done[0] < nvg:
            vg_unit(done[0])
            done[0] += 1
    stage(g1)

    def g2(al, plan):
        emit_l0_ret(nc, al, plan, qf, kf, v0, gate, dcol, eh, rowi, colj, cc, gg)
        emit_proj_res(nc, al, plan, gg, l0_wp, out, out,
                      afw=lambda b: (("rt_gst", 16 * b + 14), ("rt_gst", 16 * b + 15)))
        emit_ln(nc, al, plan, out, h, cc, xw=lambda k: (("pj_st", k),))
    stage(g2)

    stage(lambda al, plan: emit_moe(nc, al, plan, h, wg0, m0w1, m0w2, m0w3, b1a, b2a,
                                    b3a, onesrow, AF.Silu, None, out, use_shared=False))

    def g4(al, plan):
        emit_ln(nc, al, plan, out, h, cc)
        hfp = load_hfm(nc, plan, al.sb, h,
                       hw=lambda b: (("ln_st", 16 * b + 14), ("ln_st", 16 * b + 15)))
        vg_unit, nvg, _ = emit_l0_vg(nc, al, plan, h, [(l1_wv, 1024, v1, None)],
                                     hfp=hfp, defer=True)
        done = [0]

        def mate(u):
            while done[0] < nvg and done[0] < (u + 1) // 2:
                vg_unit(done[0])
                done[0] += 1
        emit_l1_qkv(nc, al, plan, h, l1_wq, l1_wk, c1, s1, r1, h2, h2t, cc, qf1, kf1,
                    hfp=hfp, mate=mate)
        while done[0] < nvg:
            vg_unit(done[0])
            done[0] += 1
    stage(g4)

    def g5(al, plan):
        emit_l1_attn(nc, al, plan, qf1, kf1, v1, l1_wo, l1_bo, onesrow, dsa, dsb, out, out)
        emit_ln(nc, al, plan, out, h, cc, xw=lambda k: (("at_st", k),))
    stage(g5)

    stage(lambda al, plan: emit_moe(nc, al, plan, h, wg1, m1w1, None, m1w3, None, None, None,
                                    None, AF.Gelu, None, out, use_shared=True))
    fin = Plan()
    fin.barrier("sync")
    fin.emit(nc, sems, bases, entry=dict(bases))
    return nc


def kernel_fused(inputs):
    sh = _prep_shared(inputs)
    ht = sh["ht"]
    x_full = np.ascontiguousarray(np.asarray(inputs["L0_attn_hidden_states"],
                                             dtype=np.float32).reshape(B * T, D))
    nc = build_fused()
    base = {"cc": ht["CONSTC"], "cq": ht["CQ0"], "sq": ht["SQ0"], "ck": ht["CK0"],
            "sk": ht["SK0"], "r0": ht["R0"], "r1": ht["R1"], "c1": ht["C1"],
            "s1": ht["S1"], "h2": ht["HALF2"], "h2t": ht["HALF2T"],
            "onesrow": ht["ONESROW"], "dsa": ht["DENSEL_A"], "dsb": ht["DENSEL_B"],
            "eh": ht["EH"], "rowi": ht["ROWI"], "colj": ht["COLJ"],
            "dcol": sh["decay_col"],
            "l0_wq": sh["l0_wq"], "l0_wk": sh["l0_wk"], "l0_wv": sh["l0_wv"],
            "l0_wg": sh["l0_wg"], "l0_wp": sh["l0_wp"], "wgate0": sh["m0_wgate"],
            "b1a": sh["m0_b1a"], "b2a": sh["m0_b2a"], "b3a": sh["m0_b3a"],
            "l1_wq": sh["l1_wq"], "l1_wk": sh["l1_wk"], "l1_wv": sh["l1_wv"],
            "l1_wo": sh["l1_wo"], "l1_bo": sh["l1_bo"], "wgate1": sh["m1_wgate"]}
    for e in range(8):
        base[f"m0w1_{e}"] = sh["m0_w1"][e]
        base[f"m0w2_{e}"] = sh["m0_w2"][e]
        base[f"m0w3_{e}"] = sh["m0_w3"][e]
    for e in range(9):
        base[f"m1w1_{e}"] = sh["m1_w1"][e]
        base[f"m1w3_{e}"] = sh["m1_w3"][e]
    in_maps = [dict(base, x=x_full[c * TC:(c + 1) * TC]) for c in range(NCORE)]
    res = run_bass_kernel_spmd(nc, in_maps, core_ids=list(range(NCORE)))
    out = np.concatenate([np.asarray(res.results[c]["out"]) for c in range(NCORE)], axis=0)
    return out.reshape(B, T, D).astype(np.float32)


FUSED = True


def _prep_shared(inputs):
    ht = host_tables()
    C = np.ascontiguousarray
    sh = {}
    wqkv = inputs["L0_attn_W_qkv"]
    sh["l0_wq"] = C(wqkv[:, :1024])
    sh["l0_wk"] = C(wqkv[:, 1024:2048])
    sh["l0_wv"] = C(wqkv[:, 2048:])
    sh["l0_wg"] = C(inputs["L0_attn_W_g"])
    sh["l0_wp"] = C(inputs["L0_attn_W_proj"])
    sh["decay_col"] = C(np.broadcast_to(np.asarray(inputs["L0_attn_decay"],
                                                   dtype=np.float32), (128, 8)))
    sh["m0_wgate"] = C(np.asarray(inputs["L0_ffn_Wg"]).T)
    sh["m0_w1"] = [C(np.asarray(inputs["L0_ffn_W1"][e]).T) for e in range(8)]
    sh["m0_w2"] = [C(np.asarray(inputs["L0_ffn_W2"][e]).T) for e in range(8)]
    sh["m0_w3"] = [C(np.asarray(inputs["L0_ffn_W3"][e]).T) for e in range(8)]
    b1 = np.asarray(inputs["L0_ffn_b1"], dtype=np.float32)
    b2 = np.asarray(inputs["L0_ffn_b2"], dtype=np.float32)
    b3 = np.asarray(inputs["L0_ffn_b3"], dtype=np.float32)
    sh["m0_b1a"] = C(b1.reshape(8, 16, 128).transpose(2, 0, 1).reshape(128, 128))
    sh["m0_b2a"] = C(b2.reshape(8, 16, 128).transpose(2, 0, 1).reshape(128, 128))
    sh["m0_b3a"] = C(b3.reshape(8, 1024))
    wqkv1 = inputs["L1_attn_Wqkv"]
    sh["l1_wq"] = C(wqkv1[:, :1024])
    sh["l1_wk"] = C(wqkv1[:, 1024:2048])
    sh["l1_wv"] = C(wqkv1[:, 2048:])
    sh["l1_wo"] = C(inputs["L1_attn_Wo"])
    sh["l1_bo"] = C(np.asarray(inputs["L1_attn_bo"], dtype=np.float32).reshape(1, 1024))
    sh["m1_wgate"] = C(np.asarray(inputs["L1_ffn_Wg"]))
    sh["m1_w1"] = [C(np.asarray(inputs["L1_ffn_Wu"][e])) for e in range(8)] + \
                  [C(np.asarray(inputs["L1_ffn_W1"]))]
    sh["m1_w3"] = [C(np.asarray(inputs["L1_ffn_Wd"][e])) for e in range(8)] + \
                  [C(np.asarray(inputs["L1_ffn_W2"]))]
    sh["ht"] = ht
    return sh


def kernel(**inputs):
    if FUSED:
        return kernel_fused(inputs)
    sh = _prep_shared(inputs)
    ht = sh["ht"]
    x_full = np.ascontiguousarray(np.asarray(inputs["L0_attn_hidden_states"],
                                             dtype=np.float32).reshape(B * T, D))
    xs = [x_full[c * TC:(c + 1) * TC] for c in range(NCORE)]
    cores = [dict() for _ in range(NCORE)]
    allc = list(range(NCORE))

    def run(build_fn, im_fn, outs):
        nc = build_fn()
        res = run_bass_kernel_spmd(nc, [im_fn(c) for c in allc], core_ids=allc)
        for c in allc:
            for o in outs:
                cores[c][o] = np.asarray(res.results[c][o])

    run(lambda: build_ln(True),
        lambda c: {"x": xs[c], "cc": ht["CONSTC"]}, ["h", "xout"])
    run(build_l0_qkv,
        lambda c: {"h": cores[c]["h"], "wq": sh["l0_wq"], "wk": sh["l0_wk"],
                   "cq": ht["CQ0"], "sq": ht["SQ0"], "ck": ht["CK0"],
                   "sk": ht["SK0"], "r0": ht["R0"]}, ["qf", "kf"])
    run(build_l0_vg,
        lambda c: {"h": cores[c]["h"], "wv": sh["l0_wv"], "wg": sh["l0_wg"]},
        ["v", "gate"])
    run(build_l0_ret,
        lambda c: {"qf": cores[c]["qf"], "kf": cores[c]["kf"], "v": cores[c]["v"],
                   "gate": cores[c]["gate"], "dcol": sh["decay_col"],
                   "eh": ht["EH"], "rowi": ht["ROWI"], "colj": ht["COLJ"],
                   "cc": ht["CONSTC"]}, ["gg"])
    run(build_l0_proj,
        lambda c: {"gg": cores[c]["gg"], "w": sh["l0_wp"], "xin": cores[c]["xout"]},
        ["xout"])
    run(lambda: build_ln(False),
        lambda c: {"x": cores[c]["xout"], "cc": ht["CONSTC"]}, ["h"])
    def im_moe0(c):
        im = {"h": cores[c]["h"], "wgate": sh["m0_wgate"], "xin": cores[c]["xout"],
              "b1a": sh["m0_b1a"], "b2a": sh["m0_b2a"], "b3a": sh["m0_b3a"],
              "onesrow": ht["ONESROW"]}
        for e in range(8):
            im[f"w1_{e}"] = sh["m0_w1"][e]
            im[f"w2_{e}"] = sh["m0_w2"][e]
            im[f"w3_{e}"] = sh["m0_w3"][e]
        return im
    run(lambda: build_moe(0), im_moe0, ["xout"])
    run(lambda: build_ln(False),
        lambda c: {"x": cores[c]["xout"], "cc": ht["CONSTC"]}, ["h"])
    run(build_l1_qkv,
        lambda c: {"h": cores[c]["h"], "wq": sh["l1_wq"], "wk": sh["l1_wk"],
                   "c1": ht["C1"], "s1": ht["S1"], "r1": ht["R1"],
                   "h2": ht["HALF2"], "h2t": ht["HALF2T"], "cc": ht["CONSTC"]},
        ["qf", "kf"])
    run(build_l1_v, lambda c: {"h": cores[c]["h"], "wv": sh["l1_wv"]}, ["v"])
    run(build_l1_attn,
        lambda c: {"qf": cores[c]["qf"], "kf": cores[c]["kf"], "v": cores[c]["v"],
                   "wo": sh["l1_wo"], "bo": sh["l1_bo"], "onesrow": ht["ONESROW"],
                   "dsa": ht["DENSEL_A"], "dsb": ht["DENSEL_B"],
                   "xin": cores[c]["xout"]}, ["xout"])
    run(lambda: build_ln(False),
        lambda c: {"x": cores[c]["xout"], "cc": ht["CONSTC"]}, ["h"])
    def im_moe1(c):
        im = {"h": cores[c]["h"], "wgate": sh["m1_wgate"], "xin": cores[c]["xout"]}
        for e in range(9):
            im[f"w1_{e}"] = sh["m1_w1"][e]
            im[f"w3_{e}"] = sh["m1_w3"][e]
        return im
    run(lambda: build_moe(1), im_moe1, ["xout"])

    out = np.concatenate([cores[c]["xout"] for c in range(NCORE)], axis=0)
    return out.reshape(B, T, D).astype(np.float32)


# version 31
# speedup vs baseline: 1.1586x; 1.1586x over previous
import math
import numpy as np
import ml_dtypes
import concourse.bass as bass
import concourse.mybir as mybir
from concourse.bass_utils import run_bass_kernel_spmd

F32 = mybir.dt.float32
BF16 = mybir.dt.bfloat16
AF = mybir.ActivationFunctionType
ALU = mybir.AluOpType
AX = mybir.AxisListType
BF16NP = ml_dtypes.bfloat16

B, T, D = 16, 2048, 1024
NCORE = 8
BPC = B // NCORE
TC = BPC * T
NTILE = TC // 128
NCH = T // 128
LN_EPS = 1e-5
GN_EPS = 1e-5

ENGINES = ("sync", "gpsimd", "tensor", "vector", "scalar")
ENG_SEM = {"sync": "ld", "gpsimd": "lw", "tensor": "mm", "vector": "vv", "scalar": "ss"}
DMA_ENG = {"sync", "gpsimd"}


class Plan:

    def __init__(self, gsfx=""):
        self.events = []
        self.gsfx = gsfx

    def ev(self, engine, fn, wait=(), tag=None, sem=None):
        assert engine in ENGINES
        if sem is None:
            sem = ENG_SEM[engine]
        elif engine == "gpsimd":
            sem = sem + self.gsfx
        self.events.append((engine, fn, tuple(wait), tag, False, sem))

    def barrier(self, engine, wait=()):
        self.events.append((engine, None, tuple(wait), None, True, None))

    def emit(self, nc, sems=None, bases=None, entry=None):
        semnames = sorted({e[5] for e in self.events if e[5] is not None})
        if sems is None:
            sems = {}
        if bases is None:
            bases = {}
        for n in semnames:
            if n not in sems:
                sems[n] = nc.semaphore(n).__enter__()
            bases.setdefault(n, 0)
        tagpos = {}
        tagidx = {}
        run = {n: bases[n] for n in semnames}
        for idx, (engine, fn, wait_tags, tag, is_b, sem) in enumerate(self.events):
            if not is_b:
                run[sem] += 16 if engine in DMA_ENG else 1
            if tag is not None:
                assert tag not in tagpos, f"dup tag {tag}"
                tagpos[tag] = (sem, run[sem])
                tagidx[tag] = idx
        with nc.Block() as block:
            for engine in ENGINES:
                evs = [(i, f, w, b) for i, (e, f, w, t, b, sm) in enumerate(self.events)
                       if e == engine]
                if not evs:
                    continue

                def body(eng, evs=evs, engine=engine):
                    seen = {n: bases[n] for n in semnames}
                    if entry:
                        for n, cnt in sorted(entry.items()):
                            if n in sems and cnt > 0:
                                eng.wait_ge(sems[n], cnt)
                                if n in seen:
                                    seen[n] = max(seen[n], cnt)
                    for gi, fn, wait_tags, is_b in evs:
                        for wt in wait_tags:
                            psem, pcount = tagpos[wt]
                            assert tagidx[wt] < gi, f"wait on later event {wt}"
                            if pcount > seen[psem]:
                                eng.wait_ge(sems[psem], pcount)
                                seen[psem] = pcount
                        if is_b:
                            continue
                        ins = fn(eng)
                        assert ins is not None, f"event on {engine} returned no instruction"
                        _, _, _, tag, _, sem = self.events[gi]
                        ins.then_inc(sems[sem], 16 if engine in DMA_ENG else 1)

                getattr(block, engine)(body)
        for n in semnames:
            bases[n] = run[n]
        return sems


import contextlib


class Alloc:

    _seq = 0

    def __init__(self, nc, es):
        self.nc = nc
        self.es = es
        Alloc._seq += 1
        self.pfx = f"g{Alloc._seq}_"

    def sb(self, name, shape, dtype):
        return self.es.enter_context(self.nc.sbuf_tensor(self.pfx + name, shape, dtype))

    def ps(self, name, shape, dtype):
        return self.es.enter_context(self.nc.psum_tensor(self.pfx + name, shape, dtype))


_HOST = {}


def host_tables():
    if _HOST:
        return _HOST
    t = np.arange(T, dtype=np.float64)
    i = np.arange(512, dtype=np.float64)
    base_scale = (2 * i + 0.4 * D) / (1.4 * D)
    scale = base_scale[None, :] ** (t[:, None] / 512.0)
    inv_freq = 1.0 / 10000.0 ** (i / 512.0)
    ang = t[:, None] * inv_freq[None, :]
    sin, cos = np.sin(ang), np.cos(ang)
    rep = lambda a: np.ascontiguousarray(np.repeat(a, 2, axis=1).T.astype(np.float32))
    _HOST["CQ0"] = rep(cos * scale)
    _HOST["SQ0"] = rep(sin * scale)
    _HOST["CK0"] = rep(cos / scale)
    _HOST["SK0"] = rep(sin / scale)
    R0 = np.zeros((128, 128), dtype=np.float32)
    for j in range(64):
        R0[2 * j + 1, 2 * j] = -1.0
        R0[2 * j, 2 * j + 1] = 1.0
    _HOST["R0"] = R0.astype(BF16NP)
    j32 = np.arange(32, dtype=np.float64)
    invf = 1.0 / 10000.0 ** (j32 / 32.0)
    ang1 = t[:, None] * invf[None, :]
    _HOST["C1"] = np.ascontiguousarray(np.tile(np.cos(ang1).T, (4, 1)).astype(np.float32))
    _HOST["S1"] = np.ascontiguousarray(np.tile(np.sin(ang1).T, (4, 1)).astype(np.float32))
    R1 = np.zeros((128, 128), dtype=np.float32)
    for h in range(2):
        for j in range(32):
            R1[64 * h + j + 32, 64 * h + j] = -1.0
            R1[64 * h + j, 64 * h + j + 32] = 1.0
    _HOST["R1"] = R1.astype(BF16NP)
    ii = np.arange(128, dtype=np.float32)
    EH = ii[None, :] - ii[:, None]
    _HOST["EH"] = np.where(EH >= 0, EH, 1.0e6).astype(np.float32)
    _HOST["ROWI"] = np.ascontiguousarray(np.broadcast_to(ii[None, :], (128, 128)).astype(np.float32))
    _HOST["COLJ"] = np.ascontiguousarray((128.0 - ii)[:, None].astype(np.float32))
    _HOST["ONESROW"] = np.ones((1, 128), dtype=np.float32)
    half2 = np.zeros((128, 2), dtype=np.float32)
    half2[:64, 0] = 1.0
    half2[64:, 1] = 1.0
    _HOST["HALF2"] = half2
    _HOST["HALF2T"] = np.ascontiguousarray(half2.T)
    dsa = np.zeros((128, 128), dtype=np.float32)
    dsa[:, :64] = 1.0
    _HOST["DENSEL_A"] = dsa
    _HOST["DENSEL_B"] = np.ascontiguousarray(1.0 - dsa)
    cc = np.zeros((128, 4), dtype=np.float32)
    cc[:, 0] = LN_EPS
    cc[:, 1] = 1e-6
    cc[:, 2] = -0.5 * math.log(128.0)
    _HOST["CONSTC"] = cc
    return _HOST


def emit_ln(nc, al, plan, x, h, cc, xout=None, xw=None):
    sb = al.sb
    ccs = sb("ln_cc", [128, 4], F32)
    xt = [sb(f"ln_xt{i}", [128, D], F32) for i in range(2)]
    ht = [sb(f"ln_ht{i}", [128, D], BF16) for i in range(2)]
    ssq = sb("ln_ssq", [128, 2], F32)
    ssum = sb("ln_sum", [128, 2], F32)
    mean = sb("ln_mean", [128, 2], F32)
    msq = sb("ln_msq", [128, 2], F32)
    var = sb("ln_var", [128, 2], F32)
    sd = sb("ln_sd", [128, 2], F32)
    rs = sb("ln_rs", [128, 2], F32)

    plan.ev("sync", lambda e: e.dma_start(ccs[:], cc), tag=("ln_cc",), sem="s_cc")
    for k in range(NTILE):
        p = k % 2
        c = lambda s_: s_[:, p:p + 1]
        w = (("ln_app", k - 2),) if k >= 2 else ()
        if xw is not None:
            w = w + tuple(xw(k))
        plan.ev("sync", lambda e, k=k, p=p: e.dma_start(xt[p][:], x[128 * k:128 * (k + 1), :]),
                wait=w, tag=("ln_ld", k), sem=f"s_x{p}")
        if xout is not None:
            plan.ev("sync", lambda e, k=k: e.dma_start(xout[128 * k:128 * (k + 1), :],
                                                       x[128 * k:128 * (k + 1), :]),
                    wait=tuple(xw(k)) if xw is not None else (), tag=("ln_xc", k))
        plan.ev("vector", lambda e, p=p: e.reduce_sum(ssum[:, p:p + 1], xt[p][:], AX.X),
                wait=(("ln_ld", k),), tag=("ln_sum", k))
        plan.ev("scalar", lambda e, p=p: e.activation(ht[p][:], xt[p][:], AF.Square,
                                                      accum_out=ssq[:, p:p + 1]),
                wait=(("ln_ld", k),) + ((("ln_var", k - 2), ("ln_st", k - 2))
                                        if k >= 2 else ()),
                tag=("ln_sq", k))
        plan.ev("vector", lambda e, p=p: e.tensor_single_scalar(mean[:, p:p + 1],
                                                                ssum[:, p:p + 1], 1.0 / D, ALU.mult),
                wait=(("ln_sum", k),), tag=("ln_mean", k))
        plan.ev("vector", lambda e, p=p: e.tensor_single_scalar(var[:, p:p + 1],
                                                                ssq[:, p:p + 1], 1.0 / D, ALU.mult),
                wait=(("ln_sq", k),), tag=("ln_ex2", k))
        plan.ev("vector", lambda e, p=p: e.tensor_mul(msq[:, p:p + 1], mean[:, p:p + 1],
                                                      mean[:, p:p + 1]),
                wait=(("ln_mean", k),), tag=("ln_msq", k))
        plan.ev("vector", lambda e, p=p: e.tensor_sub(var[:, p:p + 1], var[:, p:p + 1],
                                                      msq[:, p:p + 1]),
                wait=(("ln_ex2", k), ("ln_msq", k)), tag=("ln_var", k))
        plan.ev("scalar", lambda e, p=p: e.activation(sd[:, p:p + 1], var[:, p:p + 1],
                                                      AF.Sqrt, bias=ccs[:, 0:1]),
                wait=(("ln_var", k), ("ln_cc",)), tag=("ln_sd", k))
        plan.ev("vector", lambda e, p=p: e.reciprocal(rs[:, p:p + 1], sd[:, p:p + 1]),
                wait=(("ln_sd", k),), tag=("ln_rs", k))
        plan.ev("vector", lambda e, p=p: e.tensor_scalar(ht[p][:], xt[p][:], mean[:, p:p + 1],
                                                         rs[:, p:p + 1], ALU.subtract, ALU.mult),
                wait=(("ln_rs", k),) + ((("ln_st", k - 2),) if k >= 2 else ()),
                tag=("ln_app", k))
        plan.ev("sync", lambda e, k=k, p=p: e.dma_start(h[128 * k:128 * (k + 1), :], ht[p][:]),
                wait=(("ln_app", k),), tag=("ln_st", k), sem=f"s_h{p}")
    out_tags = [("ln_st", k) for k in range(NTILE)]
    if xout is not None:
        out_tags += [("ln_xc", k) for k in range(NTILE)]
    return out_tags


def load_hfm(nc, plan, sb, h_tm, name="hf", hw=None):
    hf = [sb(f"{name}{b}", [128, 8 * T], BF16) for b in range(BPC)]
    for b in range(BPC):
        for c in range(8):
            plan.ev("sync", lambda e, b=b, c=c: e.dma_start(
                hf[b][:, c * T:(c + 1) * T],
                h_tm[T * b:T * (b + 1), 128 * c:128 * (c + 1)], transpose=True),
                wait=tuple(hw(b)) if hw is not None else (),
                tag=(name, b, c), sem=f"sm_{name}{b}")
    ready = tuple((name, b, c) for b in range(BPC) for c in range(8))
    return hf, ready


def load_w_bf16(nc, plan, sb, w_ap, nchunk, ncol, name):
    w_sb = sb(name, [128, nchunk * ncol], BF16)
    for c in range(nchunk):
        plan.ev("gpsimd", lambda e, c=c: e.dma_start(
            w_sb[:, c * ncol:(c + 1) * ncol], w_ap[128 * c:128 * (c + 1), :]),
            tag=(name, c), sem="sm_" + name)
    return w_sb, tuple((name, c) for c in range(nchunk))


def emit_l0_qkv(nc, al, plan, h_tm, wq, wk, cq, sq_t, ck, sk_t, r0, qf, kf, hfp=None,
                mate=None):
    sb = al.sb
    hf, hf_rdy = hfp if hfp is not None else load_hfm(nc, plan, sb, h_tm)
    wq_sb, wq_rdy = load_w_bf16(nc, plan, sb, wq, 8, 1024, "wq_sb")
    wk_sb, wk_rdy = load_w_bf16(nc, plan, sb, wk, 8, 1024, "wk_sb")
    r0_sb = sb("r0_sb", [128, 128], BF16)
    plan.ev("sync", lambda e: e.dma_start(r0_sb[:], r0), tag=("r0",), sem="s_r0")
    ct = [sb(f"rq_ct{i}", [128, 512], F32) for i in range(2)]
    st_ = [sb(f"rq_st{i}", [128, 512], F32) for i in range(2)]
    tA = [sb(f"rq_tA{i}", [128, 512], F32) for i in range(2)]
    tB = [sb(f"rq_tB{i}", [128, 512], F32) for i in range(2)]
    qraw = [sb(f"rq_qraw{i}", [128, 512], BF16) for i in range(2)]
    qout = [sb(f"rq_qout{i}", [128, 512], BF16) for i in range(2)]
    ps_q = [al.ps(f"rq_psq{i}", [128, 512], F32) for i in range(2)]
    ps_r = [al.ps(f"rq_psr{i}", [128, 512], F32) for i in range(2)]

    units = []
    for pi, (w_sb, w_rdy, ctab, stab, outt) in enumerate(
            ((wq_sb, wq_rdy, cq, sq_t, qf), (wk_sb, wk_rdy, ck, sk_t, kf))):
        for b in range(BPC):
            for oc in range(8):
                for ts in range(4):
                    units.append((pi, w_sb, w_rdy, ctab, stab, outt, b, oc, ts))
    NU = len(units)

    def rq_head(u):
        (pi, w_sb, w_rdy, ctab, stab, outt, b, oc, ts) = units[u]
        p = u % 2
        wprev = lambda t: ((t, u - 2),) if u >= 2 else ()
        plan.ev("gpsimd", lambda e, p=p, ctab=ctab, oc=oc, ts=ts: e.dma_start(
            ct[p][:], ctab[128 * oc:128 * (oc + 1), 512 * ts:512 * (ts + 1)]),
            wait=wprev("rq_tA"), tag=("rq_ct", u), sem=f"s_ct{p}")
        plan.ev("gpsimd", lambda e, p=p, stab=stab, oc=oc, ts=ts: e.dma_start(
            st_[p][:], stab[128 * oc:128 * (oc + 1), 512 * ts:512 * (ts + 1)]),
            wait=wprev("rq_tB"), tag=("rq_st", u), sem=f"s_st{p}")

        def mmq(e, p=p, w_sb=w_sb, b=b, oc=oc, ts=ts):
            for dc in range(8):
                ins = e.matmul(ps_q[p][:], w_sb[:, dc * 1024 + oc * 128:dc * 1024 + oc * 128 + 128],
                               hf[b][:, dc * T + ts * 512:dc * T + ts * 512 + 512],
                               start=(dc == 0), stop=(dc == 7))
            return ins
        plan.ev("tensor", mmq, wait=(w_rdy[-1], hf_rdy[8 * b + 7])
                + wprev("rq_tA") + wprev("rq_qcp"), tag=("rq_mmq", u))
        plan.ev("scalar", lambda e, p=p: e.copy(qraw[p][:], ps_q[p][:]),
                wait=(("rq_mmq", u),) + wprev("rq_mmr"), tag=("rq_qcp", u))
        plan.ev("vector", lambda e, p=p: e.tensor_mul(tA[p][:], ps_q[p][:], ct[p][:]),
                wait=(("rq_mmq", u), ("rq_qcp", u), ("rq_ct", u)) + wprev("rq_qadd"),
                tag=("rq_tA", u))

    def rq_tail(u):
        (pi, w_sb, w_rdy, ctab, stab, outt, b, oc, ts) = units[u]
        p = u % 2
        wprev = lambda t: ((t, u - 2),) if u >= 2 else ()
        plan.ev("tensor", lambda e, p=p: e.matmul(ps_r[p][:], r0_sb[:], qraw[p][:],
                                                  start=True, stop=True),
                wait=(("rq_qcp", u), ("r0",)) + wprev("rq_tB"), tag=("rq_mmr", u))
        plan.ev("vector", lambda e, p=p: e.tensor_mul(tB[p][:], ps_r[p][:], st_[p][:]),
                wait=(("rq_mmr", u), ("rq_st", u)) + wprev("rq_qadd"), tag=("rq_tB", u))
        plan.ev("vector", lambda e, p=p: e.tensor_add(qout[p][:], tA[p][:], tB[p][:]),
                wait=(("rq_tA", u), ("rq_tB", u)) + wprev("rq_qst"), tag=("rq_qadd", u))
        plan.ev("sync", lambda e, p=p, outt=outt, b=b, oc=oc, ts=ts: e.dma_start(
            outt[oc, :, T * b + 512 * ts:T * b + 512 * (ts + 1)], qout[p][:]),
            wait=(("rq_qadd", u),), tag=("rq_qst", u), sem=f"s_qo{p}")

    for u in range(NU):
        rq_head(u)
        if u >= 1:
            rq_tail(u - 1)
        if mate is not None:
            mate(u)
    rq_tail(NU - 1)
    return [("rq_qst", u) for u in range(len(units))]


def emit_l0_vg(nc, al, plan, h_tm, specs, hfp=None, defer=False):
    sb = al.sb
    hf, hf_rdy = hfp if hfp is not None else load_hfm(nc, plan, sb, h_tm)
    stage = [sb(f"vg_st{i}", [128, 512], BF16) for i in range(2)]
    ps = [al.ps(f"vg_ps{i}", [128, 512], F32) for i in range(2)]

    units = []
    for pi, (w_ap, ncols, outt, act) in enumerate(specs):
        w_sb, w_rdy = load_w_bf16(nc, plan, sb, w_ap, 8, ncols, f"vgw{pi}")
        for b in range(BPC):
            for tt in range(16):
                for vs in range(ncols // 512):
                    units.append((pi, w_sb, w_rdy, ncols, outt, act, b, tt, vs))
    def vg_unit(u):
        (pi, w_sb, w_rdy, ncols, outt, act, b, tt, vs) = units[u]
        p = u % 2
        wprev = lambda t: ((t, u - 2),) if u >= 2 else ()

        def mm(e, p=p, w_sb=w_sb, ncols=ncols, b=b, tt=tt, vs=vs):
            for dc in range(8):
                ins = e.matmul(ps[p][:], hf[b][:, dc * T + tt * 128:dc * T + tt * 128 + 128],
                               w_sb[:, dc * ncols + vs * 512:dc * ncols + vs * 512 + 512],
                               start=(dc == 0), stop=(dc == 7))
            return ins
        plan.ev("tensor", mm, wait=(w_rdy[-1], hf_rdy[8 * b + 7]) + wprev("vg_ev"),
                tag=("vg_mm", u))
        if act is None:
            plan.ev("scalar", lambda e, p=p: e.copy(stage[p][:], ps[p][:]),
                    wait=(("vg_mm", u),) + wprev("vg_st"), tag=("vg_ev", u))
        else:
            plan.ev("scalar", lambda e, p=p, act=act: e.activation(stage[p][:], ps[p][:], act),
                    wait=(("vg_mm", u),) + wprev("vg_st"), tag=("vg_ev", u))
        plan.ev("sync", lambda e, p=p, outt=outt, b=b, tt=tt, vs=vs: e.dma_start(
            outt[T * b + 128 * tt:T * b + 128 * (tt + 1), 512 * vs:512 * (vs + 1)],
            stage[p][:]),
            wait=(("vg_ev", u),), tag=("vg_st", u), sem=f"s_vo{p}")
    return vg_unit, len(units), [("vg_st", u) for u in range(len(units))]


def emit_l0_ret(nc, al, plan, qf, kf, v_tm, gate_tm, decay_col, eh, rowi, colj, cc, gg_tm):
    sb = al.sb
    ps = al.ps
    dc_sb = sb("rt_dc", [128, 8], F32)
    eh_sb = sb("rt_eh", [128, 128], F32)
    ri_sb = sb("rt_ri", [128, 128], F32)
    cj_sb = sb("rt_cj", [128, 1], F32)
    ccs = sb("rt_cc", [128, 4], F32)
    lncol = sb("rt_lncol", [128, 8], F32)
    MT = sb("rt_MT", [128, 8 * 128], F32)
    BH = sb("rt_BH", [128, 8 * 128], BF16)
    cvec = sb("rt_cvec", [128, 8], F32)
    Dcol = sb("rt_Dcol", [128, 8], F32)
    tmpm = sb("rt_tmpm", [128, 128], F32)
    tmpb = tmpm
    S_f = sb("rt_Sf", [128, 8 * 256], F32)
    S_b = sb("rt_Sb", [128, 8 * 256], BF16)
    Qc = [sb(f"rt_Qc{i}", [128, 8 * 128], BF16) for i in range(2)]
    Kc = [sb(f"rt_Kc{i}", [128, 8 * 128], BF16) for i in range(2)]
    Vc = [sb(f"rt_Vc{i}", [128, 2048], BF16) for i in range(2)]
    Ktm = [sb(f"rt_Ktm{i}", [128, 1024], BF16) for i in range(2)]
    gatec = [sb(f"rt_gatec{i}", [128, 2048], BF16) for i in range(2)]
    Am = [sb(f"rt_Am{i}", [128, 128], BF16) for i in range(3)]
    Qd = [sb(f"rt_Qd{i}", [128, 128], BF16) for i in range(3)]
    Kw = [sb(f"rt_Kw{i}", [128, 128], BF16) for i in range(2)]
    ret = [sb(f"rt_ret{i}", [128, 2048], F32) for i in range(3)]
    ggst = [sb(f"rt_ggst{i}", [128, 2048], BF16) for i in range(2)]
    gtmp = [sb("rt_gtmp", [128, 256], BF16)] * 2
    sqsc = sb("rt_sqsc", [128, 256], BF16)
    sum8 = [sb(f"rt_sum8{i}", [128, 8], F32) for i in range(3)]
    ssq8 = [sb("rt_ssq8", [128, 8], F32)] * 3
    mean8 = [sb(f"rt_mean8{i}", [128, 8], F32) for i in range(3)]
    msq8 = [sb("rt_msq8", [128, 8], F32)] * 3
    var8 = [sb("rt_var8", [128, 8], F32)] * 3
    sd8 = [sb("rt_sd8", [128, 8], F32)] * 3
    rs8 = [sb(f"rt_rs8{i}", [128, 8], F32) for i in range(3)]
    ps_a = [ps(f"rt_psa{i}", [128, 128], F32) for i in range(2)]
    ps_o = [ps(f"rt_pso{i}", [128, 256], F32) for i in range(2)]
    ps_s = [ps(f"rt_pss{i}", [128, 256], F32) for i in range(2)]

    for name, dst, src in (("rt_pdc", dc_sb, decay_col), ("rt_peh", eh_sb, eh),
                           ("rt_pri", ri_sb, rowi), ("rt_pcj", cj_sb, colj),
                           ("rt_pcc", ccs, cc)):
        plan.ev("sync", lambda e, dst=dst, src=src: e.dma_start(dst[:], src),
                tag=(name,), sem="s_pre")
    plan.ev("scalar", lambda e: e.activation(lncol[:], dc_sb[:], AF.Ln),
            wait=(("rt_pcc",),), tag=("rt_lnc",))
    for h in range(8):
        plan.ev("vector", lambda e, h=h: e.tensor_scalar_mul(tmpm[:], eh_sb[:], lncol[:, h:h + 1]),
                wait=(("rt_lnc",),) + ((("rt_BH", h - 1),) if h else ()),
                tag=("rt_tm", h))
        plan.ev("scalar", lambda e, h=h: e.activation(MT[:, 128 * h:128 * (h + 1)], tmpm[:],
                                                      AF.Exp, bias=ccs[:, 2:3]),
                wait=(("rt_tm", h),), tag=("rt_MT", h))
        plan.ev("vector", lambda e, h=h: e.tensor_scalar_mul(tmpb[:], ri_sb[:], lncol[:, h:h + 1]),
                wait=(("rt_lnc",), ("rt_MT", h)),
                tag=("rt_tb", h))
        plan.ev("scalar", lambda e, h=h: e.activation(BH[:, 128 * h:128 * (h + 1)], tmpb[:],
                                                      AF.Exp, bias=ccs[:, 2:3]),
                wait=(("rt_tb", h),), tag=("rt_BH", h))
        plan.ev("vector", lambda e, h=h: e.tensor_scalar_mul(cvec[:, h:h + 1], cj_sb[:],
                                                             lncol[:, h:h + 1]),
                wait=(("rt_lnc",),), tag=("rt_tc", h))
        plan.ev("scalar", lambda e, h=h: e.activation(cvec[:, h:h + 1], cvec[:, h:h + 1], AF.Exp),
                wait=(("rt_tc", h),), tag=("rt_cv", h))
    plan.ev("vector", lambda e: e.tensor_single_scalar(msq8[0][:], lncol[:], 128.0, ALU.mult),
            wait=(("rt_lnc",),), tag=("rt_td",))
    plan.ev("scalar", lambda e: e.activation(Dcol[:], msq8[0][:], AF.Exp),
            wait=(("rt_td",),), tag=("rt_Dc",))

    NH = 8
    def emit_gn(g):
            p2 = g % 3
            wg2p = lambda t: ((t, g - 3),) if g >= 3 else ()
            for h in range(8):
                H = g * NH + h
                plan.ev("scalar", lambda e, h=h, p2=p2: e.activation(
                    sqsc[:], ret[p2][:, 256 * h:256 * (h + 1)], AF.Square,
                    accum_out=ssq8[p2][:, h:h + 1]),
                    wait=(("rt_rc", H),) +
                    ((("rt_gv2", g - 1),) if g >= 1 else ()),
                    tag=("rt_gs", g, h))
            plan.ev("vector", lambda e, p2=p2: e.tensor_single_scalar(
                mean8[p2][:], sum8[p2][:], 1.0 / 256, ALU.mult),
                    wait=(("rt_rc", g * NH + 7),) +
                    ((("rt_gt", g - 3, 7),) if g >= 3 else ()), tag=("rt_gv1", g))
            plan.ev("vector", lambda e, p2=p2: e.tensor_single_scalar(
                var8[p2][:], ssq8[p2][:], 1.0 / 256, ALU.mult),
                    wait=(("rt_gs", g, 7),) +
                    ((("rt_gsd", g - 1),) if g >= 1 else ()), tag=("rt_gv2", g))
            plan.ev("vector", lambda e, p2=p2: e.tensor_mul(
                msq8[p2][:], mean8[p2][:], mean8[p2][:]),
                    wait=(("rt_gv1", g), ("rt_Dc",)) +
                    ((("rt_gv4", g - 1),) if g >= 1 else ()), tag=("rt_gv3", g))
            plan.ev("vector", lambda e, p2=p2: e.tensor_sub(
                var8[p2][:], var8[p2][:], msq8[p2][:]),
                    wait=(("rt_gv2", g), ("rt_gv3", g)), tag=("rt_gv4", g))
            plan.ev("scalar", lambda e, p2=p2: e.activation(
                sd8[p2][:], var8[p2][:], AF.Sqrt, bias=ccs[:, 0:1]),
                    wait=(("rt_gv4", g), ("rt_pcc",)) +
                    ((("rt_grs", g - 1),) if g >= 1 else ()),
                    tag=("rt_gsd", g))
            plan.ev("vector", lambda e, p2=p2: e.reciprocal(rs8[p2][:], sd8[p2][:]),
                    wait=(("rt_gsd", g),) +
                    ((("rt_gt", g - 3, 7),) if g >= 3 else ()), tag=("rt_grs", g))
            for h in range(8):
                hp = h % 2
                plan.ev("vector", lambda e, hp=hp, h=h, p2=p2: e.tensor_scalar(
                    gtmp[hp][:], ret[p2][:, 256 * h:256 * (h + 1)],
                    mean8[p2][:, h:h + 1], rs8[p2][:, h:h + 1],
                    ALU.subtract, ALU.mult),
                    wait=(("rt_grs", g),) + ((("rt_gg", g, h - 1),) if h >= 1 else ()),
                    tag=("rt_gt", g, h))
                plan.ev("vector", lambda e, hp=hp, h=h, p2=p2, pg=g % 2: e.tensor_mul(
                    ggst[pg][:, 256 * h:256 * (h + 1)], gtmp[hp][:],
                    gatec[pg][:, 256 * h:256 * (h + 1)]),
                    wait=(("rt_gt", g, h), ("rt_lg", g)) +
                    ((("rt_gst", g - 2),) if g >= 2 else ()),
                    tag=("rt_gg", g, h))
            plan.ev("gpsimd", lambda e, g=g, pg=g % 2: e.dma_start(
                gg_tm[128 * g:128 * (g + 1), :], ggst[pg][:]),
                wait=(("rt_gg", g, 7),), tag=("rt_gst", g), sem=f"s_go{g % 2}")
    for b in range(BPC):
        wmf = (("rt_sbcg", 15),) if b else ()
        wmb = (("rt_mo", 127),) if b else ()
        plan.ev("gpsimd", lambda e: e.memset(S_f[:], 0.0), wait=wmf, tag=("rt_msf", b))
        plan.ev("gpsimd", lambda e: e.memset(S_b[:], 0.0), wait=wmb, tag=("rt_msb", b))
        for n in range(NCH):
            g = b * NCH + n
            p = g % 2
            wg2 = lambda t, d=2: ((t, g - d),) if g >= d else ()
            plan.ev("sync", lambda e, p=p, g=g: e.dma_start(
                Qc[p][:].rearrange("p (c q) -> p c q", c=8),
                qf[:, :, 128 * g:128 * (g + 1)].rearrange("c p q -> p c q")),
                wait=(("rt_qd", (g - 2) * NH + 7), ("rt_ma", (g - 2) * NH + 7)) if g >= 2 else (),
                tag=("rt_lq", g), sem=f"s_q{p}")
            plan.ev("sync", lambda e, p=p, g=g: e.dma_start(
                Kc[p][:].rearrange("p (c q) -> p c q", c=8),
                kf[:, :, 128 * g:128 * (g + 1)].rearrange("c p q -> p c q")),
                wait=(("rt_ma", (g - 2) * NH + 7),) if g >= 2 else (),
                tag=("rt_lk", g), sem=f"s_k{p}")
            plan.ev("sync", lambda e, pv=g % 2, g=g: e.dma_start(
                Vc[pv][:], v_tm[128 * g:128 * (g + 1), :]),
                wait=(("rt_mss", (g - 2) * NH + 7),) if g >= 2 else (),
                tag=("rt_lv", g), sem=f"s_v{g % 2}")
            plan.ev("sync", lambda e, p=p, g=g: e.dma_start(
                Ktm[p][:], kf[:, :, 128 * g:128 * (g + 1)].rearrange("c p q -> (c p) q"),
                transpose=True),
                wait=(("rt_kw", (g - 2) * NH + 7),) if g >= 2 else (),
                tag=("rt_lkt", g), sem=f"s_kt{p}")
            if g < 2:
                plan.ev("gpsimd", lambda e, pv=g % 2, g=g: e.dma_start(
                    gatec[pv][:], gate_tm[128 * g:128 * (g + 1), :]),
                    wait=(), tag=("rt_lg", g), sem=f"s_g{g % 2}")
            def head_pre(h):
                H = g * NH + h
                hp = h % 2
                hp = h % 2
                plan.ev("tensor", lambda e, p=p, hp=hp, h=h: e.matmul(
                    ps_a[hp][:], Kc[p][:, 128 * h:128 * (h + 1)],
                    Qc[p][:, 128 * h:128 * (h + 1)], start=True, stop=True),
                    wait=(("rt_lk", g), ("rt_lq", g)) + ((("rt_am", H - 2),) if H >= 2 else ()),
                    tag=("rt_ma", H))
                plan.ev("vector", lambda e, hp=hp, ha=H % 3, h=h: e.tensor_mul(
                    Am[ha][:], ps_a[hp][:], MT[:, 128 * h:128 * (h + 1)]),
                    wait=(("rt_ma", H), ("rt_MT", h)) + ((("rt_mo", H - 3),) if H >= 3 else ()),
                    tag=("rt_am", H))
                plan.ev("vector", lambda e, p=p, ha=H % 3, h=h: e.tensor_mul(
                    Qd[ha][:], Qc[p][:, 128 * h:128 * (h + 1)], BH[:, 128 * h:128 * (h + 1)]),
                    wait=(("rt_lq", g), ("rt_BH", h)) + ((("rt_mo", H - 3),) if H >= 3 else ()),
                    tag=("rt_qd", H))

            def head_main(h):
                H = g * NH + h
                hp = h % 2
                def mmo(e, pv=g % 2, ha=H % 3, hp=hp, h=h):
                    e.matmul(ps_o[hp][:], Am[ha][:], Vc[pv][:, 256 * h:256 * (h + 1)],
                             start=True, stop=False)
                    return e.matmul(ps_o[hp][:], Qd[ha][:], S_b[:, 256 * h:256 * (h + 1)],
                                    start=False, stop=True)
                plan.ev("tensor", mmo,
                        wait=(("rt_am", H), ("rt_qd", H), ("rt_lv", g),
                              (("rt_sbcg", g - 1) if n > 0 else ("rt_msb", b))) +
                             ((("rt_rc", H - 2),) if H >= 2 else ()),
                        tag=("rt_mo", H))
                plan.ev("scalar", lambda e, hp=hp, h=h, p2=g % 3: e.activation(
                    ret[p2][:, 256 * h:256 * (h + 1)], ps_o[hp][:], AF.Copy,
                    accum_out=sum8[p2][:, h:h + 1]),
                    wait=(("rt_mo", H),) + ((("rt_gt", g - 3, h),) if g >= 3 else ())
                    + ((("rt_gv1", g - 3),) if g >= 3 else ()),
                    tag=("rt_rc", H))
                plan.ev("vector", lambda e, p=p, hp=hp, h=h: e.tensor_scalar_mul(
                    Kw[hp][:], Ktm[p][:, 128 * h:128 * (h + 1)], cvec[:, h:h + 1]),
                    wait=(("rt_lkt", g), ("rt_cv", h)) + ((("rt_mss", H - 2),) if H >= 2 else ()),
                    tag=("rt_kw", H))
                plan.ev("tensor", lambda e, pv=g % 2, hp=hp, h=h: e.matmul(
                    ps_s[hp][:], Kw[hp][:], Vc[pv][:, 256 * h:256 * (h + 1)],
                    start=True, stop=True),
                    wait=(("rt_kw", H), ("rt_lv", g)) + ((("rt_su", H - 2),) if H >= 2 else ()),
                    tag=("rt_mss", H))
                plan.ev("vector", lambda e, hp=hp, h=h: e.scalar_tensor_tensor(
                    S_f[:, 256 * h:256 * (h + 1)], S_f[:, 256 * h:256 * (h + 1)],
                    Dcol[:, h:h + 1], ps_s[hp][:], ALU.mult, ALU.add),
                    wait=(("rt_mss", H), ("rt_Dc",)) +
                         ((("rt_msf", b),) if n == 0 else ()),
                    tag=("rt_su", H))
            for h in range(8):
                head_pre(h)
                if h >= 2:
                    head_main(h - 2)
            head_main(6)
            head_main(7)
            plan.ev("scalar", lambda e: e.copy(S_b[:], S_f[:]),
                    wait=(("rt_su", g * NH + 7),), tag=("rt_sbcg", g))
            if g >= 2:
                emit_gn(g - 2)
                plan.ev("gpsimd", lambda e, pv=g % 2, g=g: e.dma_start(
                    gatec[pv][:], gate_tm[128 * g:128 * (g + 1), :]),
                    wait=(("rt_gg", g - 2, 7),), tag=("rt_lg", g),
                    sem=f"s_g{g % 2}")

    emit_gn(BPC * NCH - 2)
    emit_gn(BPC * NCH - 1)
    return [("rt_gst", g) for g in range(BPC * NCH)]


def emit_proj_res(nc, al, plan, act_tm, w, x_in, x_out, nck=16, afw=None):
    sb = al.sb
    w_sb, w_rdy = load_w_bf16(nc, plan, sb, w, nck, D, "pj_w")
    af = sb("pj_af", [128, nck * T], BF16)
    xt = [sb(f"pj_xt{i}", [128, D], F32) for i in range(2)]
    xo = [sb(f"pj_xo{i}", [128, D], F32) for i in range(2)]
    pso = [al.ps(f"pj_ps{i}", [128, 512], F32) for i in range(2)]

    for b in range(BPC):
        for c in range(nck):
            wtr = (("pj_mm", b * 16 - 1, 1),) if b >= 1 else ()
            if afw is not None:
                wtr = wtr + tuple(afw(b))
            plan.ev("sync", lambda e, b=b, c=c: e.dma_start(
                af[:, c * T:(c + 1) * T],
                act_tm[T * b:T * (b + 1), 128 * c:128 * (c + 1)], transpose=True),
                wait=wtr, tag=("pj_af", b, c), sem=f"s_af{c % 2}")
        for tt in range(16):
            u = b * 16 + tt
            p = u % 2
            plan.ev("sync", lambda e, b=b, tt=tt, p=p: e.dma_start(
                xt[p][:], x_in[T * b + 128 * tt:T * b + 128 * (tt + 1), :]),
                wait=(("pj_add", u - 2, 1),) if u >= 2 else (),
                tag=("pj_xt", u), sem=f"s_x{p}")
            for ds in range(2):
                def mm(e, b=b, tt=tt, ds=ds):
                    for c in range(nck):
                        ins = e.matmul(pso[ds][:],
                                       af[:, c * T + tt * 128:c * T + tt * 128 + 128],
                                       w_sb[:, c * D + ds * 512:c * D + ds * 512 + 512],
                                       start=(c == 0), stop=(c == nck - 1))
                    return ins
                plan.ev("tensor", mm,
                        wait=(w_rdy[-1], ("pj_af", b, nck - 2), ("pj_af", b, nck - 1)) +
                             ((("pj_add", u - 1, ds),) if u >= 1 else ()),
                        tag=("pj_mm", u, ds))
                plan.ev("vector", lambda e, ds=ds, p=p: e.tensor_add(
                    xo[p][:, 512 * ds:512 * (ds + 1)], pso[ds][:],
                    xt[p][:, 512 * ds:512 * (ds + 1)]),
                    wait=(("pj_mm", u, ds), ("pj_xt", u)) +
                         ((("pj_st", u - 2),) if u >= 2 else ()),
                    tag=("pj_add", u, ds))
            plan.ev("sync", lambda e, b=b, tt=tt, p=p: e.dma_start(
                x_out[T * b + 128 * tt:T * b + 128 * (tt + 1), :], xo[p][:]),
                wait=(("pj_add", u, 0), ("pj_add", u, 1)),
                tag=("pj_st", u), sem=f"s_xo{p}")
    return [("pj_st", u) for u in range(BPC * 16)]


def emit_moe(nc, al, plan, h_tm, wg, w1l, w2l, w3l, b1a, b2a, b3a, onesrow, act,
             x_in, x_out, use_shared, hfp=None):
    E = len(w1l)
    glu = w2l is not None
    sb = al.sb
    ps = al.ps
    hfs = sb("mo_hfs", [128, 8 * T], BF16)

    def reload_hf(ei, b, wait):
        for c in range(8):
            plan.ev("sync", lambda e, b=b, c=c: e.dma_start(
                hfs[:, c * T:(c + 1) * T],
                h_tm[T * b:T * (b + 1), 128 * c:128 * (c + 1)], transpose=True),
                wait=wait if c == 0 else (), tag=("mo_hf", ei, b, c), sem="sm_mo_hf")
    wg_sb, wg_rdy = load_w_bf16(nc, plan, sb, wg, 8, 8, "mo_wg")
    w1_sb = sb("mo_w1", [128, 8 * 2048], BF16)
    w2_sb = sb("mo_w2", [128, 8 * 2048], BF16) if glu else None
    w3_sb = sb("mo_w3", [128, 16 * D], BF16)
    onesr = sb("mo_ones", [1, 128], BF16)
    if b1a is not None:
        b1_sb = sb("mo_b1", [128, E * 16], F32)
        b2_sb = sb("mo_b2", [128, E * 16], F32)
        b3_sb = sb("mo_b3", [1, D], BF16)
        plan.ev("gpsimd", lambda e: e.dma_start(b1_sb[:], b1a), tag=("mo_b1",), sem="s_pre2")
        plan.ev("gpsimd", lambda e: e.dma_start(b2_sb[:], b2a), tag=("mo_b2",), sem="s_pre2")
        plan.ev("gpsimd", lambda e: e.dma_start(onesr[:], onesrow),
                tag=("mo_on",), sem="s_pre2")
        pre2 = (("mo_on",),)
    else:
        pre2 = ()
    comb = sb("mo_comb", [128, NTILE * E], F32)
    m1c = sb("mo_m1", [128, 2], F32)
    m2c = sb("mo_m2", [128, 2], F32)
    ddc = sb("mo_dd", [128, 2], F32)
    w1c = sb("mo_w1c", [128, 2], F32)
    w2c = sb("mo_w2c", [128, 2], F32)
    mk1 = sb("mo_mk1", [128, 2 * 8], F32)
    mk2 = sb("mo_mk2", [128, 2 * 8], F32)
    gg2 = sb("mo_gg2", [128, 2 * 8], F32)
    t1c = sb("mo_t1", [128, 2 * 8], F32)
    sg = [sb(f"mo_sg{i}", [128, 512], F32) for i in range(2)]
    he = [sb(f"mo_he{i}", [128, 16 * 512], BF16) for i in range(2)]
    st4 = [sb(f"mo_st4{i}", [128, D], F32) for i in range(2)]
    pgt = [ps(f"mo_pgt{i}", [128, 8], F32) for i in range(2)]
    pg = [ps(f"mo_pg{i}", [128, 512], F32) for i in range(2)]
    pv = [ps(f"mo_pv{i}", [128, 512], F32) for i in range(2)] if glu else None
    po = [ps(f"mo_po{i}", [128, 512], F32) for i in range(2)]

    if x_in is not None:
        for k in range(NTILE):
            plan.ev("gpsimd", lambda e, k=k: e.dma_start(x_out[128 * k:128 * (k + 1), :],
                                                         x_in[128 * k:128 * (k + 1), :]),
                    tag=("mo_xcp", k), sem=f"s_mac{k % 2}")
        acc_prev0 = (("mo_xcp", NTILE - 2), ("mo_xcp", NTILE - 1))
    else:
        acc_prev0 = ()

    if use_shared:
        plan.ev("gpsimd", lambda e: e.memset(comb[:], 1.0), tag=("mo_cmbi",))
        cmbi = (("mo_cmbi",),)
    else:
        cmbi = ()
    reload_hf(0, 0, ())

    def emit_gating(k):
        b, tt = divmod(k, 16)
        p = k % 2

        def mmg(e, b=b, tt=tt, p=p):
            for dc in range(8):
                ins = e.matmul(pgt[p][:], hfs[:, dc * T + tt * 128:dc * T + tt * 128 + 128],
                               wg_sb[:, dc * 8:dc * 8 + 8],
                               start=(dc == 0), stop=(dc == 7))
            return ins
        plan.ev("tensor", mmg, wait=(wg_rdy[-1], ("mo_hf", 0, b, 7)) +
                ((("mo_mk2", k - 2),) if k >= 2 else ()), tag=("mo_gmm", k))
        plan.ev("vector", lambda e, p=p: e.reduce_max(m1c[:, p:p + 1], pgt[p][:], AX.X),
                wait=(("mo_gmm", k),) + ((("mo_cw2", k - 2),) if k >= 2 else ()),
                tag=("mo_m1", k))
        plan.ev("vector", lambda e, p=p: e.tensor_single_scalar(
            mk1[:, 8 * p:8 * (p + 1)], pgt[p][:], m1c[:, p:p + 1], ALU.is_ge),
            wait=(("mo_m1", k),) + ((("mo_ct1", k - 2),) if k >= 2 else ()),
            tag=("mo_mk1", k))
        plan.ev("vector", lambda e, p=p: e.scalar_tensor_tensor(
            gg2[:, 8 * p:8 * (p + 1)], mk1[:, 8 * p:8 * (p + 1)], -1.0e30, pgt[p][:],
            ALU.mult, ALU.add),
            wait=(("mo_mk1", k),), tag=("mo_g2", k))
        plan.ev("vector", lambda e, p=p: e.reduce_max(m2c[:, p:p + 1],
                                                      gg2[:, 8 * p:8 * (p + 1)], AX.X),
                wait=(("mo_g2", k),), tag=("mo_m2", k))
        plan.ev("vector", lambda e, p=p: e.tensor_single_scalar(
            mk2[:, 8 * p:8 * (p + 1)], gg2[:, 8 * p:8 * (p + 1)], m2c[:, p:p + 1], ALU.is_ge),
            wait=(("mo_m2", k),), tag=("mo_mk2", k))
        plan.ev("vector", lambda e, p=p: e.tensor_sub(ddc[:, p:p + 1], m1c[:, p:p + 1],
                                                      m2c[:, p:p + 1]),
                wait=(("mo_m2", k),), tag=("mo_dd", k))
        plan.ev("scalar", lambda e, p=p: e.activation(w1c[:, p:p + 1], ddc[:, p:p + 1],
                                                      AF.Sigmoid),
                wait=(("mo_dd", k),), tag=("mo_cw1", k))
        plan.ev("vector", lambda e, p=p: e.tensor_scalar(w2c[:, p:p + 1], w1c[:, p:p + 1],
                                                         -1.0, 1.0, ALU.mult, ALU.add),
                wait=(("mo_cw1", k),), tag=("mo_cw2", k))
        plan.ev("vector", lambda e, p=p: e.tensor_scalar_mul(
            t1c[:, 8 * p:8 * (p + 1)], mk1[:, 8 * p:8 * (p + 1)], w1c[:, p:p + 1]),
            wait=(("mo_cw1", k),), tag=("mo_ct1", k))
        plan.ev("vector", lambda e, p=p, k=k, E=E: e.scalar_tensor_tensor(
            comb[:, E * k:E * k + 8], mk2[:, 8 * p:8 * (p + 1)], w2c[:, p:p + 1],
            t1c[:, 8 * p:8 * (p + 1)], ALU.mult, ALU.add),
            wait=(("mo_cw2", k), ("mo_ct1", k)) + cmbi, tag=("mo_cmb", k))

    for k in range(16):
        emit_gating(k)
    gating_b1_pending = [True]

    def emit_w12(ei):
        for c in range(8):
            if ei >= 1:
                gu_last = ((ei - 1) * 8 + 7) * 16 + 15
                wl = (("mo_gv", ei - 1, 7, 15),) if glu else (("mo_g1", gu_last),)
            else:
                wl = ()
            plan.ev("gpsimd", lambda e, ei=ei, c=c: e.dma_start(
                w1_sb[:, c * 2048:(c + 1) * 2048], w1l[ei][128 * c:128 * (c + 1), :]),
                wait=wl, tag=("mo_lw1", ei, c), sem="s_w1")
            if glu:
                plan.ev("gpsimd", lambda e, ei=ei, c=c: e.dma_start(
                    w2_sb[:, c * 2048:(c + 1) * 2048], w2l[ei][128 * c:128 * (c + 1), :]),
                    wait=wl, tag=("mo_lw2", ei, c), sem="s_w2")

    def emit_w3(ei):
        wl = (("mo_w3", ei - 1, 7, 3, 1),) if ei >= 1 else ()
        if b1a is not None:
            plan.ev("gpsimd", lambda e, ei=ei: e.dma_start(b3_sb[:], b3a[ei:ei + 1, :]),
                    wait=wl, tag=("mo_lb3", ei), sem="s_b3")
        for c in range(16):
            plan.ev("gpsimd", lambda e, ei=ei, c=c: e.dma_start(
                w3_sb[:, c * D:(c + 1) * D], w3l[ei][128 * c:128 * (c + 1), :]),
                wait=wl, tag=("mo_lw3", ei, c), sem="s_w3")

    emit_w12(0)
    emit_w3(0)

    def gv_section(ei, tsl, w1rdy):
        b, ts = divmod(tsl, 4)
        tp = tsl % 2
        if tsl >= 2:
            heprev = (("mo_w3", ei, tsl - 2, 3, 1),)
        elif ei >= 1:
            heprev = (("mo_w3", ei - 1, tsl + 6, 3, 1),)
        else:
            heprev = ()
        for fc in range(16):
            gu = (ei * 8 + tsl) * 16 + fc
            fp = fc % 2

            def mm1(e, b=b, ts=ts, fc=fc, fp=fp):
                for dc in range(8):
                    ins = e.matmul(pg[fp][:],
                                   w1_sb[:, dc * 2048 + fc * 128:dc * 2048 + fc * 128 + 128],
                                   hfs[:, dc * T + ts * 512:dc * T + ts * 512 + 512],
                                   start=(dc == 0), stop=(dc == 7))
                return ins
            plan.ev("tensor", mm1, wait=w1rdy + (("mo_hf", ei, b, 7),) +
                    ((("mo_sg", gu - 2),) if gu >= 2 else ()), tag=("mo_g1", gu))
            if glu:
                def mm2(e, b=b, ts=ts, fc=fc, fp=fp):
                    for dc in range(8):
                        ins = e.matmul(pv[fp][:],
                                       w2_sb[:, dc * 2048 + fc * 128:dc * 2048 + fc * 128 + 128],
                                       hfs[:, dc * T + ts * 512:dc * T + ts * 512 + 512],
                                       start=(dc == 0), stop=(dc == 7))
                    return ins
                plan.ev("tensor", mm2, wait=((("mo_he", gu - 2),) if gu >= 2 else ()),
                        tag=("mo_gv", ei, tsl, fc))
                plan.ev("scalar", lambda e, fp=fp, ei=ei, fc=fc: e.activation(
                    sg[fp][:], pg[fp][:], act,
                    bias=b1_sb[:, ei * 16 + fc:ei * 16 + fc + 1]),
                    wait=(("mo_g1", gu), ("mo_on",)) +
                         ((("mo_he", gu - 2),) if gu >= 2 else ()),
                    tag=("mo_sg", gu))
                plan.ev("vector", lambda e, tp=tp, fp=fp, fc=fc, ei=ei: e.scalar_tensor_tensor(
                    he[tp][:, fc * 512:(fc + 1) * 512], pv[fp][:],
                    b2_sb[:, ei * 16 + fc:ei * 16 + fc + 1], sg[fp][:],
                    ALU.add, ALU.mult),
                    wait=(("mo_gv", ei, tsl, fc), ("mo_sg", gu), ("mo_on",)) + heprev,
                    tag=("mo_he", gu))
            else:
                plan.ev("scalar", lambda e, tp=tp, fp=fp, fc=fc: e.activation(
                    he[tp][:, fc * 512:(fc + 1) * 512], pg[fp][:], act),
                    wait=(("mo_g1", gu),) + heprev, tag=("mo_sg", gu))

    def w3_section(ei, tsl, w3rdy):
        b, ts = divmod(tsl, 4)
        tp = tsl % 2
        hedone = [("mo_sg", (ei * 8 + tsl) * 16 + fc) for fc in range(16)] if not glu \
            else [("mo_he", (ei * 8 + tsl) * 16 + fc) for fc in range(16)]
        for tt4 in range(4):
            su = (ei * 8 + tsl) * 4 + tt4
            sp = su % 2
            k = b * 16 + ts * 4 + tt4
            for ds in range(2):
                def mm3(e, tp=tp, tt4=tt4, ds=ds, ei=ei):
                    withb3 = b1a is not None
                    for c in range(16):
                        ins = e.matmul(po[ds][:],
                                       he[tp][:, c * 512 + tt4 * 128:c * 512 + tt4 * 128 + 128],
                                       w3_sb[:, c * D + ds * 512:c * D + ds * 512 + 512],
                                       start=(c == 0), stop=(c == 15 and not withb3))
                    if withb3:
                        ins = e.matmul(po[ds][:], onesr[:],
                                       b3_sb[0:1, ds * 512:ds * 512 + 512],
                                       start=False, stop=True)
                    return ins
                b3w = (("mo_lb3", ei),) if b1a is not None else ()
                plan.ev("tensor", mm3, wait=tuple(hedone) + w3rdy + pre2 + b3w +
                        ((("mo_cb", su - 1, ds),) if su >= 1 else ()),
                        tag=("mo_w3", ei, tsl, tt4, ds))
                plan.ev("vector", lambda e, sp=sp, ds=ds, k=k, ei=ei, E=E: e.tensor_scalar_mul(
                    st4[sp][:, ds * 512:(ds + 1) * 512], po[ds][:],
                    comb[:, E * k + ei:E * k + ei + 1]),
                    wait=(("mo_w3", ei, tsl, tt4, ds), ("mo_cmb", k)) +
                         ((("mo_acc", su - 2),) if su >= 2 else ()),
                    tag=("mo_cb", su, ds))
            accw = acc_prev0 if ei == 0 else (("mo_acc", (ei - 1) * 32 + 30),
                                              ("mo_acc", (ei - 1) * 32 + 31))
            plan.ev("gpsimd", lambda e, sp=sp, b=b, ts=ts, tt4=tt4: e.dma_start(
                x_out[T * b + 128 * (ts * 4 + tt4):T * b + 128 * (ts * 4 + tt4 + 1), :],
                st4[sp][:], accum_op=ALU.add),
                wait=(("mo_cb", su, 0), ("mo_cb", su, 1)) + accw,
                tag=("mo_acc", su), sem=f"s_mac{su % 2}")

    for ei in range(E):
        w1rdy = (("mo_lw1", ei, 7),) + ((("mo_lw2", ei, 7),) if glu else ())
        w3rdy = (("mo_lw3", ei, 15),)
        for tsl in range(8):
            if tsl == 4:
                rw = (("mo_g1", (ei * 8 + 3) * 16 + 15),)
                if glu:
                    rw = rw + (("mo_gv", ei, 3, 15),)
                if ei == 0:
                    rw = rw + (("mo_gmm", 15),)
                reload_hf(ei, 1, rw)
                if gating_b1_pending[0]:
                    gating_b1_pending[0] = False
                    for k in range(16, NTILE):
                        emit_gating(k)
            gv_section(ei, tsl, w1rdy)
            if tsl == 7 and ei + 1 < E:
                emit_w12(ei + 1)
            if tsl >= 1:
                w3_section(ei, tsl - 1, w3rdy)
        w3_section(ei, 7, w3rdy)
        if ei + 1 < E:
            rw = (("mo_g1", (ei * 8 + 7) * 16 + 15),)
            if glu:
                rw = rw + (("mo_gv", ei, 7, 15),)
            if ei == 0:
                rw = rw + (("mo_gmm", 31),)
            reload_hf(ei + 1, 0, rw)
            emit_w3(ei + 1)
    return [("mo_acc", (E - 1) * 32 + 31)]


def emit_l1_qkv(nc, al, plan, h_tm, wq, wk, c1, s1, r1, half2, half2t, cc, qn_f, kn_f,
                hfp=None, mate=None):
    sb = al.sb
    ps = al.ps
    hf, hf_rdy = hfp if hfp is not None else load_hfm(nc, plan, sb, h_tm, name="l1_hf")
    wq_sb, wq_rdy = load_w_bf16(nc, plan, sb, wq, 8, 1024, "l1_wq")
    wk_sb, wk_rdy = load_w_bf16(nc, plan, sb, wk, 8, 1024, "l1_wk")
    r1_sb = sb("l1_r1", [128, 128], BF16)
    h2_sb = sb("l1_h2", [128, 2], BF16)
    h2t_sb = sb("l1_h2t", [2, 128], BF16)
    c1_sb = sb("l1_c1", [128, T], F32)
    s1_sb = sb("l1_s1", [128, T], F32)
    ccs = sb("l1_cc", [128, 4], F32)
    plan.ev("sync", lambda e: e.dma_start(r1_sb[:], r1), tag=("l1_r1",), sem="s_pre")
    plan.ev("sync", lambda e: e.dma_start(c1_sb[:], c1), tag=("l1_c1",), sem="s_pre")
    plan.ev("sync", lambda e: e.dma_start(s1_sb[:], s1), tag=("l1_s1",), sem="s_pre")
    plan.ev("sync", lambda e: e.dma_start(ccs[:], cc), tag=("l1_ccs",), sem="s_pre")
    plan.ev("gpsimd", lambda e: e.dma_start(h2_sb[:], half2), tag=("l1_h2",), sem="s_pre2")
    plan.ev("gpsimd", lambda e: e.dma_start(h2t_sb[:], half2t), tag=("l1_h2t",), sem="s_pre2")
    tA = [sb(f"l1_tA{i}", [128, 512], F32) for i in range(2)]
    tB = [sb(f"l1_tB{i}", [128, 512], F32) for i in range(2)]
    qraw = [sb(f"l1_qraw{i}", [128, 512], BF16) for i in range(2)]
    qr = [sb(f"l1_qr{i}", [128, 512], F32) for i in range(2)]
    sqq = [sb(f"l1_sqq{i}", [128, 512], BF16) for i in range(2)]
    sd2 = [sb(f"l1_sd2{i}", [2, 512], F32) for i in range(2)]
    rsf = [sb(f"l1_rsf{i}", [2, 512], F32) for i in range(2)]
    rs2 = [sb(f"l1_rs2{i}", [2, 512], BF16) for i in range(2)]
    qn = [sb(f"l1_qn{i}", [128, 512], BF16) for i in range(2)]
    ps_q = [ps(f"l1_psq{i}", [128, 512], F32) for i in range(2)]
    ps_r = [ps(f"l1_psr{i}", [128, 512], F32) for i in range(2)]
    ps_sbc = [ps(f"l1_pssbc{i}", [128, 512], F32) for i in range(2)]
    ps_ss = [t[0:2, :] for t in ps_sbc]
    ps_bc = ps_sbc

    units = []
    for pi, (w_sb, w_rdy, outt) in enumerate(((wq_sb, wq_rdy, qn_f), (wk_sb, wk_rdy, kn_f))):
        for b in range(BPC):
            for oc in range(8):
                for ts in range(4):
                    units.append((pi, w_sb, w_rdy, outt, b, oc, ts))
    NU = len(units)

    def part_a(u):
        (pi, w_sb, w_rdy, outt, b, oc, ts) = units[u]
        p = u % 2
        wprev = lambda t: ((t, u - 2),) if u >= 2 else ()

        def mmq(e, p=p, w_sb=w_sb, b=b, oc=oc, ts=ts):
            for dc in range(8):
                ins = e.matmul(ps_q[p][:], w_sb[:, dc * 1024 + oc * 128:dc * 1024 + oc * 128 + 128],
                               hf[b][:, dc * T + ts * 512:dc * T + ts * 512 + 512],
                               start=(dc == 0), stop=(dc == 7))
            return ins
        plan.ev("tensor", mmq, wait=(w_rdy[-1], hf_rdy[8 * b + 7]) + wprev("l1_tA")
                + wprev("l1_qcp"), tag=("l1_mmq", u))
        plan.ev("scalar", lambda e, p=p: e.copy(qraw[p][:], ps_q[p][:]),
                wait=(("l1_mmq", u),) + wprev("l1_mmr"), tag=("l1_qcp", u))

    def part_b(u):
        p = u % 2
        wprev = lambda t: ((t, u - 2),) if u >= 2 else ()
        plan.ev("tensor", lambda e, p=p: e.matmul(ps_r[p][:], r1_sb[:], qraw[p][:],
                                                  start=True, stop=True),
                wait=(("l1_qcp", u), ("l1_ccs",)) + wprev("l1_tB"), tag=("l1_mmr", u))

    def part_c(u):
        (pi, w_sb, w_rdy, outt, b, oc, ts) = units[u]
        p = u % 2
        wprev = lambda t: ((t, u - 2),) if u >= 2 else ()
        plan.ev("vector", lambda e, p=p, ts=ts: e.tensor_mul(
            tA[p][:], ps_q[p][:], c1_sb[:, 512 * ts:512 * (ts + 1)]),
            wait=(("l1_mmq", u), ("l1_qcp", u), ("l1_ccs",)) + wprev("l1_qr"),
            tag=("l1_tA", u))
        plan.ev("vector", lambda e, p=p, ts=ts: e.tensor_mul(
            tB[p][:], ps_r[p][:], s1_sb[:, 512 * ts:512 * (ts + 1)]),
            wait=(("l1_mmr", u), ("l1_ccs",)) + wprev("l1_qr"), tag=("l1_tB", u))
        plan.ev("vector", lambda e, p=p: e.tensor_add(qr[p][:], tA[p][:], tB[p][:]),
                wait=(("l1_tA", u), ("l1_tB", u)) + wprev("l1_qn"), tag=("l1_qr", u))
        plan.ev("scalar", lambda e, p=p: e.activation(sqq[p][:], qr[p][:], AF.Square),
                wait=(("l1_qr", u),) + wprev("l1_mss"), tag=("l1_sq", u))

    def part_d(u):
        p = u % 2
        wprev = lambda t: ((t, u - 2),) if u >= 2 else ()
        plan.ev("tensor", lambda e, p=p: e.matmul(ps_ss[p][:], h2_sb[:], sqq[p][:],
                                                  start=True, stop=True),
                wait=(("l1_sq", u), ("l1_h2t",)) + wprev("l1_qn"), tag=("l1_mss", u))
        plan.ev("scalar", lambda e, p=p: e.activation(sd2[p][:], ps_ss[p][:], AF.Ln,
                                                      bias=ccs[0:2, 1:2]),
                wait=(("l1_mss", u), ("l1_ccs",)) + wprev("l1_rcc"), tag=("l1_sd", u))
        plan.ev("scalar", lambda e, p=p: e.activation(rs2[p][:], sd2[p][:], AF.Exp,
                                                      scale=-0.5),
                wait=(("l1_sd", u),) + wprev("l1_mbc"), tag=("l1_rcc", u))

    def part_e(u):
        (pi, w_sb, w_rdy, outt, b, oc, ts) = units[u]
        p = u % 2
        wprev = lambda t: ((t, u - 2),) if u >= 2 else ()
        plan.ev("tensor", lambda e, p=p: e.matmul(ps_bc[p][:], h2t_sb[:], rs2[p][:],
                                                  start=True, stop=True),
                wait=(("l1_rcc", u), ("l1_h2t",)) + wprev("l1_qn"), tag=("l1_mbc", u))
        plan.ev("vector", lambda e, p=p: e.tensor_mul(qn[p][:], qr[p][:], ps_bc[p][:]),
                wait=(("l1_qr", u), ("l1_mbc", u)) + wprev("l1_st"), tag=("l1_qn", u))
        plan.ev("sync", lambda e, p=p, outt=outt, b=b, oc=oc, ts=ts: e.dma_start(
            outt[oc, :, T * b + 512 * ts:T * b + 512 * (ts + 1)], qn[p][:]),
            wait=(("l1_qn", u),), tag=("l1_st", u), sem=f"s_o{p}")

    for u in range(NU + 3):
        if u < NU:
            part_a(u)
        if 1 <= u <= NU:
            part_b(u - 1)
        if 3 <= u <= NU + 2:
            part_e(u - 3)
        if 2 <= u <= NU + 1:
            part_d(u - 2)
        if 1 <= u <= NU:
            part_c(u - 1)
        if mate is not None:
            mate(u)
    return [("l1_st", u) for u in range(len(units))]


def emit_l1_attn(nc, al, plan, qn_f, kn_f, v1_tm, wo, bo, onesrow, dsel_a, dsel_b,
                 x_in, x_out):
    sb = al.sb
    ps = al.ps
    woA = sb("at_woA", [64, 8 * 1024], BF16)
    woB = sb("at_woB", [64, 8 * 1024], BF16)
    for c in range(8):
        plan.ev("gpsimd", lambda e, c=c: e.dma_start(
            woA[:, c * 1024:(c + 1) * 1024], wo[128 * c:128 * c + 64, :]),
            tag=("at_woA", c), sem="s_pre")
        plan.ev("gpsimd", lambda e, c=c: e.dma_start(
            woB[:, c * 1024:(c + 1) * 1024], wo[128 * c + 64:128 * (c + 1), :]),
            tag=("at_woB", c), sem="s_pre")
    wo_rdy = (("at_woA", 7), ("at_woB", 7))
    bo_sb = sb("at_bo", [1, D], BF16)
    onesr = sb("at_ones", [1, 128], BF16)
    dsa = sb("at_dsa", [128, 128], BF16)
    for nm, dst, src in (("at_bo", bo_sb, bo), ("at_on", onesr, onesrow),
                         ("at_da", dsa, dsel_a)):
        plan.ev("gpsimd", lambda e, dst=dst, src=src: e.dma_start(dst[:], src),
                tag=(nm,), sem="s_pre2")
    pre2 = (("at_da",),)
    KnP = [[sb(f"at_Kn{i}_{hf}", [64, T], BF16) for hf in range(2)] for i in range(2)]
    QnP = [[sb(f"at_Qn{i}_{hf}", [64, T], BF16) for hf in range(2)] for i in range(2)]
    V1P = [sb(f"at_V1{i}", [128, 16 * 130], BF16) for i in range(2)]
    onesf = sb("at_onesf", [1, 128], F32)
    plan.ev("gpsimd", lambda e: e.dma_start(onesf[:], onesrow), tag=("at_onf",), sem="s_pre2")
    for i in range(2):
        plan.ev("gpsimd", lambda e, i=i: e.memset(V1P[i][:], 1.0), tag=("at_vms", i))
    PA = [sb(f"at_PA{i}", [128, 8 * 512], BF16) for i in range(2)]
    PB = [sb(f"at_PB{i}", [128, 8 * 512], BF16) for i in range(2)]
    rdA = sb("at_rdA", [1, 512], F32)
    rdB = sb("at_rdB", [1, 512], F32)
    rbA = sb("at_rbA", [64, 512], F32)
    rbB = sb("at_rbB", [64, 512], F32)
    pvsA = [sb(f"at_pvsA{i}", [128, 512], F32) for i in range(2)]
    pvsB = [sb(f"at_pvsB{i}", [128, 512], F32) for i in range(2)]
    attnA = sb("at_attnA", [64, 8 * T], BF16)
    attnB = sb("at_attnB", [64, 8 * T], BF16)
    xt = [sb(f"at_xt{i}", [128, D], F32) for i in range(2)]
    psA = [ps(f"at_psA{i}", [128, 512], F32) for i in range(2)]
    psB = [ps(f"at_psB{i}", [128, 512], F32) for i in range(2)]
    ps_pvA = ps("at_pspvA", [128, 512], F32)
    ps_pvB = ps("at_pspvB", [128, 512], F32)
    ps_bcA = ps("at_psbcA", [128, 512], F32)
    ps_bcB = ps("at_psbcB", [128, 512], F32)

    done_t1 = set()
    done_t2 = set()

    def emit_tail1(iu):
        if iu in done_t1 or iu < 0:
            return
        done_t1.add(iu)
        bq, rem = divmod(iu, 32)
        prq, tsq = divmod(rem, 4)
        ppq = prq % 2
        ipq = iu % 2
        for hh in range(2):
            pdone = tuple(("at_eA", iu, 8 * hh + sl) for sl in range(8)) + \
                    tuple(("at_eB", iu, 8 * hh + sl) for sl in range(8))

            def mmpv(e, pp=ppq, hh=hh):
                for sl in range(8):
                    sc = 8 * hh + sl
                    e.matmul(ps_pvA[0:65, :],
                             V1P[pp][:, 130 * sc:130 * sc + 65],
                             PA[hh][:, 512 * sl:512 * (sl + 1)],
                             start=(sc == 0), stop=(sc == 15))
                    ins = e.matmul(ps_pvB[0:65, :],
                                   V1P[pp][:, 130 * sc + 65:130 * (sc + 1)],
                                   PB[hh][:, 512 * sl:512 * (sl + 1)],
                                   start=(sc == 0), stop=(sc == 15))
                return ins
            plan.ev("tensor", mmpv, wait=pdone +
                    ((("at_cpA", iu - 1), ("at_cpB", iu - 1)) if (iu and hh == 0) else ()),
                    tag=("at_pv", iu, hh))
        plan.ev("vector", lambda e, ip=ipq: e.tensor_copy(pvsA[ip][0:65, :], ps_pvA[0:65, :]),
                wait=(("at_pv", iu, 1),) + ((("at_oA", iu - 2),) if iu >= 2 else ()),
                tag=("at_cpA", iu))
        plan.ev("vector", lambda e, ip=ipq: e.tensor_copy(pvsB[ip][0:65, :], ps_pvB[0:65, :]),
                wait=(("at_pv", iu, 1),) + ((("at_oB", iu - 2),) if iu >= 2 else ()),
                tag=("at_cpB", iu))

    def emit_tail2(iu):
        if iu in done_t2 or iu < 0:
            return
        done_t2.add(iu)
        bq, rem = divmod(iu, 32)
        prq, tsq = divmod(rem, 4)
        ipq = iu % 2
        plan.ev("vector", lambda e, ip=ipq: e.reciprocal(rdA[:], pvsA[ip][64:65, :]),
                wait=(("at_cpA", iu),) + ((("at_mbc", iu - 1, 0),) if iu else ()),
                tag=("at_rdA", iu))
        plan.ev("vector", lambda e, ip=ipq: e.reciprocal(rdB[:], pvsB[ip][64:65, :]),
                wait=(("at_cpB", iu),) + ((("at_mbc", iu - 1, 1),) if iu else ()),
                tag=("at_rdB", iu))
        plan.ev("tensor", lambda e: e.matmul(ps_bcA[0:64, :], onesf[0:1, 0:64], rdA[:],
                                             start=True, stop=True),
                wait=(("at_rdA", iu), ("at_onf",)) + ((("at_cbc", iu - 1, 0),) if iu else ()),
                tag=("at_mbc", iu, 0))
        plan.ev("tensor", lambda e: e.matmul(ps_bcB[0:64, :], onesf[0:1, 0:64], rdB[:],
                                             start=True, stop=True),
                wait=(("at_rdB", iu), ("at_onf",)) + ((("at_cbc", iu - 1, 1),) if iu else ()),
                tag=("at_mbc", iu, 1))
        plan.ev("vector", lambda e: e.tensor_copy(rbA[:], ps_bcA[0:64, :]),
                wait=(("at_mbc", iu, 0),) + ((("at_oA", iu - 1),) if iu else ()),
                tag=("at_cbc", iu, 0))
        plan.ev("vector", lambda e: e.tensor_copy(rbB[:], ps_bcB[0:64, :]),
                wait=(("at_mbc", iu, 1),) + ((("at_oB", iu - 1),) if iu else ()),
                tag=("at_cbc", iu, 1))
        bwo = (("at_mo", bq * 16 - 1, 1),) if (bq and rem == 0) else ()
        plan.ev("vector", lambda e, pr=prq, ts=tsq, ip=ipq: e.tensor_mul(
            attnA[:, pr * T + 512 * ts:pr * T + 512 * (ts + 1)],
            pvsA[ip][0:64, :], rbA[:]),
            wait=(("at_cbc", iu, 0), ("at_cpA", iu)) + bwo, tag=("at_oA", iu))
        plan.ev("vector", lambda e, pr=prq, ts=tsq, ip=ipq: e.tensor_mul(
            attnB[:, pr * T + 512 * ts:pr * T + 512 * (ts + 1)],
            pvsB[ip][0:64, :], rbB[:]),
            wait=(("at_cbc", iu, 1), ("at_cpB", iu)) + bwo, tag=("at_oB", iu))

    for b in range(BPC):
        bw = (("at_mo", b * 16 - 1, 1),) if b else ()
        for pr in range(8):
            pp = pr % 2
            if pr >= 2 or b:
                prev = (b * 8 + pr - 2) * 4 + 3 if pr >= 2 else ((b - 1) * 8 + pr + 6) * 4 + 3
                bwp = (("at_sA", prev, 15), ("at_sB", prev, 15))
            else:
                bwp = ()
            bvp = (("at_pv", ((b * 8 + pr - 2) * 4 + 3) if pr >= 2 else
                    (((b - 1) * 8 + pr + 6) * 4 + 3), 1),) if (pr >= 2 or b) else ()
            for hf2 in range(2):
                plan.ev("sync", lambda e, b=b, pr=pr, pp=pp, hf2=hf2: e.dma_start(
                    V1P[pp][:].rearrange("p (n c) -> p n c", c=130)[:, :, 65 * hf2:65 * hf2 + 64],
                    v1_tm[T * b:T * (b + 1), 128 * pr + 64 * hf2:128 * pr + 64 * (hf2 + 1)]
                    .rearrange("(n p) q -> p n q", p=128)),
                    wait=bvp + bw + (("at_vms", pp),), tag=("at_lv", b, pr, hf2),
                    sem=f"s_v1{pp}{hf2}")
            for hf2 in range(2):
                plan.ev("sync", lambda e, b=b, pr=pr, pp=pp, hf2=hf2: e.dma_start(
                    KnP[pp][hf2][:],
                    kn_f[pr, 64 * hf2:64 * (hf2 + 1), T * b:T * (b + 1)]),
                    wait=bwp + bw, tag=("at_lk", b, pr, hf2), sem=f"s_kn{pp}{hf2}")
                plan.ev("sync", lambda e, b=b, pr=pr, pp=pp, hf2=hf2: e.dma_start(
                    QnP[pp][hf2][:],
                    qn_f[pr, 64 * hf2:64 * (hf2 + 1), T * b:T * (b + 1)]),
                    wait=bwp + bw, tag=("at_lq", b, pr, hf2), sem=f"s_qn{pp}{hf2}")
            lds = (("at_lk", b, pr, 0), ("at_lq", b, pr, 0),
                   ("at_lk", b, pr, 1), ("at_lq", b, pr, 1),
                   ("at_lv", b, pr, 0), ("at_lv", b, pr, 1))
            for ts in range(4):
                iu = (b * 8 + pr) * 4 + ts

                def emit_sc(sc, iu=iu, pp=pp, ts=ts):
                    gsc = iu * 16 + sc
                    p = gsc % 2
                    wA = (("at_eA", (gsc - 2) // 16, (gsc - 2) % 16),) if gsc >= 2 else ()
                    wB = (("at_eB", (gsc - 2) // 16, (gsc - 2) % 16),) if gsc >= 2 else ()
                    plan.ev("tensor", lambda e, p=p, pp=pp, ts=ts, sc=sc: e.matmul(
                        psA[p][:], KnP[pp][0][:, 128 * sc:128 * (sc + 1)],
                        QnP[pp][0][:, 512 * ts:512 * (ts + 1)],
                        start=True, stop=True),
                        wait=lds + wA, tag=("at_sA", iu, sc))
                    plan.ev("tensor", lambda e, p=p, pp=pp, ts=ts, sc=sc: e.matmul(
                        psB[p][:], KnP[pp][1][:, 128 * sc:128 * (sc + 1)],
                        QnP[pp][1][:, 512 * ts:512 * (ts + 1)],
                        start=True, stop=True),
                        wait=lds + wB, tag=("at_sB", iu, sc))

                def emit_exp(sc, iu=iu):
                    p = (iu * 16 + sc) % 2
                    plan.ev("scalar", lambda e, p=p, hh=sc // 8, sl=sc % 8: e.activation(
                        PA[hh][:, 512 * sl:512 * (sl + 1)], psA[p][:], AF.Exp, scale=0.125),
                        wait=(("at_sA", iu, sc),) +
                             ((("at_pv", iu - 1, sc // 8),) if iu >= 1 else ()),
                        tag=("at_eA", iu, sc))
                    plan.ev("scalar", lambda e, p=p, hh=sc // 8, sl=sc % 8: e.activation(
                        PB[hh][:, 512 * sl:512 * (sl + 1)], psB[p][:], AF.Exp, scale=0.125),
                        wait=(("at_sB", iu, sc),) +
                             ((("at_pv", iu - 1, sc // 8),) if iu >= 1 else ()),
                        tag=("at_eB", iu, sc))

                emit_sc(0)
                emit_sc(1)
                emit_tail1(iu - 1)
                emit_tail2(iu - 2)
                for sc in range(16):
                    emit_exp(sc)
                    if sc + 2 < 16:
                        emit_sc(sc + 2)

        emit_tail1((b * 8 + 7) * 4 + 3)
        emit_tail2((b * 8 + 7) * 4 + 2)
        emit_tail2((b * 8 + 7) * 4 + 3)
        adone = tuple(("at_oA", (b * 8 + pr) * 4 + ts) for pr in range(8) for ts in range(4)) +                 tuple(("at_oB", (b * 8 + pr) * 4 + ts) for pr in range(8) for ts in range(4))
        for tt in range(16):
            u = b * 16 + tt
            p = u % 2
            plan.ev("sync", lambda e, b=b, tt=tt, p=p: e.dma_start(
                xt[p][:], x_in[T * b + 128 * tt:T * b + 128 * (tt + 1), :]),
                wait=(("at_add", u - 2, 1),) if u >= 2 else (),
                tag=("at_xt", u), sem=f"s_x{p}")
            for ds in range(2):
                def mmo(e, b=b, tt=tt, ds=ds):
                    for c in range(8):
                        e.matmul(psA[ds][:], attnA[:, c * T + tt * 128:c * T + tt * 128 + 128],
                                 woA[:, c * 1024 + ds * 512:c * 1024 + ds * 512 + 512],
                                 start=(c == 0), stop=False)
                        e.matmul(psA[ds][:], attnB[:, c * T + tt * 128:c * T + tt * 128 + 128],
                                 woB[:, c * 1024 + ds * 512:c * 1024 + ds * 512 + 512],
                                 start=False, stop=False)
                    return e.matmul(psA[ds][:], onesr[:],
                                    bo_sb[0:1, ds * 512:ds * 512 + 512],
                                    start=False, stop=True)
                plan.ev("tensor", mmo, wait=adone + wo_rdy + (("at_da",),) +
                        ((("at_add", u - 1, ds),) if u >= 1 else ()),
                        tag=("at_mo", u, ds))
                plan.ev("vector", lambda e, ds=ds, p=p: e.tensor_add(
                    xt[p][:, 512 * ds:512 * (ds + 1)], psA[ds][:],
                    xt[p][:, 512 * ds:512 * (ds + 1)]),
                    wait=(("at_mo", u, ds), ("at_xt", u)) +
                         ((("at_st", u - 2),) if u >= 2 else ()),
                    tag=("at_add", u, ds))
            plan.ev("sync", lambda e, b=b, tt=tt, p=p: e.dma_start(
                x_out[T * b + 128 * tt:T * b + 128 * (tt + 1), :], xt[p][:]),
                wait=(("at_add", u, 0), ("at_add", u, 1)),
                tag=("at_st", u), sem=f"s_xo{p}")
    return [("at_st", u) for u in range(BPC * 16)]


def build_ln(copy_x):
    nc = bass.Bass("TRN2", target_bir_lowering=False)
    x = nc.dram_tensor("x", [TC, D], F32, kind="ExternalInput").ap()
    cc = nc.dram_tensor("cc", [128, 4], F32, kind="ExternalInput").ap()
    h = nc.dram_tensor("h", [TC, D], BF16, kind="ExternalOutput").ap()
    xout = nc.dram_tensor("xout", [TC, D], F32, kind="ExternalOutput").ap() if copy_x else None
    es = contextlib.ExitStack()
    al = Alloc(nc, es)
    plan = Plan()
    out = emit_ln(nc, al, plan, x, h, cc, xout)
    plan.barrier("sync", wait=tuple(out))
    plan.emit(nc)
    return nc


def build_l0_qkv():
    nc = bass.Bass("TRN2", target_bir_lowering=False)
    ap = lambda n, shp, dt, k: nc.dram_tensor(n, shp, dt, kind=k).ap()
    h = ap("h", [TC, D], BF16, "ExternalInput")
    wq = ap("wq", [D, D], F32, "ExternalInput")
    wk = ap("wk", [D, D], F32, "ExternalInput")
    cq = ap("cq", [D, T], F32, "ExternalInput")
    sq = ap("sq", [D, T], F32, "ExternalInput")
    ck = ap("ck", [D, T], F32, "ExternalInput")
    sk = ap("sk", [D, T], F32, "ExternalInput")
    r0 = ap("r0", [128, 128], BF16, "ExternalInput")
    qf = ap("qf", [8, 128, TC], BF16, "ExternalOutput")
    kf = ap("kf", [8, 128, TC], BF16, "ExternalOutput")
    es = contextlib.ExitStack()
    al = Alloc(nc, es)
    plan = Plan()
    out = emit_l0_qkv(nc, al, plan, h, wq, wk, cq, sq, ck, sk, r0, qf, kf)
    plan.barrier("sync", wait=tuple(out))
    plan.emit(nc)
    return nc


def build_l0_vg():
    nc = bass.Bass("TRN2", target_bir_lowering=False)
    ap = lambda n, shp, dt, k: nc.dram_tensor(n, shp, dt, kind=k).ap()
    h = ap("h", [TC, D], BF16, "ExternalInput")
    wv = ap("wv", [D, 2048], F32, "ExternalInput")
    wg = ap("wg", [D, 2048], F32, "ExternalInput")
    v = ap("v", [TC, 2048], BF16, "ExternalOutput")
    gate = ap("gate", [TC, 2048], BF16, "ExternalOutput")
    es = contextlib.ExitStack()
    al = Alloc(nc, es)
    plan = Plan()
    vg_unit, nvg, out = emit_l0_vg(nc, al, plan, h, [(wv, 2048, v, None), (wg, 2048, gate, AF.Silu)])
    for u in range(nvg):
        vg_unit(u)
    plan.barrier("sync", wait=tuple(out))
    plan.emit(nc)
    return nc


def build_l0_ret():
    nc = bass.Bass("TRN2", target_bir_lowering=False)
    ap = lambda n, shp, dt, k: nc.dram_tensor(n, shp, dt, kind=k).ap()
    qf = ap("qf", [8, 128, TC], BF16, "ExternalInput")
    kf = ap("kf", [8, 128, TC], BF16, "ExternalInput")
    v = ap("v", [TC, 2048], BF16, "ExternalInput")
    gate = ap("gate", [TC, 2048], BF16, "ExternalInput")
    dcol = ap("dcol", [128, 8], F32, "ExternalInput")
    eh = ap("eh", [128, 128], F32, "ExternalInput")
    rowi = ap("rowi", [128, 128], F32, "ExternalInput")
    colj = ap("colj", [128, 1], F32, "ExternalInput")
    cc = ap("cc", [128, 4], F32, "ExternalInput")
    gg = ap("gg", [TC, 2048], BF16, "ExternalOutput")
    es = contextlib.ExitStack()
    al = Alloc(nc, es)
    plan = Plan()
    out = emit_l0_ret(nc, al, plan, qf, kf, v, gate, dcol, eh, rowi, colj, cc, gg)
    plan.barrier("sync", wait=tuple(out))
    plan.emit(nc)
    return nc


def build_l0_proj():
    nc = bass.Bass("TRN2", target_bir_lowering=False)
    ap = lambda n, shp, dt, k: nc.dram_tensor(n, shp, dt, kind=k).ap()
    gg = ap("gg", [TC, 2048], BF16, "ExternalInput")
    w = ap("w", [2048, D], F32, "ExternalInput")
    xin = ap("xin", [TC, D], F32, "ExternalInput")
    xout = ap("xout", [TC, D], F32, "ExternalOutput")
    es = contextlib.ExitStack()
    al = Alloc(nc, es)
    plan = Plan()
    out = emit_proj_res(nc, al, plan, gg, w, xin, xout)
    plan.barrier("sync", wait=tuple(out))
    plan.emit(nc)
    return nc


def build_moe(layer):
    nc = bass.Bass("TRN2", target_bir_lowering=False)
    ap = lambda n, shp, dt, k: nc.dram_tensor(n, shp, dt, kind=k).ap()
    h = ap("h", [TC, D], BF16, "ExternalInput")
    wg = ap("wgate", [D, 8], F32, "ExternalInput")
    E = 8 if layer == 0 else 9
    w1l = [ap(f"w1_{e}", [D, 2048], F32, "ExternalInput") for e in range(E)]
    w3l = [ap(f"w3_{e}", [2048, D], F32, "ExternalInput") for e in range(E)]
    xin = ap("xin", [TC, D], F32, "ExternalInput")
    xout = ap("xout", [TC, D], F32, "ExternalOutput")
    es = contextlib.ExitStack()
    al = Alloc(nc, es)
    plan = Plan()
    if layer == 0:
        w2l = [ap(f"w2_{e}", [D, 2048], F32, "ExternalInput") for e in range(E)]
        b1a = ap("b1a", [128, E * 16], F32, "ExternalInput")
        b2a = ap("b2a", [128, E * 16], F32, "ExternalInput")
        b3a = ap("b3a", [E, D], F32, "ExternalInput")
        onesrow = ap("onesrow", [1, 128], F32, "ExternalInput")
        out = emit_moe(nc, al, plan, h, wg, w1l, w2l, w3l, b1a, b2a, b3a, onesrow,
                       AF.Silu, xin, xout, use_shared=False)
    else:
        out = emit_moe(nc, al, plan, h, wg, w1l, None, w3l, None, None, None, None,
                       AF.Gelu, xin, xout, use_shared=True)
    plan.barrier("gpsimd", wait=tuple(out))
    plan.emit(nc)
    return nc


def build_l1_qkv():
    nc = bass.Bass("TRN2", target_bir_lowering=False)
    ap = lambda n, shp, dt, k: nc.dram_tensor(n, shp, dt, kind=k).ap()
    h = ap("h", [TC, D], BF16, "ExternalInput")
    wq = ap("wq", [D, D], F32, "ExternalInput")
    wk = ap("wk", [D, D], F32, "ExternalInput")
    c1 = ap("c1", [128, T], F32, "ExternalInput")
    s1 = ap("s1", [128, T], F32, "ExternalInput")
    r1 = ap("r1", [128, 128], BF16, "ExternalInput")
    h2 = ap("h2", [128, 2], F32, "ExternalInput")
    h2t = ap("h2t", [2, 128], F32, "ExternalInput")
    cc = ap("cc", [128, 4], F32, "ExternalInput")
    qf = ap("qf", [8, 128, TC], BF16, "ExternalOutput")
    kf = ap("kf", [8, 128, TC], BF16, "ExternalOutput")
    es = contextlib.ExitStack()
    al = Alloc(nc, es)
    plan = Plan()
    out = emit_l1_qkv(nc, al, plan, h, wq, wk, c1, s1, r1, h2, h2t, cc, qf, kf)
    plan.barrier("sync", wait=tuple(out))
    plan.emit(nc)
    return nc


def build_l1_v():
    nc = bass.Bass("TRN2", target_bir_lowering=False)
    ap = lambda n, shp, dt, k: nc.dram_tensor(n, shp, dt, kind=k).ap()
    h = ap("h", [TC, D], BF16, "ExternalInput")
    wv = ap("wv", [D, D], F32, "ExternalInput")
    v = ap("v", [TC, D], BF16, "ExternalOutput")
    es = contextlib.ExitStack()
    al = Alloc(nc, es)
    plan = Plan()
    vg_unit, nvg, out = emit_l0_vg(nc, al, plan, h, [(wv, 1024, v, None)])
    for u in range(nvg):
        vg_unit(u)
    plan.barrier("sync", wait=tuple(out))
    plan.emit(nc)
    return nc


def build_l1_attn():
    nc = bass.Bass("TRN2", target_bir_lowering=False)
    ap = lambda n, shp, dt, k: nc.dram_tensor(n, shp, dt, kind=k).ap()
    qf = ap("qf", [8, 128, TC], BF16, "ExternalInput")
    kf = ap("kf", [8, 128, TC], BF16, "ExternalInput")
    v = ap("v", [TC, D], BF16, "ExternalInput")
    wo = ap("wo", [D, D], F32, "ExternalInput")
    bo = ap("bo", [1, D], F32, "ExternalInput")
    onesrow = ap("onesrow", [1, 128], F32, "ExternalInput")
    dsa = ap("dsa", [128, 128], F32, "ExternalInput")
    dsb = ap("dsb", [128, 128], F32, "ExternalInput")
    xin = ap("xin", [TC, D], F32, "ExternalInput")
    xout = ap("xout", [TC, D], F32, "ExternalOutput")
    es = contextlib.ExitStack()
    al = Alloc(nc, es)
    plan = Plan()
    out = emit_l1_attn(nc, al, plan, qf, kf, v, wo, bo, onesrow, dsa, dsb, xin, xout)
    plan.barrier("sync", wait=tuple(out))
    plan.emit(nc)
    return nc


def build_fused():
    nc = bass.Bass("TRN2", target_bir_lowering=False)
    ap = lambda n, shp, dt, k: nc.dram_tensor(n, shp, dt, kind=k).ap()
    I = lambda n, shp, dt: ap(n, shp, dt, "ExternalInput")
    N = lambda n, shp, dt: ap(n, shp, dt, "Internal")
    x_in = I("x", [TC, D], F32)
    cc = I("cc", [128, 4], F32)
    cq, sq, ck, sk = (I(n, [D, T], F32) for n in ("cq", "sq", "ck", "sk"))
    r0 = I("r0", [128, 128], BF16)
    r1 = I("r1", [128, 128], BF16)
    c1 = I("c1", [128, T], F32)
    s1 = I("s1", [128, T], F32)
    h2 = I("h2", [128, 2], F32)
    h2t = I("h2t", [2, 128], F32)
    onesrow = I("onesrow", [1, 128], F32)
    dsa = I("dsa", [128, 128], F32)
    dsb = I("dsb", [128, 128], F32)
    eh = I("eh", [128, 128], F32)
    rowi = I("rowi", [128, 128], F32)
    colj = I("colj", [128, 1], F32)
    dcol = I("dcol", [128, 8], F32)
    l0_wq = I("l0_wq", [D, D], F32)
    l0_wk = I("l0_wk", [D, D], F32)
    l0_wv = I("l0_wv", [D, 2048], F32)
    l0_wg = I("l0_wg", [D, 2048], F32)
    l0_wp = I("l0_wp", [2048, D], F32)
    wg0 = I("wgate0", [D, 8], F32)
    m0w1 = [I(f"m0w1_{e}", [D, 2048], F32) for e in range(8)]
    m0w2 = [I(f"m0w2_{e}", [D, 2048], F32) for e in range(8)]
    m0w3 = [I(f"m0w3_{e}", [2048, D], F32) for e in range(8)]
    b1a = I("b1a", [128, 128], F32)
    b2a = I("b2a", [128, 128], F32)
    b3a = I("b3a", [8, D], F32)
    l1_wq = I("l1_wq", [D, D], F32)
    l1_wk = I("l1_wk", [D, D], F32)
    l1_wv = I("l1_wv", [D, D], F32)
    l1_wo = I("l1_wo", [D, D], F32)
    l1_bo = I("l1_bo", [1, D], F32)
    wg1 = I("wgate1", [D, 8], F32)
    m1w1 = [I(f"m1w1_{e}", [D, 2048], F32) for e in range(9)]
    m1w3 = [I(f"m1w3_{e}", [2048, D], F32) for e in range(9)]
    out = ap("out", [TC, D], F32, "ExternalOutput")
    h = N("h", [TC, D], BF16)
    qf = N("qf", [8, 128, TC], BF16)
    kf = N("kf", [8, 128, TC], BF16)
    v0 = N("v0", [TC, 2048], BF16)
    gate = N("gate", [TC, 2048], BF16)
    gg = N("gg", [TC, 2048], BF16)
    qf1 = N("qf1", [8, 128, TC], BF16)
    kf1 = N("kf1", [8, 128, TC], BF16)
    v1 = N("v1", [TC, D], BF16)

    sems = {}
    bases = {}
    entry = [{}]

    snum = [0]

    def stage(emitfn):
        snum[0] += 1
        with contextlib.ExitStack() as es:
            al = Alloc(nc, es)
            plan = Plan(gsfx=f"_{snum[0]}")
            emitfn(al, plan)
            plan.emit(nc, sems, bases, entry=dict(entry[0]))
        entry[0] = dict(bases)

    def g1(al, plan):
        emit_ln(nc, al, plan, x_in, h, cc, out)
        hfp = load_hfm(nc, plan, al.sb, h,
                       hw=lambda b: (("ln_st", 16 * b + 14), ("ln_st", 16 * b + 15)))
        vg_unit, nvg, _ = emit_l0_vg(nc, al, plan, h,
                                     [(l0_wv, 2048, v0, None),
                                      (l0_wg, 2048, gate, AF.Silu)], hfp=hfp,
                                     defer=True)
        done = [0]

        def mate(u):
            while done[0] < nvg and done[0] < 2 * (u + 1):
                vg_unit(done[0])
                done[0] += 1
        emit_l0_qkv(nc, al, plan, h, l0_wq, l0_wk, cq, sq, ck, sk, r0, qf, kf, hfp=hfp,
                    mate=mate)
        while done[0] < nvg:
            vg_unit(done[0])
            done[0] += 1
    stage(g1)

    def g2(al, plan):
        emit_l0_ret(nc, al, plan, qf, kf, v0, gate, dcol, eh, rowi, colj, cc, gg)
        emit_proj_res(nc, al, plan, gg, l0_wp, out, out,
                      afw=lambda b: (("rt_gst", 16 * b + 14), ("rt_gst", 16 * b + 15)))
        emit_ln(nc, al, plan, out, h, cc, xw=lambda k: (("pj_st", k),))
    stage(g2)

    stage(lambda al, plan: emit_moe(nc, al, plan, h, wg0, m0w1, m0w2, m0w3, b1a, b2a,
                                    b3a, onesrow, AF.Silu, None, out, use_shared=False))

    def g4(al, plan):
        emit_ln(nc, al, plan, out, h, cc)
        hfp = load_hfm(nc, plan, al.sb, h,
                       hw=lambda b: (("ln_st", 16 * b + 14), ("ln_st", 16 * b + 15)))
        vg_unit, nvg, _ = emit_l0_vg(nc, al, plan, h, [(l1_wv, 1024, v1, None)],
                                     hfp=hfp, defer=True)
        done = [0]

        def mate(u):
            while done[0] < nvg and done[0] < (u + 1) // 2:
                vg_unit(done[0])
                done[0] += 1
        emit_l1_qkv(nc, al, plan, h, l1_wq, l1_wk, c1, s1, r1, h2, h2t, cc, qf1, kf1,
                    hfp=hfp, mate=mate)
        while done[0] < nvg:
            vg_unit(done[0])
            done[0] += 1
    stage(g4)

    def g5(al, plan):
        emit_l1_attn(nc, al, plan, qf1, kf1, v1, l1_wo, l1_bo, onesrow, dsa, dsb, out, out)
        emit_ln(nc, al, plan, out, h, cc, xw=lambda k: (("at_st", k),))
    stage(g5)

    stage(lambda al, plan: emit_moe(nc, al, plan, h, wg1, m1w1, None, m1w3, None, None, None,
                                    None, AF.Gelu, None, out, use_shared=True))
    fin = Plan()
    fin.barrier("sync")
    fin.emit(nc, sems, bases, entry=dict(bases))
    return nc


def kernel_fused(inputs):
    sh = _prep_shared(inputs)
    ht = sh["ht"]
    x_full = np.ascontiguousarray(np.asarray(inputs["L0_attn_hidden_states"],
                                             dtype=np.float32).reshape(B * T, D))
    nc = build_fused()
    base = {"cc": ht["CONSTC"], "cq": ht["CQ0"], "sq": ht["SQ0"], "ck": ht["CK0"],
            "sk": ht["SK0"], "r0": ht["R0"], "r1": ht["R1"], "c1": ht["C1"],
            "s1": ht["S1"], "h2": ht["HALF2"], "h2t": ht["HALF2T"],
            "onesrow": ht["ONESROW"], "dsa": ht["DENSEL_A"], "dsb": ht["DENSEL_B"],
            "eh": ht["EH"], "rowi": ht["ROWI"], "colj": ht["COLJ"],
            "dcol": sh["decay_col"],
            "l0_wq": sh["l0_wq"], "l0_wk": sh["l0_wk"], "l0_wv": sh["l0_wv"],
            "l0_wg": sh["l0_wg"], "l0_wp": sh["l0_wp"], "wgate0": sh["m0_wgate"],
            "b1a": sh["m0_b1a"], "b2a": sh["m0_b2a"], "b3a": sh["m0_b3a"],
            "l1_wq": sh["l1_wq"], "l1_wk": sh["l1_wk"], "l1_wv": sh["l1_wv"],
            "l1_wo": sh["l1_wo"], "l1_bo": sh["l1_bo"], "wgate1": sh["m1_wgate"]}
    for e in range(8):
        base[f"m0w1_{e}"] = sh["m0_w1"][e]
        base[f"m0w2_{e}"] = sh["m0_w2"][e]
        base[f"m0w3_{e}"] = sh["m0_w3"][e]
    for e in range(9):
        base[f"m1w1_{e}"] = sh["m1_w1"][e]
        base[f"m1w3_{e}"] = sh["m1_w3"][e]
    in_maps = [dict(base, x=x_full[c * TC:(c + 1) * TC]) for c in range(NCORE)]
    res = run_bass_kernel_spmd(nc, in_maps, core_ids=list(range(NCORE)))
    out = np.concatenate([np.asarray(res.results[c]["out"]) for c in range(NCORE)], axis=0)
    return out.reshape(B, T, D).astype(np.float32)


FUSED = True


def _prep_shared(inputs):
    ht = host_tables()
    C = np.ascontiguousarray
    sh = {}
    wqkv = inputs["L0_attn_W_qkv"]
    sh["l0_wq"] = C(wqkv[:, :1024])
    sh["l0_wk"] = C(wqkv[:, 1024:2048])
    sh["l0_wv"] = C(wqkv[:, 2048:])
    sh["l0_wg"] = C(inputs["L0_attn_W_g"])
    sh["l0_wp"] = C(inputs["L0_attn_W_proj"])
    sh["decay_col"] = C(np.broadcast_to(np.asarray(inputs["L0_attn_decay"],
                                                   dtype=np.float32), (128, 8)))
    sh["m0_wgate"] = C(np.asarray(inputs["L0_ffn_Wg"]).T)
    sh["m0_w1"] = [C(np.asarray(inputs["L0_ffn_W1"][e]).T) for e in range(8)]
    sh["m0_w2"] = [C(np.asarray(inputs["L0_ffn_W2"][e]).T) for e in range(8)]
    sh["m0_w3"] = [C(np.asarray(inputs["L0_ffn_W3"][e]).T) for e in range(8)]
    b1 = np.asarray(inputs["L0_ffn_b1"], dtype=np.float32)
    b2 = np.asarray(inputs["L0_ffn_b2"], dtype=np.float32)
    b3 = np.asarray(inputs["L0_ffn_b3"], dtype=np.float32)
    sh["m0_b1a"] = C(b1.reshape(8, 16, 128).transpose(2, 0, 1).reshape(128, 128))
    sh["m0_b2a"] = C(b2.reshape(8, 16, 128).transpose(2, 0, 1).reshape(128, 128))
    sh["m0_b3a"] = C(b3.reshape(8, 1024))
    wqkv1 = inputs["L1_attn_Wqkv"]
    sh["l1_wq"] = C(wqkv1[:, :1024])
    sh["l1_wk"] = C(wqkv1[:, 1024:2048])
    sh["l1_wv"] = C(wqkv1[:, 2048:])
    sh["l1_wo"] = C(inputs["L1_attn_Wo"])
    sh["l1_bo"] = C(np.asarray(inputs["L1_attn_bo"], dtype=np.float32).reshape(1, 1024))
    sh["m1_wgate"] = C(np.asarray(inputs["L1_ffn_Wg"]))
    sh["m1_w1"] = [C(np.asarray(inputs["L1_ffn_Wu"][e])) for e in range(8)] + \
                  [C(np.asarray(inputs["L1_ffn_W1"]))]
    sh["m1_w3"] = [C(np.asarray(inputs["L1_ffn_Wd"][e])) for e in range(8)] + \
                  [C(np.asarray(inputs["L1_ffn_W2"]))]
    sh["ht"] = ht
    return sh


def kernel(**inputs):
    if FUSED:
        return kernel_fused(inputs)
    sh = _prep_shared(inputs)
    ht = sh["ht"]
    x_full = np.ascontiguousarray(np.asarray(inputs["L0_attn_hidden_states"],
                                             dtype=np.float32).reshape(B * T, D))
    xs = [x_full[c * TC:(c + 1) * TC] for c in range(NCORE)]
    cores = [dict() for _ in range(NCORE)]
    allc = list(range(NCORE))

    def run(build_fn, im_fn, outs):
        nc = build_fn()
        res = run_bass_kernel_spmd(nc, [im_fn(c) for c in allc], core_ids=allc)
        for c in allc:
            for o in outs:
                cores[c][o] = np.asarray(res.results[c][o])

    run(lambda: build_ln(True),
        lambda c: {"x": xs[c], "cc": ht["CONSTC"]}, ["h", "xout"])
    run(build_l0_qkv,
        lambda c: {"h": cores[c]["h"], "wq": sh["l0_wq"], "wk": sh["l0_wk"],
                   "cq": ht["CQ0"], "sq": ht["SQ0"], "ck": ht["CK0"],
                   "sk": ht["SK0"], "r0": ht["R0"]}, ["qf", "kf"])
    run(build_l0_vg,
        lambda c: {"h": cores[c]["h"], "wv": sh["l0_wv"], "wg": sh["l0_wg"]},
        ["v", "gate"])
    run(build_l0_ret,
        lambda c: {"qf": cores[c]["qf"], "kf": cores[c]["kf"], "v": cores[c]["v"],
                   "gate": cores[c]["gate"], "dcol": sh["decay_col"],
                   "eh": ht["EH"], "rowi": ht["ROWI"], "colj": ht["COLJ"],
                   "cc": ht["CONSTC"]}, ["gg"])
    run(build_l0_proj,
        lambda c: {"gg": cores[c]["gg"], "w": sh["l0_wp"], "xin": cores[c]["xout"]},
        ["xout"])
    run(lambda: build_ln(False),
        lambda c: {"x": cores[c]["xout"], "cc": ht["CONSTC"]}, ["h"])
    def im_moe0(c):
        im = {"h": cores[c]["h"], "wgate": sh["m0_wgate"], "xin": cores[c]["xout"],
              "b1a": sh["m0_b1a"], "b2a": sh["m0_b2a"], "b3a": sh["m0_b3a"],
              "onesrow": ht["ONESROW"]}
        for e in range(8):
            im[f"w1_{e}"] = sh["m0_w1"][e]
            im[f"w2_{e}"] = sh["m0_w2"][e]
            im[f"w3_{e}"] = sh["m0_w3"][e]
        return im
    run(lambda: build_moe(0), im_moe0, ["xout"])
    run(lambda: build_ln(False),
        lambda c: {"x": cores[c]["xout"], "cc": ht["CONSTC"]}, ["h"])
    run(build_l1_qkv,
        lambda c: {"h": cores[c]["h"], "wq": sh["l1_wq"], "wk": sh["l1_wk"],
                   "c1": ht["C1"], "s1": ht["S1"], "r1": ht["R1"],
                   "h2": ht["HALF2"], "h2t": ht["HALF2T"], "cc": ht["CONSTC"]},
        ["qf", "kf"])
    run(build_l1_v, lambda c: {"h": cores[c]["h"], "wv": sh["l1_wv"]}, ["v"])
    run(build_l1_attn,
        lambda c: {"qf": cores[c]["qf"], "kf": cores[c]["kf"], "v": cores[c]["v"],
                   "wo": sh["l1_wo"], "bo": sh["l1_bo"], "onesrow": ht["ONESROW"],
                   "dsa": ht["DENSEL_A"], "dsb": ht["DENSEL_B"],
                   "xin": cores[c]["xout"]}, ["xout"])
    run(lambda: build_ln(False),
        lambda c: {"x": cores[c]["xout"], "cc": ht["CONSTC"]}, ["h"])
    def im_moe1(c):
        im = {"h": cores[c]["h"], "wgate": sh["m1_wgate"], "xin": cores[c]["xout"]}
        for e in range(9):
            im[f"w1_{e}"] = sh["m1_w1"][e]
            im[f"w3_{e}"] = sh["m1_w3"][e]
        return im
    run(lambda: build_moe(1), im_moe1, ["xout"])

    out = np.concatenate([cores[c]["xout"] for c in range(NCORE)], axis=0)
    return out.reshape(B, T, D).astype(np.float32)
